# version 09
# baseline (speedup 1.0000x reference)
import numpy as np
from contextlib import ExitStack

import concourse.bass as bass
import concourse.mybir as mybir
from concourse.bass_utils import run_bass_kernel_spmd

F32 = mybir.dt.float32
BF16 = mybir.dt.bfloat16
AF = mybir.ActivationFunctionType
ALU = mybir.AluOpType

B, S, D = 8, 4096, 2048
HK, HV = 16, 32
DK = DV = 128
KDIM, VDIM = HK * DK, HV * DV
CONV_DIM, KSIZE = 2 * KDIM + VDIM, 4
HFF = 5632
NHC = HFF // 128
T = 512
NT = S // T
C = 128
NCH = T // C
NGRP = 8
NKC = D // 128
LNEPS, L2EPS, RMSEPS, FFEPS = 1e-5, 1e-6, 1e-6, 1e-8
RING_COLS = 18176

ENGINES = ("pe", "dve", "act", "pool", "sp")
ENG_SEM = {e: "s_" + e for e in ENGINES}


class Ev:
    __slots__ = ("sem", "val")

    def __init__(self, sem, val):
        self.sem, self.val = sem, val


class KB:
    def __init__(self):
        self.ops = {e: [] for e in ENGINES}
        self.semtot = {}
        self.waited = {e: {} for e in ENGINES}

    def now(self, eng):
        return Ev(ENG_SEM[eng], self.semtot.get(ENG_SEM[eng], 0))

    def _wait(self, eng, ev):
        if ev is None or ev.val <= 0:
            return
        cur = self.waited[eng].get(ev.sem, 0)
        if ev.val > cur:
            self.ops[eng].append(("w", ev.sem, ev.val))
            self.waited[eng][ev.sem] = ev.val

    def op(self, eng, fn, waits=(), sig=False, dma=None):
        for w in waits:
            self._wait(eng, w)
        sem = inc = None
        if dma is not None:
            sem, inc = dma, 16
        elif sig:
            sem, inc = ENG_SEM[eng], 1
        if sem is not None:
            self.semtot[sem] = self.semtot.get(sem, 0) + inc
            self.ops[eng].append(("c", fn, sem, inc))
            return Ev(sem, self.semtot[sem])
        self.ops[eng].append(("c", fn, None, 0))
        return None

    def wait_only(self, eng, evs):
        for w in evs:
            self._wait(eng, w)

    def check_deadlock(self):
        ptr = {e: 0 for e in ENGINES}
        cnt = {}
        progress = True
        while progress:
            progress = False
            for e in ENGINES:
                while ptr[e] < len(self.ops[e]):
                    it = self.ops[e][ptr[e]]
                    if it[0] == "w" and cnt.get(it[1], 0) < it[2]:
                        break
                    if it[0] == "c" and it[2] is not None:
                        cnt[it[2]] = cnt.get(it[2], 0) + it[3]
                    ptr[e] += 1
                    progress = True
        stuck = {e: ptr[e] for e in ENGINES if ptr[e] < len(self.ops[e])}
        if stuck:
            detail = {}
            for e in stuck:
                it = self.ops[e][ptr[e]]
                detail[e] = (f"op {ptr[e]}/{len(self.ops[e])} waiting "
                             f"{it[1]}>={it[2]} (now {cnt.get(it[1], 0)})")
            raise RuntimeError(f"build-time deadlock: {detail}")

    def emit(self, eng, engobj, sems):
        for item in self.ops[eng]:
            if item[0] == "w":
                engobj.wait_ge(sems[item[1]], item[2])
            else:
                inst = item[1](engobj)
                if item[2] is not None:
                    inst.then_inc(sems[item[2]], item[3])


class Bank:
    def __init__(self, ap):
        self.ap = ap
        self.wev = None
        self.rev = None


def build_core(taps=None, stop_after="full"):
    taps = taps or []
    tap_out = []

    nc = bass.Bass("TRN2", target_bir_lowering=False, debug=False)

    x_h = nc.dram_tensor("x", [S, D], F32, kind="ExternalInput").ap()
    wqkv_h = nc.dram_tensor("w_qkv", [D, CONV_DIM], F32, kind="ExternalInput").ap()
    wz_h = nc.dram_tensor("w_z", [D, VDIM], F32, kind="ExternalInput").ap()
    wb_h = nc.dram_tensor("w_b", [D, HV], F32, kind="ExternalInput").ap()
    wa_h = nc.dram_tensor("w_a", [D, HV], F32, kind="ExternalInput").ap()
    wout_h = nc.dram_tensor("w_out", [VDIM, D], F32, kind="ExternalInput").ap()
    convw_h = nc.dram_tensor("conv_w", [CONV_DIM, KSIZE], F32, kind="ExternalInput").ap()
    dtb_h = nc.dram_tensor("dt_bias", [HV], F32, kind="ExternalInput").ap()
    alog_h = nc.dram_tensor("A_log", [HV], F32, kind="ExternalInput").ap()
    nw_h = nc.dram_tensor("norm_weight", [DV], F32, kind="ExternalInput").ap()
    w1_h = nc.dram_tensor("w1", [HFF, D], F32, kind="ExternalInput").ap()
    w2_h = nc.dram_tensor("w2", [HFF, D], F32, kind="ExternalInput").ap()
    w3_h = nc.dram_tensor("w3", [D, HFF], F32, kind="ExternalInput").ap()
    y_h = nc.dram_tensor("y", [S, D], F32, kind="ExternalOutput").ap()

    SEG_QQKK = 16 * 512
    SEG_WOUT = 4 * 2048
    SEG_Q12 = 16 * 128
    SEG_Q3 = NHC * 128
    tape_cols = 0
    seg_off = {}

    def add_seg(name, cols):
        nonlocal tape_cols
        seg_off[name] = tape_cols
        tape_cols += cols

    for g in range(NGRP):
        add_seg(f"qqkk{g}", SEG_QQKK)
        add_seg(f"vvvv{g}", SEG_QQKK)
        add_seg(f"zzzz{g}", SEG_QQKK)
        add_seg(f"wout{g}", SEG_WOUT)
    for hc in range(NHC):
        add_seg(f"q1_{hc}", SEG_Q12)
        add_seg(f"q2_{hc}", SEG_Q12)
    for oc in range(NKC):
        add_seg(f"q3_{oc}", SEG_Q3)
    wtape = nc.dram_tensor("wtape", [128, tape_cols], BF16, kind="Internal").ap()

    kb = KB()

    with ExitStack() as ctx:
        def sb(name, shape, dt):
            return ctx.enter_context(nc.sbuf_tensor(name, shape, dt))

        xn_fm = sb("xn_fm", [128, NKC * T], BF16)
        attn_bf = sb("attn_bf", [128, NKC * T], BF16)
        x_tm = sb("x_tm", [128, 2 * NCH * D], BF16)
        work_bf = sb("work_bf", [128, 23296], BF16)
        work_f32 = sb("work_f32", [128, 6720], F32)
        ring = sb("ring", [128, RING_COLS], BF16)
        s_f32 = sb("s_f32", [128, HV * DV], F32)
        wba = sb("wba", [128, NKC * 64], BF16)
        conv_w = sb("conv_w_sb", [128, 64 * KSIZE], F32)
        conv_hist = sb("conv_hist", [128, 64 * (KSIZE - 1)], BF16)
        ident_b = sb("ident_b", [128, 128], BF16)
        ident_f = sb("ident_f", [128, 128], F32)
        two_i = sb("two_i", [128, 128], BF16)
        mask_i = sb("mask_i", [128, 128], F32)
        mask_s = sb("mask_s", [128, 128], F32)
        sel4 = sb("sel4", [4, 4 * 128], F32)
        ones_cb = sb("ones_cb", [128, 1], BF16)
        ones_rb = sb("ones_rb", [1, 128], BF16)
        ones_rf = sb("ones_rf", [1, 128], F32)
        eps_ln = sb("eps_ln", [128, 1], F32)
        eps_l2 = sb("eps_l2", [128, 1], F32)
        eps_rms = sb("eps_rms", [128, 1], F32)
        qsc_b = sb("qsc_b", [128, 1], F32)
        zero_c = sb("zero_c", [128, 1], F32)
        one_c = sb("one_c", [128, 1], F32)
        nw_c = sb("nw_c", [128, 1], F32)
        dtb_c = sb("dtb_c", [64, 1], F32)
        nega_c = sb("nega_c", [64, 1], F32)
        ba_sb = sb("ba_sb", [64, 3 * T], F32)
        gb_tmh = sb("gb_tmh", [128, NCH * 64], F32)
        exps_t = sb("exps_t", [128, NCH * 32], F32)
        bl_t = sb("bl_t", [128, NCH * 32], F32)
        st4 = sb("st4", [128, 16], F32)
        mrow = sb("mrow", [4, 256], F32)
        mbsb = sb("mbsb", [128, NCH * 256], BF16)
        gam_sb = sb("gam_sb", [128, 4], F32)
        nth_sb = sb("nth_sb", [128, 4], F32)
        absc = sb("absc", [128, NHC], F32)
        gred = sb("gred", [128, 128], F32)

        WB = {}
        _o = 0
        for nm, w in [("qg", 2 * T), ("kg", 2 * T), ("vg", 4 * T), ("zg", 4 * T),
                      ("kkkq", 2 * NCH * 256), ("di", 4 * NCH * C),
                      ("wt", 4 * NCH * C), ("tt", 4 * NCH * C), ("xt0", NCH * C),
                      ("vt", 4 * NCH * C), ("kt", 2 * NCH * C),
                      ("at", NCH * C), ("g_nw", NCH * C), ("z1", NCH * C),
                      ("u_sb", 4 * C), ("gtd", 4 * C), ("dl", 4 * C), ("dl2", 4 * C),
                      ("sbf", 4 * C)]:
            WB[nm] = _o
            _o += w
        assert _o <= 23296, _o
        WB["gated"] = WB["kkkq"]
        WB["xt1"] = WB["at"]
        WB["qn"] = WB["u_sb"]
        WB["sqb"] = WB["gtd"]
        WB["cvo"] = WB["dl"]
        WB["mxp"] = WB["dl2"]
        WB_HID = 0
        WB_SIG = 22528
        assert NHC * T <= 22528 + 512

        PRE_Q = 0
        PRE_Q2 = 8192
        PRE_T = 16384
        WF = {}
        _o = 0
        for nm, w in [("e1", NCH * C), ("e2", NCH * C), ("osb", 4 * C), ("g1", 4 * C),
                      ("u1", 4 * C), ("sq", D), ("fsa", 2 * T), ("fsb", 2 * T),
                      ("ms", 8), ("rstd", 8), ("lcc", 16)]:
            WF[nm] = _o
            _o += w
        assert _o <= 6720, _o
        PRE_SQ = 0

        pb = [Bank(ctx.enter_context(nc.psum_tensor(f"pb{i}", [128, 512], F32)).ap())
              for i in range(7)]
        pt = Bank(ctx.enter_context(nc.psum_tensor("pt7", [128, 1024], BF16)).ap())

        sem_names = list(ENG_SEM.values()) + ["dw", "dx", "dy", "dpre"]
        sems = {n: ctx.enter_context(nc.semaphore(n)) for n in sem_names}
        block = ctx.enter_context(nc.Block())

        def mm_group(bank, mm_list, extra_waits=()):
            waits = [w for w in extra_waits if w is not None]
            if bank.rev is not None:
                waits.append(bank.rev)
            n = len(mm_list)
            for i, fn in enumerate(mm_list):
                if i < n - 1:
                    kb.op("pe", fn, waits=waits if i == 0 else ())
                else:
                    bank.wev = kb.op("pe", fn, waits=waits if i == 0 else (), sig=True)
            return bank.wev

        def bank_read(eng, bank, fn, extra_waits=()):
            waits = [bank.wev] + [w for w in extra_waits if w is not None]
            ev = kb.op(eng, fn, waits=waits, sig=True)
            bank.rev = ev
            return ev

        ring_state = {"cursor": 0, "occ": []}

        def ring_push(seg_name, cols):
            cur = ring_state["cursor"]
            if cur + cols > RING_COLS:
                cur = 0
            start, end = cur, cur + cols
            waits, keep = [], []
            for ent in ring_state["occ"]:
                if ent[0] < end and ent[1] > start:
                    assert ent[2] is not None, (
                        f"ring overlap with unreleased segment {ent} pushing "
                        f"{seg_name} at {start}:{end}")
                    waits.append(ent[2])
                else:
                    keep.append(ent)
            ring_state["occ"] = keep
            ent = [start, end, None]
            ring_state["occ"].append(ent)
            toff = seg_off[seg_name]
            ev = kb.op("sp",
                       lambda e, a=toff, n=cols, c=start: e.dma_start(
                           out=ring[:, c:c + n], in_=wtape[:, a:a + n]),
                       waits=waits, dma="dw")
            ring_state["cursor"] = end
            return start, ev, ent

        def ring_release(ent, ev):
            ent[2] = ev

        tap_ev = [None]

        def tap(name, ap_fn, shape, dt, waits):
            if name in taps:
                th = nc.dram_tensor("tap_" + name, shape, dt, kind="ExternalOutput").ap()
                tap_out.append("tap_" + name)
                tap_ev[0] = kb.op("sp", lambda e: e.dma_start(out=th, in_=ap_fn()),
                                  waits=waits, dma="dy")

        def wb(off, cols):
            return work_bf[:, off:off + cols]

        def wf(off, cols):
            return work_f32[:, off:off + cols]

        def wb3(off, n, step, cols):
            a = work_bf.ap()
            return bass.AP(a.tensor, a.offset + off, [a.ap[0], [step, n], [1, cols]])

        def wf3(off, n, step, cols):
            a = work_f32.ap()
            return bass.AP(a.tensor, a.offset + off, [a.ap[0], [step, n], [1, cols]])

        def bcast3(tensor, off, n, step, cols):
            a = tensor.ap()
            return bass.AP(a.tensor, a.offset + off, [a.ap[0], [step, n], [0, cols]])

        def aff(out_ap, fill, pattern, cm, base=0, cmp=ALU.not_equal):
            return lambda e: e.affine_select(out=out_ap, in_=out_ap, compare_op=cmp,
                                             fill=fill, base=base, pattern=pattern,
                                             channel_multiplier=cm)

        kb.op("pool", lambda e: e.memset(ident_b[:], 0.0))
        kb.op("pool", aff(ident_b[:], 1.0, [[-1, 128]], 1))
        kb.op("pool", lambda e: e.memset(ident_f[:], 0.0))
        kb.op("pool", aff(ident_f[:], 1.0, [[-1, 128]], 1))
        kb.op("pool", lambda e: e.memset(two_i[:], 0.0))
        kb.op("pool", aff(two_i[:], 2.0, [[-1, 128]], 1))
        kb.op("pool", lambda e: e.memset(mask_i[:], 0.0))
        kb.op("pool", aff(mask_i[:], -1e9, [[1, 128]], -1, cmp=ALU.is_ge))
        kb.op("pool", lambda e: e.memset(mask_s[:], 0.0))
        kb.op("pool", aff(mask_s[:], 1e9, [[-1, 128]], 1, cmp=ALU.is_gt))
        kb.op("pool", lambda e: e.memset(sel4[:], 0.0))
        kb.op("pool", aff(sel4[:], 1.0, [[-1, 4], [0, 128]], 1))
        for t_, v_ in ((ones_cb, 1.0), (ones_rb, 1.0), (ones_rf, 1.0),
                       (eps_ln, LNEPS), (eps_l2, L2EPS), (eps_rms, RMSEPS),
                       (qsc_b, float(np.log(DK ** -0.5))), (zero_c, 0.0),
                       (one_c, 1.0)):
            kb.op("pool", lambda e, t=t_, v=v_: e.memset(t[:], v))
        kb.op("pool", lambda e: e.memset(conv_hist[:], 0.0))
        kb.op("pool", lambda e: e.memset(s_f32[:], 0.0))
        ev_nw = kb.op("pool", lambda e: e.dma_start(out=nw_c[:], in_=nw_h[:, None]),
                      dma="dpre")
        kb.op("pool", lambda e: e.dma_start(out=dtb_c[32:64, :], in_=dtb_h[:, None]),
              dma="dpre")
        ev_al = kb.op("pool", lambda e: e.dma_start(out=nega_c[32:64, :],
                                                    in_=alog_h[:, None]), dma="dpre")
        ev_cw = kb.op("pool",
                      lambda e: e.dma_start(
                          out=conv_w.ap().rearrange("p (o j) -> p o j", o=64),
                          in_=convw_h.rearrange("(o p) j -> p o j", p=128)),
                      dma="dpre")
        ev_consts = kb.op("pool", lambda e: e.memset(absc[:], 0.0), sig=True)

        ev_na1 = kb.op("act", lambda e: e.activation(out=nega_c[32:64, :],
                                                     in_=nega_c[32:64, :], func=AF.Exp),
                       waits=[ev_al], sig=True)
        ev_nega = kb.op("dve", lambda e: e.tensor_scalar_mul(
            nega_c[32:64, :], nega_c[32:64, :], -1.0), waits=[ev_na1], sig=True)

        for g in range(NGRP if stop_after != "consts" else 0):
            so = seg_off[f"qqkk{g}"]
            for half, c0 in ((0, 2 * g * 128), (1, (16 + 2 * g) * 128)):
                src = bass.AP(wqkv_h.tensor, wqkv_h.offset + c0,
                              [[128 * CONV_DIM, NKC], [CONV_DIM, 128], [1, 256]])
                dst = bass.AP(wtape.tensor, wtape.offset + so + half * 256,
                              [[512, NKC], [tape_cols, 128], [1, 256]])
                kb.op("pool", lambda e, s=src, d=dst: e.dma_start(out=d, in_=s),
                      dma="dpre")
            so = seg_off[f"vvvv{g}"]
            src = bass.AP(wqkv_h.tensor, wqkv_h.offset + (32 + 4 * g) * 128,
                          [[128 * CONV_DIM, NKC], [CONV_DIM, 128], [1, 512]])
            dst = bass.AP(wtape.tensor, wtape.offset + so,
                          [[512, NKC], [tape_cols, 128], [1, 512]])
            kb.op("pool", lambda e, s=src, d=dst: e.dma_start(out=d, in_=s), dma="dpre")
            so = seg_off[f"zzzz{g}"]
            src = bass.AP(wz_h.tensor, wz_h.offset + g * 512,
                          [[128 * VDIM, NKC], [VDIM, 128], [1, 512]])
            dst = bass.AP(wtape.tensor, wtape.offset + so,
                          [[512, NKC], [tape_cols, 128], [1, 512]])
            kb.op("pool", lambda e, s=src, d=dst: e.dma_start(out=d, in_=s), dma="dpre")
        ev_wba = None
        for (wsrc, half) in ((wb_h, 0), (wa_h, 1)):
            src = bass.AP(wsrc.tensor, wsrc.offset,
                          [[HV, 128], [128 * HV, NKC], [1, 32]])
            a = wba.ap()
            dst = bass.AP(a.tensor, a.offset + half * 32,
                          [a.ap[0], [64, NKC], [1, 32]])
            ev_wba = kb.op("pool", lambda e, s=src, d=dst: e.dma_start(out=d, in_=s),
                           dma="dpre")

        ev_prev_tape = None
        for g in range(NGRP if stop_after != "consts" else 0):
            for kc in range(4):
                r0 = g * 512 + kc * 128
                src = bass.AP(wout_h.tensor, wout_h.offset + r0 * D, [[D, 128], [1, D]])
                evl = kb.op("pool", lambda e, s=src: e.dma_start(out=wb(PRE_Q, D), in_=s),
                            waits=[ev_prev_tape] if ev_prev_tape else [], dma="dpre")
                ev_sc = kb.op("dve", lambda e: e.tensor_scalar_mul(
                    wb(PRE_Q, D), wb(PRE_Q, D), nw_c[:]), waits=[evl, ev_nw], sig=True)
                so = seg_off[f"wout{g}"] + kc * 2048
                ev_prev_tape = kb.op("sp", lambda e, a=so: e.dma_start(
                    out=wtape[:, a:a + 2048], in_=wb(PRE_Q, D)),
                    waits=[ev_sc], dma="dw")

        ev_gam_row = {}
        _gl = ((1, w1_h, NHC, D), (2, w2_h, NHC, D), (3, w3_h, NKC, HFF)) \
            if stop_after not in ("consts", "tape") else ()
        for mi, wh, nslab, cols in _gl:
            ev_abs = None
            for sl in range(nslab):
                src = bass.AP(wh.tensor, wh.offset + sl * 128 * cols,
                              [[cols, 128], [1, cols]])
                po = (sl % 2) * 2048 if cols == D else 0
                war = ev_abs if (cols != D or sl >= 2) else None
                evl = kb.op("pool", lambda e, s=src, c=cols, p=po: e.dma_start(
                    out=wf(PRE_SQ + p, c), in_=s),
                    waits=[w for w in (war,) if w], dma="dpre")
                ev_abs = kb.op("act", lambda e, j=sl, c=cols, p=po: e.activation(
                    out=wf(PRE_SQ + p, c), in_=wf(PRE_SQ + p, c), func=AF.Abs,
                    accum_out=absc[:, j:j + 1]), waits=[evl, ev_consts], sig=True)
            ev_red = kb.op("act", lambda e, m=mi, n=nslab: e.activation(
                out=gred[:, 0:n], in_=absc[:, 0:n], func=AF.Copy,
                accum_out=st4[:, 12 + m:13 + m]), waits=[ev_abs], sig=True)
            bnk = pb[2]
            mm_group(bnk, [lambda t, m=mi, b=bnk: t.matmul(
                b.ap[0:1, 0:128], st4[:, 12 + m:13 + m], ident_f[:],
                is_transpose=True, start=True, stop=True)],
                extra_waits=[ev_red])
            ev_gam_row[mi] = bank_read("act", bnk, lambda e, m=mi, b=bnk: e.activation(
                out=gred[0:1, 0:128], in_=b.ap[0:1, 0:128], func=AF.Copy,
                accum_out=st4[0:1, 8 + m:9 + m]))
        ev_th = None
        for mi in ((1, 2, 3) if stop_after not in ("consts", "tape") else ()):
            numel = HFF * D
            ev1 = kb.op("dve", lambda e, m=mi, n=numel: e.tensor_scalar(
                out=st4[0:1, 8 + m:9 + m], in0=st4[0:1, 8 + m:9 + m],
                scalar1=0.5 / n, scalar2=0.5 * FFEPS, op0=ALU.mult, op1=ALU.add),
                waits=[ev_gam_row[mi]], sig=True)
            bnk = pb[2]
            mm_group(bnk, [lambda t, m=mi, b=bnk: t.matmul(
                b.ap[:, 0:1], ones_rf[:], st4[0:1, 8 + m:9 + m],
                start=True, stop=True)], extra_waits=[ev1])
            ev2 = bank_read("dve", bnk, lambda e, m=mi, b=bnk: e.tensor_copy(
                gam_sb[:, m:m + 1], b.ap[:, 0:1]))
            ev_th = kb.op("dve", lambda e, m=mi: e.tensor_scalar_mul(
                nth_sb[:, m:m + 1], gam_sb[:, m:m + 1], -1.0), waits=[ev2], sig=True)
        tap("gamma_th", lambda: gam_sb[:, 0:4], [128, 4], F32, [ev_th])
        tap("pre_st4", lambda: st4[:, 0:16], [128, 16], F32, [ev_th])
        tap("pre_absc", lambda: absc[:, 0:NHC], [128, NHC], F32, [ev_th])

        ev_presq_free = [ev_gam_row.get(3), ev_gam_row.get(3)]
        ev_stage_pe = [None, None]
        ev_tape_slot = [None, None]
        par_ct = [0]

        def quant_and_transpose(src_ap, cols, mi, nblk, seg):
            nonlocal ev_prev_tape
            if cols == D:
                par = par_ct[0] % 2
                par_ct[0] += 1
                oq, oq2, ot = PRE_Q + par * 2048, PRE_Q2 + par * 2048, \
                    PRE_T + par * 2560
                presq = PRE_SQ + par * 2048
            else:
                par = 0
                oq, oq2, ot = PRE_Q, PRE_Q2, PRE_T
                presq = PRE_SQ
            waitl = [ev_presq_free[par]]
            if cols != D:
                waitl += [ev_presq_free[1 - par], ev_stage_pe[1 - par],
                          ev_tape_slot[1 - par]]
            evl = kb.op("pool", lambda e, s=src_ap, c=cols, o=presq: e.dma_start(
                out=wf(o, c), in_=s),
                waits=[w for w in waitl if w], dma="dpre")
            kb.op("dve", lambda e, c=cols, m=mi, o=presq, q=oq:
                  e.tensor_single_scalar(
                      wb(q, c), wf(o, c), gam_sb[:, m:m + 1], ALU.is_gt),
                  waits=[w for w in (evl, ev_th, ev_stage_pe[par]) if w])
            ev_presq_free[par] = kb.op(
                "dve", lambda e, c=cols, m=mi, o=presq, q=oq2:
                e.tensor_single_scalar(
                    wb(q, c), wf(o, c), nth_sb[:, m:m + 1], ALU.is_lt), sig=True)
            ev_q = kb.op("dve", lambda e, c=cols, q=oq, q2=oq2: e.tensor_tensor(
                out=wb(q, c), in0=wb(q, c), in1=wb(q2, c),
                op=ALU.subtract), sig=True)
            ev_ev = None
            tape_war = ev_tape_slot[par]
            for g0 in range(0, nblk, 8):
                gn = min(8, nblk - g0)
                mml = []
                for j in range(gn):
                    blkc = oq + (g0 + j) * 128
                    mml.append(lambda t, bc=blkc, jj=j, gn=gn: t.matmul(
                        pt.ap[:, jj * 128:(jj + 1) * 128], wb(bc, 128), ident_b[:],
                        is_transpose=True, start=(jj == 0), stop=(jj == gn - 1)))
                mm_group(pt, mml, extra_waits=[ev_q, ev_consts])
                ev_ev = bank_read("dve", pt, lambda e, g0=g0, gn=gn, o=ot:
                                  e.tensor_copy(
                                      wb(o + g0 * 128, gn * 128),
                                      pt.ap[:, 0:gn * 128]),
                                  extra_waits=[tape_war] if g0 == 0 else ())
            ev_stage_pe[par] = kb.now("pe")
            ev = kb.op("sp", lambda e, a=seg_off[seg], n=nblk, o=ot: e.dma_start(
                out=wtape[:, a:a + n * 128], in_=wb(o, n * 128)),
                waits=[ev_ev], dma="dw")
            ev_tape_slot[par] = ev
            ev_prev_tape = ev

        _ql = ((1, w1_h), (2, w2_h)) if stop_after not in ("consts", "tape") else ()
        for mi, wh in _ql:
            for hb in range(NHC):
                src = bass.AP(wh.tensor, wh.offset + hb * 128 * D, [[D, 128], [1, D]])
                quant_and_transpose(src, D, mi, NKC, f"q{mi}_{hb}")
        for ob in range(NKC if stop_after not in ("consts", "tape") else 0):
            src = bass.AP(w3_h.tensor, w3_h.offset + ob * 128 * HFF,
                          [[HFF, 128], [1, HFF]])
            quant_and_transpose(src, HFF, 3, NHC, f"q3_{ob}")

        ev_tape_done = Ev("dw", kb.semtot.get("dw", 0))
        ev_pre_dve = kb.now("dve")
        ev_pre_pe = kb.now("pe")
        ev_pre_act = kb.now("act")

        x_free = [None, None]

        _nt = 0 if stop_after in ("consts", "tape", "pre") else NT
        if stop_after in ("xload", "lnstats", "lnT", "ln1",
                          "ba1", "ba2", "ba3"):
            _nt = 1
        for ti in range(_nt):
            ring_state["cursor"] = 0
            png = ti % 2
            xoff = png * NCH * D
            a = x_tm.ap()
            dst_x = bass.AP(a.tensor, a.offset + xoff, [a.ap[0], [D, NCH], [1, D]])
            src_x = bass.AP(x_h.tensor, x_h.offset + ti * T * D,
                            [[D, 128], [128 * D, NCH], [1, D]])
            wv = [x_free[png]] if x_free[png] else []
            ev_x = kb.op("pool", lambda e, s=src_x, d=dst_x: e.dma_start(out=d, in_=s),
                         waits=wv, dma="dx")
            if ti == 0:
                tap("x_tm", lambda: x_tm[:, 0:NCH * D], [128, NCH * D], BF16, [ev_x])
            if stop_after == "xload":
                break

            ev_st = None
            first_w = [ev_x]
            if ti == 0:
                first_w += [ev_pre_act, ev_pre_dve, ev_pre_pe, ev_consts]
            else:
                first_w += [kb.now("dve"), kb.now("pe")]
            for tc in range(NCH):
                kb.op("act", lambda e, tc=tc, xo=xoff: e.activation(
                    out=wf(WF["sq"], D), in_=x_tm[:, xo + tc * D:xo + (tc + 1) * D],
                    func=AF.Square, accum_out=st4[:, 4 + tc:5 + tc]),
                    waits=first_w if tc == 0 else ())
                ev_st = kb.op("act", lambda e, tc=tc, xo=xoff: e.activation(
                    out=wf(WF["sq"], D), in_=x_tm[:, xo + tc * D:xo + (tc + 1) * D],
                    func=AF.Copy, accum_out=st4[:, tc:tc + 1]), sig=True)
            e1w = [ev_st]
            if ti == 0:
                e1w += [ev_pre_pe, ev_tape_done]
            e1 = kb.op("dve", lambda e: e.tensor_scalar_mul(st4[:, 0:8], st4[:, 0:8],
                                                            1.0 / D),
                       waits=e1w, sig=True)
            e2 = kb.op("dve", lambda e: e.tensor_tensor(
                out=st4[:, 8:12], in0=st4[:, 0:4], in1=st4[:, 0:4], op=ALU.mult),
                waits=[e1], sig=True)
            e3 = kb.op("dve", lambda e: e.tensor_tensor(
                out=st4[:, 8:12], in0=st4[:, 4:8], in1=st4[:, 8:12], op=ALU.subtract),
                waits=[e2], sig=True)
            e4 = kb.op("act", lambda e: e.activation(
                out=st4[:, 8:12], in_=st4[:, 8:12], func=AF.Sqrt, bias=eps_ln[:]),
                waits=[e3], sig=True)
            e5 = kb.op("dve", lambda e: e.reciprocal(st4[:, 12:16], st4[:, 8:12]),
                       waits=[e4], sig=True)
            ev_xn_tm = []
            for tc in range(NCH):
                ev = kb.op("dve", lambda e, tc=tc, xo=xoff: e.tensor_scalar(
                    out=x_tm[:, xo + tc * D:xo + (tc + 1) * D],
                    in0=x_tm[:, xo + tc * D:xo + (tc + 1) * D],
                    scalar1=st4[:, tc:tc + 1], scalar2=st4[:, 12 + tc:13 + tc],
                    op0=ALU.subtract, op1=ALU.mult), waits=[e5], sig=True)
                ev_xn_tm.append(ev)
            if ti == 0:
                tap("xn_tm", lambda: x_tm[:, 0:NCH * D], [128, NCH * D], BF16,
                    [ev_xn_tm[-1]])
            if stop_after == "lnstats":
                break
            mm_group(pb[3], [
                lambda t: t.matmul(pb[3].ap[0:4, 0:128], st4[:, 0:4], ident_f[:],
                                   is_transpose=True, start=True, stop=False),
                lambda t: t.matmul(pb[3].ap[0:4, 128:256], st4[:, 8:12], ident_f[:],
                                   is_transpose=True, start=False, stop=True)],
                extra_waits=[e5])
            ev_mr = bank_read("dve", pb[3], lambda e: e.tensor_copy(
                mrow[:], pb[3].ap[0:4, 0:256]))
            ev_mbsb = None
            for bk, tcs in ((pb[3], (0, 1)), (pb[4], (2, 3))):
                mml = []
                for j, tc in enumerate(tcs):
                    mml.append(lambda t, tc=tc, j=j, b=bk: t.matmul(
                        b.ap[:, j * 256:(j + 1) * 256],
                        sel4[:, tc * 128:(tc + 1) * 128], mrow[:],
                        start=(j == 0), stop=(j == 1)))
                mm_group(bk, mml, extra_waits=[ev_mr])
                ev_mbsb = bank_read("dve", bk, lambda e, b=bk, t0=tcs[0]: e.tensor_copy(
                    mbsb[:, t0 * 256:t0 * 256 + 512], b.ap[:, 0:512]))

            ev_xn = None
            for tc in range(NCH):
                for g0 in range(0, NKC, 8):
                    mml = []
                    for j in range(8):
                        fc = g0 + j
                        mml.append(lambda t, fc=fc, j=j, tc=tc, xo=xoff: t.matmul(
                            pt.ap[:, j * 128:(j + 1) * 128],
                            x_tm[:, xo + tc * D + fc * 128:xo + tc * D + fc * 128 + 128],
                            ident_b[:], is_transpose=True,
                            start=(j == 0), stop=(j == 7)))
                    mm_group(pt, mml, extra_waits=[ev_xn_tm[tc]])
                    an = xn_fm.ap()
                    dst = bass.AP(an.tensor, an.offset + g0 * T + tc * 128,
                                  [an.ap[0], [T, 8], [1, 128]])
                    ev_xn = bank_read("dve", pt, lambda e, d=dst: e.tensor_copy(
                        d, pt.ap.rearrange("p (a b) -> p a b", a=8)))
            x_free[png] = ev_xn
            if ti == 0:
                tap("xn_fm", lambda: xn_fm[:, 0:NKC * T], [128, NKC * T], BF16, [ev_xn])
            if stop_after == "lnT":
                break

            bnk = pb[5]
            mml = []
            for kc in range(NKC):
                mml.append(lambda t, kc=kc, b=bnk: t.matmul(
                    b.ap[0:64, 0:T], wba[:, kc * 64:(kc + 1) * 64],
                    xn_fm[:, kc * T:(kc + 1) * T],
                    start=(kc == 0), stop=(kc == NKC - 1)))
            mm_group(bnk, mml, extra_waits=[ev_xn, ev_wba])
            evb = bank_read("act", bnk, lambda e, b=bnk: e.activation(
                out=ba_sb[0:32, 0:T], in_=b.ap[0:32, 0:T], func=AF.Sigmoid))
            eva1 = kb.op("act", lambda e, b=bnk: e.activation(
                out=ba_sb[32:64, T:2 * T], in_=b.ap[32:64, 0:T], func=AF.Abs,
                bias=dtb_c[32:64, :]), waits=[bnk.wev], sig=True)
            eva2 = kb.op("act", lambda e: e.activation(
                out=ba_sb[32:64, T:2 * T], in_=ba_sb[32:64, T:2 * T], func=AF.Exp,
                scale=-1.0), waits=[eva1], sig=True)
            eva3 = kb.op("act", lambda e: e.activation(
                out=ba_sb[32:64, T:2 * T], in_=ba_sb[32:64, T:2 * T], func=AF.Ln,
                bias=one_c[32:64, :]), waits=[eva2], sig=True)
            eva4 = kb.op("dve", lambda e, b=bnk: e.tensor_scalar(
                out=ba_sb[32:64, 0:T], in0=b.ap[32:64, 0:T],
                scalar1=dtb_c[32:64, :], scalar2=0.0, op0=ALU.add, op1=ALU.max),
                waits=[bnk.wev], sig=True)
            bnk.rev = eva4
            eva5 = kb.op("dve", lambda e: e.tensor_tensor(
                out=ba_sb[32:64, 0:T], in0=ba_sb[32:64, 0:T],
                in1=ba_sb[32:64, T:2 * T], op=ALU.add), waits=[eva3, eva4], sig=True)
            eva6 = kb.op("dve", lambda e: e.tensor_scalar_mul(
                ba_sb[32:64, 0:T], ba_sb[32:64, 0:T], nega_c[32:64, :]),
                waits=[eva5, ev_nega], sig=True)
            if ti == 0:
                tap("ba_g", lambda: ba_sb[0:64, 0:T], [64, T], F32, [eva6, evb])
            if stop_after == "ba1":
                break
            ev_gamma = None
            for cc in range(NCH):
                ev_gamma = kb.op("dve", lambda e, cc=cc: e.tensor_tensor_scan(
                    out=ba_sb[32:64, 2 * T + cc * C:2 * T + (cc + 1) * C],
                    data0=ba_sb[32:64, cc * C:(cc + 1) * C],
                    data1=ba_sb[32:64, cc * C:(cc + 1) * C],
                    initial=0.0, op0=ALU.add, op1=ALU.bypass),
                    waits=[eva6], sig=True)
            if ti == 0:
                tap("ba_gam", lambda: ba_sb[0:64, 2 * T:3 * T], [64, T], F32,
                    [ev_gamma])
            if stop_after == "ba2":
                break
            mml = []
            for cc in range(NCH):
                mml.append(lambda t, cc=cc: t.matmul(
                    pb[5].ap[:, cc * 32:(cc + 1) * 32],
                    ba_sb[32:64, 2 * T + cc * C:2 * T + (cc + 1) * C],
                    ident_f[32:64, 32:64], is_transpose=True,
                    start=(cc == 0), stop=(cc == NCH - 1)))
            mm_group(pb[5], mml, extra_waits=[evb, ev_gamma])
            ev_gb1 = bank_read("dve", pb[5], lambda e: e.tensor_copy(
                bass.AP(gb_tmh.ap().tensor, gb_tmh.ap().offset,
                        [gb_tmh.ap().ap[0], [64, NCH], [1, 32]]),
                pb[5].ap[:, 0:NCH * 32].rearrange("p (a b) -> p a b", a=NCH)))
            mml = []
            for cc in range(NCH):
                mml.append(lambda t, cc=cc: t.matmul(
                    pb[6].ap[:, cc * 32:(cc + 1) * 32],
                    ba_sb[0:32, cc * C:(cc + 1) * C],
                    ident_f[0:32, 0:32], is_transpose=True,
                    start=(cc == 0), stop=(cc == NCH - 1)))
            mm_group(pb[6], mml, extra_waits=[evb, ev_gamma])
            ev_gb2 = bank_read("dve", pb[6], lambda e: e.tensor_copy(
                bass.AP(gb_tmh.ap().tensor, gb_tmh.ap().offset + 32,
                        [gb_tmh.ap().ap[0], [64, NCH], [1, 32]]),
                pb[6].ap[:, 0:NCH * 32].rearrange("p (a b) -> p a b", a=NCH)))
            ev_gb = ev_gb2
            if ti == 0:
                tap("gb_early", lambda: gb_tmh[:, 0:NCH * 64], [128, NCH * 64], F32,
                    [ev_gb])
            if stop_after == "ba3":
                break
            ae, ag = exps_t.ap(), gb_tmh.ap()
            ev_exps = kb.op("act", lambda e: e.activation(
                out=bass.AP(ae.tensor, ae.offset, [ae.ap[0], [32, NCH], [1, 32]]),
                in_=bass.AP(ag.tensor, ag.offset, [ag.ap[0], [64, NCH], [1, 32]]),
                func=AF.Exp), waits=[ev_gb], sig=True)
            ab = bl_t.ap()
            ev_bl = kb.op("dve", lambda e: e.tensor_tensor(
                out=bass.AP(ab.tensor, ab.offset, [ab.ap[0], [32, NCH], [1, 32]]),
                in0=bass.AP(ae.tensor, ae.offset, [ae.ap[0], [32, NCH], [1, 32]]),
                in1=bass.AP(ag.tensor, ag.offset + 32, [ag.ap[0], [64, NCH], [1, 32]]),
                op=ALU.mult), waits=[ev_exps], sig=True)
            if ti == 0:
                tap("gb_tmh", lambda: gb_tmh[:, 0:NCH * 64], [128, NCH * 64], F32,
                    [ev_gb])
            if stop_after in ("ln", "ln1"):
                break

            carry = {"silu_pe": kb.now("pe"),
                     "bcast_pe": kb.now("pe"),
                     "qk_dve": kb.now("dve"),
                     "cvo_act": kb.now("act"),
                     "vg_pe": kb.now("pe"),
                     "qkg_pe": kb.now("pe"),
                     "zg_dve": kb.now("dve"),
                     "sbf_pe": kb.now("pe"),
                     "gated_pe": kb.now("pe")}

            for g in range(NGRP):
                o_qk, ev_qk, ent_qk = ring_push(f"qqkk{g}", SEG_QQKK)
                o_v, ev_v, ent_v = ring_push(f"vvvv{g}", SEG_QQKK)

                def proj_mm_conv(ring_off, ring_sub, qn_off, sq_off, bnk_i,
                                 ready_ev, conv_ch, kind, dst_off=None):
                    bnkp = pb[bnk_i]
                    mml = []
                    for kc in range(NKC):
                        mml.append(lambda t, kc=kc, b=bnkp, ro=ring_off,
                                   rs=ring_sub: t.matmul(
                            b.ap[:, 0:T],
                            ring[:, ro + kc * 512 + rs:ro + kc * 512 + rs + 128],
                            xn_fm[:, kc * T:(kc + 1) * T],
                            start=(kc == 0), stop=(kc == NKC - 1)))
                    mm_group(bnkp, mml, extra_waits=[ready_ev, ev_xn])
                    hco = conv_ch * 3
                    kb.op("dve", lambda e, hco=hco: e.tensor_copy(
                        wb(WB["mxp"], 3), conv_hist[:, hco:hco + 3]),
                        waits=[ev_consts, carry["cvo_act"], carry["sbf_pe"],
                               tap_ev[0]])
                    bank_read("dve", bnkp, lambda e, b=bnkp: e.tensor_copy(
                        wb(WB["mxp"] + 3, T), b.ap[:, 0:T]))
                    kb.op("dve", lambda e, hco=hco: e.tensor_copy(
                        conv_hist[:, hco:hco + 3], wb(WB["mxp"] + T, 3)))
                    if conv_ch == 0:
                        tap("mxp0", lambda: wb(WB["mxp"], T + 3), [128, T + 3],
                            BF16, [kb.now("dve")])
                    kb.op("dve", lambda e, cc_=conv_ch: e.tensor_scalar(
                        out=wb(WB["cvo"], T), in0=wb(WB["mxp"], T),
                        scalar1=conv_w[:, cc_ * 4:cc_ * 4 + 1], scalar2=None,
                        op0=ALU.mult), waits=[ev_cw])
                    for j in (1, 2):
                        kb.op("dve", lambda e, j=j, cc_=conv_ch: e.scalar_tensor_tensor(
                            out=wb(WB["cvo"], T), in0=wb(WB["mxp"] + j, T),
                            scalar=conv_w[:, cc_ * 4 + j:cc_ * 4 + j + 1],
                            in1=wb(WB["cvo"], T), op0=ALU.mult, op1=ALU.add))
                    ev_cv = kb.op("dve", lambda e, cc_=conv_ch: e.scalar_tensor_tensor(
                        out=wb(WB["cvo"], T), in0=wb(WB["mxp"] + 3, T),
                        scalar=conv_w[:, cc_ * 4 + 3:cc_ * 4 + 4],
                        in1=wb(WB["cvo"], T), op0=ALU.mult, op1=ALU.add), sig=True)
                    if kind == "v":
                        ev = kb.op("act", lambda e, d=dst_off: e.activation(
                            out=wb(d, T), in_=wb(WB["cvo"], T), func=AF.Silu),
                            waits=[ev_cv, carry["vg_pe"]], sig=True)
                        carry["cvo_act"] = ev
                        return ev
                    ev_s = kb.op("act", lambda e, q=qn_off: e.activation(
                        out=wb(q, T), in_=wb(WB["cvo"], T), func=AF.Silu),
                        waits=[ev_cv, carry["qk_dve"], carry["silu_pe"],
                               carry["sbf_pe"]], sig=True)
                    carry["cvo_act"] = ev_s
                    ev_sq = kb.op("act", lambda e, q=qn_off, s=sq_off: e.activation(
                        out=wb(s, T), in_=wb(q, T), func=AF.Square),
                        waits=[ev_s], sig=True)
                    return ev_sq

                def proj_l2_finish(qn_off, sq_off, dst_off, kind, ev_sq):
                    b2 = pb[6]
                    mm_group(b2, [lambda t, b=b2, s=sq_off: t.matmul(
                        b.ap[0:1, 0:T], ones_cb[:], wb(s, T),
                        start=True, stop=True)], extra_waits=[ev_sq])
                    carry["silu_pe"] = b2.wev
                    ev_ln1 = bank_read("act", b2, lambda e, b=b2: e.activation(
                        out=wf(WF["u1"], T)[0:1, :], in_=b.ap[0:1, 0:T],
                        func=AF.Ln, bias=eps_l2[0:1, :]))
                    ev_ln2 = kb.op("act", lambda e, k=kind: e.activation(
                        out=wb(WB["z1"], T)[0:1, :], in_=wf(WF["u1"], T)[0:1, :],
                        func=AF.Exp, scale=-0.5,
                        bias=(qsc_b[0:1, :] if k == "q" else zero_c[0:1, :])),
                        waits=[ev_ln1, carry["bcast_pe"]], sig=True)
                    mm_group(b2, [lambda t, b=b2: t.matmul(
                        b.ap[:, 0:T], ones_rb[:], wb(WB["z1"], T)[0:1, :],
                        start=True, stop=True)], extra_waits=[ev_ln2])
                    carry["bcast_pe"] = b2.wev
                    ev = bank_read("dve", b2, lambda e, b=b2, q=qn_off, d=dst_off:
                                   e.tensor_tensor(
                                       out=wb(d, T), in0=wb(q, T), in1=b.ap[:, 0:T],
                                       op=ALU.mult), extra_waits=[carry["qkg_pe"]])
                    carry["qk_dve"] = ev
                    return ev

                ev_qg, ev_kg, ev_vg = [None, None], [None, None], [None] * 4
                qk_phase = []
                for i in range(2):
                    ev_sq = proj_mm_conv(o_qk, i * 128, WB["di"] + i * T,
                                         WB["wt"] + i * T, i % 2, ev_qk,
                                         2 * g + i, "q")
                    qk_phase.append(("q", i, ev_sq))
                for i in range(2):
                    ev_sq = proj_mm_conv(o_qk, 256 + i * 128, WB["di"] + (2 + i) * T,
                                         WB["wt"] + (2 + i) * T, i % 2, ev_qk,
                                         16 + 2 * g + i, "k")
                    qk_phase.append(("k", i, ev_sq))
                ring_release(ent_qk, kb.now("pe"))
                o_z, ev_z, ent_z = ring_push(f"zzzz{g}", SEG_QQKK)
                for kind, i, ev_sq in qk_phase:
                    j = i if kind == "q" else 2 + i
                    dst = (WB["qg"] if kind == "q" else WB["kg"]) + i * T
                    ev = proj_l2_finish(WB["di"] + j * T, WB["wt"] + j * T,
                                        dst, kind, ev_sq)
                    if kind == "q":
                        ev_qg[i] = ev
                    else:
                        ev_kg[i] = ev
                for i in range(4):
                    ev_vg[i] = proj_mm_conv(o_v, i * 128, 0, 0, i % 2, ev_v,
                                            32 + 4 * g + i, "v",
                                            dst_off=WB["vg"] + i * T)
                ring_release(ent_v, kb.now("pe"))
                o_wo, ev_wo, ent_wo = ring_push(f"wout{g}", SEG_WOUT)

                if ti == 0 and g == 0:
                    evq = kb.now("dve")
                    tap("qg0", lambda: wb(WB["qg"], 2 * T), [128, 2 * T], BF16, [evq])
                    tap("kg0", lambda: wb(WB["kg"], 2 * T), [128, 2 * T], BF16, [evq])
                    eva = ev_vg[3]
                    tap("vg0", lambda: wb(WB["vg"], 4 * T), [128, 4 * T], BF16, [eva])

                ev_zg = None
                for cc in range(NCH):
                    bnkz = pb[2 + 2 * (cc % 2)]
                    mml = []
                    for kc in range(NKC):
                        mml.append(lambda t, kc=kc, cc=cc, b=bnkz, oz=o_z: t.matmul(
                            b.ap[:, 0:512],
                            xn_fm[:, kc * T + cc * C:kc * T + (cc + 1) * C],
                            ring[:, oz + kc * 512:oz + (kc + 1) * 512],
                            start=(kc == 0), stop=(kc == NKC - 1)))
                    mm_group(bnkz, mml, extra_waits=[ev_z, ev_xn])
                    ev_zg = bank_read("act", bnkz, lambda e, cc=cc, b=bnkz:
                                      e.activation(
                                          out=wb(WB["zg"] + cc * 512, 512),
                                          in_=b.ap[:, 0:512], func=AF.Silu),
                                      extra_waits=[carry["zg_dve"], tap_ev[0]])
                ring_release(ent_z, kb.now("pe"))

                ev_kkkq = None
                for kl in range(2):
                    for c0 in range(0, NCH, 2):
                        bnkk = pb[2 + 2 * ((c0 // 2) % 2)]
                        mml = []
                        for j, cc in enumerate((c0, c0 + 1)):
                            mml.append(lambda t, cc=cc, kl=kl, j=j, b=bnkk: t.matmul(
                                b.ap[:, j * 256:j * 256 + 128],
                                wb(WB["kg"] + kl * T + cc * C, C),
                                wb(WB["kg"] + kl * T + cc * C, C),
                                start=(j == 0), stop=False))
                            mml.append(lambda t, cc=cc, kl=kl, j=j, b=bnkk: t.matmul(
                                b.ap[:, j * 256 + 128:j * 256 + 256],
                                wb(WB["kg"] + kl * T + cc * C, C),
                                wb(WB["qg"] + kl * T + cc * C, C),
                                start=False, stop=(j == 1)))
                        mm_group(bnkk, mml, extra_waits=[ev_kg[kl], ev_qg[kl]])
                        ev_kkkq = bank_read("dve", bnkk,
                                            lambda e, kl=kl, c0=c0, b=bnkk:
                                            e.tensor_copy(
                                                wb(WB["kkkq"] + kl * NCH * 256 + c0 * 256, 512),
                                                b.ap[:, 0:512]),
                                            extra_waits=[carry["gated_pe"],
                                                         tap_ev[0]])
                carry["qkg_pe"] = kb.now("pe")
                ev_vt = None
                for b0 in range(0, 16, 8):
                    mml = []
                    for j in range(8):
                        hh, cc = divmod(b0 + j, NCH)
                        mml.append(lambda t, hh=hh, cc=cc, j=j: t.matmul(
                            pt.ap[:, j * 128:(j + 1) * 128],
                            wb(WB["vg"] + hh * T + cc * C, C),
                            ident_b[:], is_transpose=True,
                            start=(j == 0), stop=(j == 7)))
                    mm_group(pt, mml, extra_waits=ev_vg)
                    ev_vt = bank_read("dve", pt, lambda e, b0=b0: e.tensor_copy(
                        wb(WB["vt"] + b0 * 128, 1024), pt.ap[:, 0:1024]))
                mml = []
                for j in range(8):
                    kl, cc = divmod(j, NCH)
                    mml.append(lambda t, kl=kl, cc=cc, j=j: t.matmul(
                        pt.ap[:, j * 128:(j + 1) * 128],
                        wb(WB["kg"] + kl * T + cc * C, C),
                        ident_b[:], is_transpose=True, start=(j == 0), stop=(j == 7)))
                mm_group(pt, mml, extra_waits=[ev_kg[0], ev_kg[1]])
                ev_kt = bank_read("dve", pt, lambda e: e.tensor_copy(
                    wb(WB["kt"], 1024), pt.ap[:, 0:1024]))
                carry["vg_pe"] = kb.now("pe")

                ev_lcc = None
                G_SLOT = (WB["g_nw"], WB["dl"])
                Z_SLOT = (WB["z1"], WB["dl2"])
                XT_A = (WB["xt0"], WB["u_sb"])
                XT_B = (WB["at"], WB["gtd"])
                XB = (pb[2], pb[3])
                PBANK = (pb[4], pb[5])
                XTBANK = (pb[6], pb[3])
                ev_zslot = [None] * 4
                ev_slot_pe = None

                def p1_prep(hh):
                    p = hh % 2
                    h32 = (4 * g + hh) % 32
                    kl = hh // 2
                    bnkx = XB[p]
                    mml = []
                    for cc in range(NCH):
                        view = bass.AP(gb_tmh.ap().tensor,
                                       gb_tmh.ap().offset + cc * 64 + h32,
                                       [gb_tmh.ap().ap[0], [0, C]])
                        mml.append(lambda t, v=view, cc=cc, b=bnkx: t.matmul(
                            b.ap[:, cc * C:(cc + 1) * C], v, ident_f[:],
                            is_transpose=True, start=(cc == 0), stop=(cc == NCH - 1)))
                    mm_group(bnkx, mml, extra_waits=[ev_gb])
                    gview = bcast3(gb_tmh, h32, NCH, 64, C)
                    ev_e1 = bank_read("dve", bnkx, lambda e, gv=gview, b=bnkx:
                                      e.tensor_tensor(
                                          out=wf3(WF["e1"], NCH, C, C),
                                          in0=b.ap.rearrange("p (a b) -> p a b", a=NCH),
                                          in1=gv, op=ALU.subtract))
                    mi_v = bass.AP(mask_i.ap().tensor, mask_i.ap().offset,
                                   [mask_i.ap().ap[0], [0, NCH], [1, C]])
                    ev_e2 = kb.op("dve", lambda e, mv=mi_v: e.tensor_tensor(
                        out=wf3(WF["e2"], NCH, C, C), in0=wf3(WF["e1"], NCH, C, C),
                        in1=mv, op=ALU.add), waits=[ev_e1], sig=True)
                    ev_di = kb.op("act", lambda e, hh=hh: e.activation(
                        out=wb(WB["di"] + hh * NCH * C, NCH * C),
                        in_=wf(WF["e2"], NCH * C), func=AF.Exp),
                        waits=[ev_e2, tap_ev[0]], sig=True)
                    ms_v = bass.AP(mask_s.ap().tensor, mask_s.ap().offset,
                                   [mask_s.ap().ap[0], [0, NCH], [1, C]])
                    ev_e3 = kb.op("dve", lambda e, mv=ms_v: e.tensor_tensor(
                        out=wf3(WF["e2"], NCH, C, C), in0=wf3(WF["e1"], NCH, C, C),
                        in1=mv, op=ALU.add), waits=[ev_di], sig=True)
                    gso = G_SLOT[p]
                    ev_ds = kb.op("act", lambda e, go=gso: e.activation(
                        out=wb(go, NCH * C), in_=wf(WF["e2"], NCH * C),
                        func=AF.Exp, scale=-1.0),
                        waits=[ev_e3, ev_slot_pe], sig=True)
                    kk_v = wb3(WB["kkkq"] + kl * NCH * 256, NCH, 256, C)
                    zso = Z_SLOT[p]
                    ev_a1 = kb.op("dve", lambda e, kv=kk_v, go=gso, zo=zso:
                                  e.tensor_tensor(
                                      out=wb3(zo, NCH, C, C), in0=wb3(go, NCH, C, C),
                                      in1=kv, op=ALU.mult),
                                  waits=[ev_ds, ev_kkkq, ev_slot_pe], sig=True)
                    b_v = bcast3(gb_tmh, 32 + h32, NCH, 64, C)
                    ev_a = kb.op("dve", lambda e, bv=b_v, zo=zso: e.tensor_tensor(
                        out=wb3(zo, NCH, C, C), in0=wb3(zo, NCH, C, C),
                        in1=bv, op=ALU.mult), waits=[ev_a1], sig=True)
                    kq_v = wb3(WB["kkkq"] + kl * NCH * 256 + 128, NCH, 256, C)
                    kb.op("dve", lambda e, kv=kq_v, hh=hh: e.tensor_tensor(
                        out=wb3(WB["wt"] + hh * NCH * C, NCH, C, C),
                        in0=wb3(WB["di"] + hh * NCH * C, NCH, C, C),
                        in1=kv, op=ALU.mult), waits=[ev_di])
                    i_v = bass.AP(ident_b.ap().tensor, ident_b.ap().offset,
                                  [ident_b.ap().ap[0], [0, NCH], [1, C]])
                    ev_g = kb.op("dve", lambda e, iv=i_v, go=gso, zo=zso:
                                 e.tensor_tensor(
                                     out=wb3(go, NCH, C, C), in0=wb3(zo, NCH, C, C),
                                     in1=iv, op=ALU.add), waits=[ev_a], sig=True)
                    ev_z1n = kb.op("dve", lambda e, iv=i_v, zo=zso:
                                   e.scalar_tensor_tensor(
                                       out=wb3(zo, NCH, C, C),
                                       in0=wb3(zo, NCH, C, C),
                                       scalar=-1.0, in1=iv, op0=ALU.mult,
                                       op1=ALU.add), waits=[ev_g], sig=True)
                    ev_zslot[hh] = ev_z1n
                    ed_v = bass.AP(work_bf.ap().tensor,
                                   work_bf.ap().offset + WB["di"] + hh * NCH * C + 127,
                                   [work_bf.ap().ap[0], [C, NCH]])
                    ex_v = bass.AP(exps_t.ap().tensor, exps_t.ap().offset + h32,
                                   [exps_t.ap().ap[0], [32, NCH]])
                    nonlocal_lcc = kb.op("dve", lambda e, hh=hh, ev2=ed_v, xv=ex_v:
                                   e.tensor_tensor(
                                       out=bass.AP(work_f32.ap().tensor,
                                                   work_f32.ap().offset + WF["lcc"] + hh * NCH,
                                                   [work_f32.ap().ap[0], [1, NCH]]),
                                       in0=ev2, in1=xv, op=ALU.mult),
                                   waits=[ev_di, ev_exps, ev_bl], sig=True)
                    return nonlocal_lcc

                for hp in (0, 2):
                    for hh in (hp, hp + 1):
                        ev_lcc = p1_prep(hh)
                    xt_cur = [None, None]
                    ev_xt = [None, None]
                    for hh in (hp, hp + 1):
                        p = hh % 2
                        mml = []
                        for cc in range(NCH):
                            mml.append(lambda t, cc=cc, zo=Z_SLOT[p]: t.matmul(
                                pt.ap[:, cc * C:(cc + 1) * C], wb(zo + cc * C, C),
                                ident_b[:], is_transpose=True,
                                start=(cc == 0), stop=(cc == NCH - 1)))
                        mm_group(pt, mml, extra_waits=[ev_zslot[hh]])
                        ev_xt[p] = bank_read("dve", pt,
                                             lambda e, d=XT_A[p]: e.tensor_copy(
                                                 wb(d, NCH * C), pt.ap[:, 0:NCH * C]),
                                             extra_waits=[ev_slot_pe])
                        xt_cur[p] = XT_A[p]
                    for step in range(2):
                        pevs = [None, None]
                        for hh in (hp, hp + 1):
                            p = hh % 2
                            bpn = PBANK[p]
                            mml = []
                            for cc in range(NCH):
                                mml.append(lambda t, cc=cc, b=bpn, c=xt_cur[p],
                                           go=G_SLOT[p]: t.matmul(
                                    b.ap[:, cc * C:(cc + 1) * C], wb(c + cc * C, C),
                                    wb(go + cc * C, C),
                                    start=(cc == 0), stop=(cc == NCH - 1)))
                            mm_group(bpn, mml, extra_waits=[ev_xt[p]])
                            pevs[p] = bpn
                        for hh in (hp, hp + 1):
                            p = hh % 2
                            bpn = pevs[p]
                            t_v = bass.AP(two_i.ap().tensor, two_i.ap().offset,
                                          [two_i.ap().ap[0], [0, NCH], [1, C]])
                            ev_zz = bank_read("dve", bpn,
                                              lambda e, b=bpn, tv=t_v, zo=Z_SLOT[p]:
                                              e.scalar_tensor_tensor(
                                                  out=wb3(zo, NCH, C, C),
                                                  in0=b.ap.rearrange(
                                                      "p (a b) -> p a b", a=NCH),
                                                  scalar=-1.0, in1=tv,
                                                  op0=ALU.mult, op1=ALU.add))
                            bq = XTBANK[p]
                            mml = []
                            for cc in range(NCH):
                                mml.append(lambda t, cc=cc, b=bq, c=xt_cur[p],
                                           zo=Z_SLOT[p]: t.matmul(
                                    b.ap[:, cc * C:(cc + 1) * C],
                                    wb(zo + cc * C, C), wb(c + cc * C, C),
                                    start=(cc == 0), stop=(cc == NCH - 1)))
                            mm_group(bq, mml, extra_waits=[ev_zz])
                            if step < 1:
                                dsto = XT_B[p] if xt_cur[p] == XT_A[p] else XT_A[p]
                            else:
                                dsto = WB["tt"] + hh * NCH * C
                            ev_xt[p] = bank_read("dve", bq,
                                                 lambda e, b=bq, d=dsto: e.tensor_copy(
                                                     wb(d, NCH * C),
                                                     b.ap[:, 0:NCH * C]),
                                                 extra_waits=[tap_ev[0]])
                            xt_cur[p] = dsto
                    ev_slot_pe = kb.now("pe")
                ev_p1 = kb.now("dve")
                if ti == 0 and g == 0:
                    tap("tt0", lambda: wb(WB["tt"], 4 * NCH * C), [128, 4 * NCH * C],
                        BF16, [ev_p1])
                    tap("di0", lambda: wb(WB["di"], 4 * NCH * C), [128, 4 * NCH * C],
                        BF16, [ev_p1])

                prev_ks = prev_qs = None
                for cc in range(NCH):
                    ev_sb = kb.op("dve", lambda e, g=g: e.tensor_copy(
                        wb(WB["sbf"], 512), s_f32[:, g * 512:(g + 1) * 512]),
                        waits=[w for w in (prev_ks, prev_qs, carry["sbf_pe"]) if w],
                        sig=True)
                    bKS, bQS = pb[3], pb[4]
                    mml = []
                    for hh in range(4):
                        kl = hh // 2
                        mml.append(lambda t, hh=hh, kl=kl, cc=cc, b=bKS: t.matmul(
                            b.ap[:, hh * 128:(hh + 1) * 128],
                            wb(WB["kg"] + kl * T + cc * C, C),
                            wb(WB["sbf"] + hh * 128, 128),
                            start=(hh == 0), stop=(hh == 3)))
                    mm_group(bKS, mml, extra_waits=[ev_sb])
                    prev_ks = bKS.wev
                    mml = []
                    for hh in range(4):
                        kl = hh // 2
                        mml.append(lambda t, hh=hh, kl=kl, cc=cc, b=bQS: t.matmul(
                            b.ap[:, hh * 128:(hh + 1) * 128],
                            wb(WB["qg"] + kl * T + cc * C, C),
                            wb(WB["sbf"] + hh * 128, 128),
                            start=(hh == 0), stop=(hh == 3)))
                    mm_group(bQS, mml, extra_waits=[ev_sb])
                    prev_qs = bQS.wev
                    bl_v = bcast3(bl_t, cc * 32 + 4 * g - 32 * (g // 8), 4, 1, C)
                    bl_v = bcast3(bl_t, cc * 32 + (4 * g) % 32, 4, 1, C)
                    ev_u1 = bank_read("dve", bKS, lambda e, bv=bl_v, b=bKS:
                                      e.tensor_tensor(
                                          out=wf3(WF["u1"], 4, C, C),
                                          in0=b.ap.rearrange("p (a b) -> p a b", a=4),
                                          in1=bv, op=ALU.mult),
                                      extra_waits=[ev_bl])
                    vt_v = wb3(WB["vt"] + cc * C, 4, NCH * C, C)
                    be_v = bcast3(gb_tmh, cc * 64 + 32 + (4 * g) % 32, 4, 1, C)
                    ev_u2 = kb.op("dve", lambda e, vv=vt_v, bv=be_v: e.tensor_tensor(
                        out=wf3(WF["g1"], 4, C, C), in0=vv, in1=bv, op=ALU.mult),
                        waits=[ev_vt, ev_gb], sig=True)
                    ev_u = kb.op("dve", lambda e: e.tensor_tensor(
                        out=wb(WB["u_sb"], 512), in0=wf(WF["g1"], 512),
                        in1=wf(WF["u1"], 512), op=ALU.subtract),
                        waits=[ev_u1, ev_u2], sig=True)
                    bD = pb[5]
                    mml = []
                    for hh in range(4):
                        mml.append(lambda t, hh=hh, cc=cc, b=bD: t.matmul(
                            b.ap[:, hh * 128:(hh + 1) * 128],
                            wb(WB["tt"] + hh * NCH * C + cc * C, C),
                            wb(WB["u_sb"] + hh * 128, 128),
                            start=(hh == 0), stop=(hh == 3)))
                    mm_group(bD, mml, extra_waits=[ev_u, ev_p1])
                    ev_dl = bank_read("dve", bD, lambda e, b=bD: e.tensor_copy(
                        wb(WB["dl"], 512), b.ap[:, 0:512]))
                    ed_v = bass.AP(work_bf.ap().tensor,
                                   work_bf.ap().offset + WB["di"] + cc * C + 127,
                                   [work_bf.ap().ap[0], [NCH * C, 4], [0, C]])
                    ev_dl2 = kb.op("dve", lambda e, ev2=ed_v, b=bD: e.tensor_tensor(
                        out=wb(WB["dl2"], 512),
                        in0=b.ap.rearrange("p (a b) -> p a b", a=4),
                        in1=ev2, op=ALU.mult), waits=[ev_dl], sig=True)
                    bD.rev = ev_dl2
                    bO = pb[3]
                    mml = []
                    for hh in range(4):
                        mml.append(lambda t, hh=hh, cc=cc, b=bO: t.matmul(
                            b.ap[:, hh * 128:(hh + 1) * 128],
                            wb(WB["wt"] + hh * NCH * C + cc * C, C),
                            wb(WB["dl"] + hh * 128, 128),
                            start=(hh == 0), stop=(hh == 3)))
                    mm_group(bO, mml, extra_waits=[ev_dl])
                    prev_ks = bO.wev
                    eg_v = bcast3(exps_t, cc * 32 + (4 * g) % 32, 4, 1, C)
                    ev_g1 = bank_read("dve", bQS, lambda e, gv=eg_v, b=bQS:
                                      e.tensor_tensor(
                                          out=wf3(WF["g1"], 4, C, C),
                                          in0=b.ap.rearrange("p (a b) -> p a b", a=4),
                                          in1=gv, op=ALU.mult))
                    ev_o = bank_read("dve", bO, lambda e, b=bO: e.tensor_tensor(
                        out=wf(WF["osb"], 512), in0=b.ap[:, 0:512],
                        in1=wf(WF["g1"], 512), op=ALU.add), extra_waits=[ev_g1])
                    ev_ms = None
                    for hh in range(4):
                        ev_ms = kb.op("act", lambda e, hh=hh: e.activation(
                            out=wf(WF["u1"] + hh * C, C),
                            in_=wf(WF["osb"] + hh * C, C),
                            func=AF.Square,
                            accum_out=wf(WF["ms"] + hh, 1)),
                            waits=[ev_o] if hh == 0 else (), sig=(hh == 3))
                    ev_r1 = kb.op("act", lambda e: e.activation(
                        out=wf(WF["rstd"], 4), in_=wf(WF["ms"], 4), func=AF.Ln,
                        scale=1.0 / DV, bias=eps_rms[:]), waits=[ev_ms], sig=True)
                    ev_r2 = kb.op("act", lambda e: e.activation(
                        out=wf(WF["rstd"], 4), in_=wf(WF["rstd"], 4), func=AF.Exp,
                        scale=-0.5), waits=[ev_r1], sig=True)
                    rs_v = bcast3(work_f32, WF["rstd"], 4, 1, C)
                    ev_gt1 = kb.op("dve", lambda e, rv=rs_v: e.tensor_tensor(
                        out=wf3(WF["g1"], 4, C, C), in0=wf3(WF["osb"], 4, C, C),
                        in1=rv, op=ALU.mult), waits=[ev_r2], sig=True)
                    ev_gt = kb.op("dve", lambda e, cc=cc: e.tensor_tensor(
                        out=wb(WB["gtd"], 512), in0=wf(WF["g1"], 512),
                        in1=wb(WB["zg"] + cc * 512, 512), op=ALU.mult),
                        waits=[ev_gt1, ev_zg], sig=True)
                    carry["zg_dve"] = ev_gt
                    mml = []
                    for hh in range(4):
                        mml.append(lambda t, hh=hh: t.matmul(
                            pt.ap[:, hh * 128:(hh + 1) * 128],
                            wb(WB["gtd"] + hh * 128, 128), ident_b[:],
                            is_transpose=True, start=(hh == 0), stop=(hh == 3)))
                    mm_group(pt, mml, extra_waits=[ev_gt])
                    aw = work_bf.ap()
                    dstg = bass.AP(aw.tensor, aw.offset + WB["gated"] + cc * C,
                                   [aw.ap[0], [T, 4], [1, C]])
                    bank_read("dve", pt, lambda e, d=dstg: e.tensor_copy(
                        d, pt.ap[:, 0:512].rearrange("p (a b) -> p a b", a=4)),
                        extra_waits=[carry["gated_pe"], tap_ev[0]])
                    bS = pb[4]
                    mml = []
                    for hh in range(4):
                        kl = hh // 2
                        mml.append(lambda t, hh=hh, kl=kl, cc=cc, b=bS: t.matmul(
                            b.ap[:, hh * 128:(hh + 1) * 128],
                            wb(WB["kt"] + (kl * NCH + cc) * C, C),
                            wb(WB["dl2"] + hh * 128, 128),
                            start=(hh == 0), stop=(hh == 3)))
                    mm_group(bS, mml, extra_waits=[ev_dl2, ev_kt])
                    prev_qs = bS.wev
                    for hh in range(4):
                        lcc_ap = bass.AP(work_f32.ap().tensor,
                                         work_f32.ap().offset + WF["lcc"] + hh * NCH + cc,
                                         [work_f32.ap().ap[0], [1, 1]])
                        bank_read("dve", bS, lambda e, hh=hh, g=g, sc=lcc_ap:
                                  e.scalar_tensor_tensor(
                                      out=s_f32[:, g * 512 + hh * 128:g * 512 + (hh + 1) * 128],
                                      in0=s_f32[:, g * 512 + hh * 128:g * 512 + (hh + 1) * 128],
                                      scalar=sc,
                                      in1=bS.ap[:, hh * 128:(hh + 1) * 128],
                                      op0=ALU.mult, op1=ALU.add),
                                  extra_waits=[ev_lcc])
                carry["sbf_pe"] = kb.now("pe")
                if ti == 0 and g == 0:
                    ev_gg = kb.now("dve")
                    tap("gated0", lambda: wb(WB["gated"], 4 * T), [128, 4 * T], BF16,
                        [ev_gg])
                    tap("s0", lambda: s_f32[:, 0:512], [128, 512], F32, [ev_gg])

                ev_gated_all = kb.now("dve")
                for oc in range(NKC):
                    bnko = pb[oc % 2]
                    mml = []
                    for kc in range(4):
                        mml.append(lambda t, kc=kc, oc=oc, b=bnko, ow=o_wo: t.matmul(
                            b.ap[:, 0:T],
                            ring[:, ow + kc * 2048 + oc * 128:
                                 ow + kc * 2048 + oc * 128 + 128],
                            wb(WB["gated"] + kc * T, T),
                            start=(kc == 0), stop=(kc == 3)))
                    mm_group(bnko, mml, extra_waits=[ev_wo, ev_gated_all])
                    if g == 0:
                        s_v = bass.AP(mbsb.ap().tensor, mbsb.ap().offset + 128,
                                      [mbsb.ap().ap[0], [256, NCH], [1, 128]])
                        m_v = bass.AP(mbsb.ap().tensor, mbsb.ap().offset,
                                      [mbsb.ap().ap[0], [256, NCH], [1, 128]])
                        an = xn_fm.ap()
                        xn_v = bass.AP(an.tensor, an.offset + oc * T,
                                       [an.ap[0], [C, NCH], [1, C]])
                        kb.op("dve", lambda e, xv=xn_v, sv=s_v: e.tensor_tensor(
                            out=wf3(WF["u1"], NCH, C, C), in0=xv, in1=sv,
                            op=ALU.mult), waits=[ev_mbsb, ev_xn])
                        kb.op("dve", lambda e, mv=m_v: e.tensor_tensor(
                            out=wf3(WF["u1"], NCH, C, C), in0=wf3(WF["u1"], NCH, C, C),
                            in1=mv, op=ALU.add))
                        bank_read("dve", bnko, lambda e, oc=oc, b=bnko: e.tensor_tensor(
                            out=attn_bf[:, oc * T:(oc + 1) * T], in0=b.ap[:, 0:T],
                            in1=wf(WF["u1"], T), op=ALU.add))
                    else:
                        bank_read("dve", bnko, lambda e, oc=oc, b=bnko: e.tensor_tensor(
                            out=attn_bf[:, oc * T:(oc + 1) * T], in0=b.ap[:, 0:T],
                            in1=attn_bf[:, oc * T:(oc + 1) * T], op=ALU.add))
                carry["gated_pe"] = kb.now("pe")
                ring_release(ent_wo, kb.now("pe"))

            ev_attn = kb.now("dve")
            if ti == 0:
                tap("attn0", lambda: attn_bf[:, 0:NKC * T], [128, NKC * T], BF16,
                    [ev_attn])
            if stop_after == "attn":
                break

            ev_hid = None
            prev_hid_pe = None
            for hc in range(NHC):
                o1, e_q1, ent1 = ring_push(f"q1_{hc}", SEG_Q12)
                if ti == 0 and hc == 0:
                    tap("ring_q1", lambda o=o1: ring[:, o:o + SEG_Q12],
                        [128, SEG_Q12], BF16, [e_q1])
                o2, e_q2, ent2 = ring_push(f"q2_{hc}", SEG_Q12)
                b1, b2 = pb[0], pb[1]
                mml = []
                for kc in range(NKC):
                    mml.append(lambda t, kc=kc, o=o1, b=b1: t.matmul(
                        b.ap[:, 0:T], ring[:, o + kc * 128:o + (kc + 1) * 128],
                        xn_fm[:, kc * T:(kc + 1) * T],
                        start=(kc == 0), stop=(kc == NKC - 1)))
                mm_group(b1, mml, extra_waits=[e_q1, ev_xn, ev_attn if hc == 0 else None])
                mml = []
                for kc in range(NKC):
                    mml.append(lambda t, kc=kc, o=o2, b=b2: t.matmul(
                        b.ap[:, 0:T], ring[:, o + kc * 128:o + (kc + 1) * 128],
                        xn_fm[:, kc * T:(kc + 1) * T],
                        start=(kc == 0), stop=(kc == NKC - 1)))
                mm_group(b2, mml, extra_waits=[e_q2])
                ev_sg = bank_read("act", b1, lambda e, b=b1: e.activation(
                    out=wb(WB_SIG, T), in_=b.ap[:, 0:T], func=AF.Sigmoid),
                    extra_waits=[ev_hid])
                if ti == 0 and hc == 0:
                    tap("sig0", lambda: wb(WB_SIG, T), [128, T], BF16, [ev_sg])
                ev_hid = bank_read("dve", b2, lambda e, hc=hc, b=b2: e.tensor_tensor(
                    out=wb(WB_HID + hc * T, T), in0=b.ap[:, 0:T],
                    in1=wb(WB_SIG, T), op=ALU.mult),
                    extra_waits=[ev_sg, tap_ev[0] if hc == 0 else None])
                ring_release(ent1, kb.now("pe"))
                ring_release(ent2, kb.now("pe"))
            if ti == 0:
                tap("hid0", lambda: wb(WB_HID, 4 * T), [128, 4 * T], BF16, [ev_hid])

            if ti == 0:
                fs_dy = {0: None, 1: None}
            for oc in range(NKC):
                o3, e_q3, ent3 = ring_push(f"q3_{oc}", SEG_Q3)
                bnkf = pb[oc % 2]
                mml = []
                for kc in range(NHC):
                    mml.append(lambda t, kc=kc, o=o3, b=bnkf: t.matmul(
                        b.ap[:, 0:T], ring[:, o + kc * 128:o + (kc + 1) * 128],
                        wb(WB_HID + kc * T, T),
                        start=(kc == 0), stop=(kc == NHC - 1)))
                mm_group(bnkf, mml, extra_waits=[e_q3, ev_hid])
                fo = (oc % 2) * T
                ev_f = bank_read("dve", bnkf, lambda e, oc=oc, b=bnkf, fo=fo:
                                 e.tensor_tensor(
                                     out=wf(WF["fsa"] + fo, T), in0=b.ap[:, 0:T],
                                     in1=attn_bf[:, oc * T:(oc + 1) * T], op=ALU.add),
                                 extra_waits=[fs_dy[oc % 2]])
                bt = pb[2 + oc % 2]
                mml = []
                for tc in range(NCH):
                    mml.append(lambda t, tc=tc, b=bt, fo=fo: t.matmul(
                        b.ap[:, tc * 128:(tc + 1) * 128],
                        wf(WF["fsa"] + fo + tc * 128, 128), ident_f[:],
                        is_transpose=True, start=(tc == 0), stop=(tc == NCH - 1)))
                mm_group(bt, mml, extra_waits=[ev_f])
                ev_fb = bank_read("dve", bt, lambda e, b=bt, fo=fo: e.tensor_copy(
                    wf(WF["fsb"] + fo, T), b.ap[:, 0:512]))
                dsty = bass.AP(y_h.tensor, y_h.offset + ti * T * D + oc * 128,
                               [[D, 128], [128 * D, NCH], [1, 128]])
                srcy = bass.AP(work_f32.ap().tensor,
                               work_f32.ap().offset + WF["fsb"] + fo,
                               [work_f32.ap().ap[0], [128, NCH], [1, 128]])
                fs_dy[oc % 2] = kb.op("sp", lambda e, d=dsty, s=srcy:
                                      e.dma_start(out=d, in_=s),
                                      waits=[ev_fb], dma="dy")
                ring_release(ent3, kb.now("pe"))

            if stop_after == "tile0":
                break

        kb.wait_only("sp", [Ev("dy", kb.semtot.get("dy", 0))])

        kb.check_deadlock()

        @block.tensor
        def _(eng):
            kb.emit("pe", eng, sems)

        @block.vector
        def _(eng):
            kb.emit("dve", eng, sems)

        @block.scalar
        def _(eng):
            kb.emit("act", eng, sems)

        @block.gpsimd
        def _(eng):
            kb.emit("pool", eng, sems)

        @block.sync
        def _(eng):
            kb.emit("sp", eng, sems)

    return nc, tap_out


def kernel(L0_attn_hidden_states, L0_attn_w_qkv, L0_attn_w_z, L0_attn_w_b,
           L0_attn_w_a, L0_attn_w_out, L0_attn_conv_w, L0_attn_dt_bias,
           L0_attn_A_log, L0_attn_norm_weight, L0_ffn_w1, L0_ffn_w2, L0_ffn_w3):
    x = np.ascontiguousarray(np.asarray(L0_attn_hidden_states, dtype=np.float32))
    common = {
        "w_qkv": np.ascontiguousarray(L0_attn_w_qkv, dtype=np.float32),
        "w_z": np.ascontiguousarray(L0_attn_w_z, dtype=np.float32),
        "w_b": np.ascontiguousarray(L0_attn_w_b, dtype=np.float32),
        "w_a": np.ascontiguousarray(L0_attn_w_a, dtype=np.float32),
        "w_out": np.ascontiguousarray(L0_attn_w_out, dtype=np.float32),
        "conv_w": np.ascontiguousarray(L0_attn_conv_w, dtype=np.float32),
        "dt_bias": np.ascontiguousarray(L0_attn_dt_bias, dtype=np.float32),
        "A_log": np.ascontiguousarray(L0_attn_A_log, dtype=np.float32),
        "norm_weight": np.ascontiguousarray(L0_attn_norm_weight, dtype=np.float32),
        "w1": np.ascontiguousarray(L0_ffn_w1, dtype=np.float32),
        "w2": np.ascontiguousarray(L0_ffn_w2, dtype=np.float32),
        "w3": np.ascontiguousarray(L0_ffn_w3, dtype=np.float32),
    }
    nc, _ = build_core()
    in_maps = [dict(common, x=np.ascontiguousarray(x[b])) for b in range(B)]
    res = run_bass_kernel_spmd(nc, in_maps, core_ids=list(range(B)))
    return np.stack([res.results[b]["y"] for b in range(B)], axis=0)


# version 10
# speedup vs baseline: 1.0137x; 1.0137x over previous
import numpy as np
from contextlib import ExitStack

import concourse.bass as bass
import concourse.mybir as mybir
from concourse.bass_utils import run_bass_kernel_spmd

F32 = mybir.dt.float32
BF16 = mybir.dt.bfloat16
AF = mybir.ActivationFunctionType
ALU = mybir.AluOpType

B, S, D = 8, 4096, 2048
HK, HV = 16, 32
DK = DV = 128
KDIM, VDIM = HK * DK, HV * DV
CONV_DIM, KSIZE = 2 * KDIM + VDIM, 4
HFF = 5632
NHC = HFF // 128
T = 512
NT = S // T
C = 128
NCH = T // C
NGRP = 8
NKC = D // 128
LNEPS, L2EPS, RMSEPS, FFEPS = 1e-5, 1e-6, 1e-6, 1e-8
RING_COLS = 18176

ENGINES = ("pe", "dve", "act", "pool", "sp")
ENG_SEM = {e: "s_" + e for e in ENGINES}


class Ev:
    __slots__ = ("sem", "val")

    def __init__(self, sem, val):
        self.sem, self.val = sem, val


class KB:
    def __init__(self):
        self.ops = {e: [] for e in ENGINES}
        self.semtot = {}
        self.waited = {e: {} for e in ENGINES}

    def now(self, eng):
        return Ev(ENG_SEM[eng], self.semtot.get(ENG_SEM[eng], 0))

    def _wait(self, eng, ev):
        if ev is None or ev.val <= 0:
            return
        cur = self.waited[eng].get(ev.sem, 0)
        if ev.val > cur:
            self.ops[eng].append(("w", ev.sem, ev.val))
            self.waited[eng][ev.sem] = ev.val

    def op(self, eng, fn, waits=(), sig=False, dma=None):
        for w in waits:
            self._wait(eng, w)
        sem = inc = None
        if dma is not None:
            sem, inc = dma, 16
        elif sig:
            sem, inc = ENG_SEM[eng], 1
        if sem is not None:
            self.semtot[sem] = self.semtot.get(sem, 0) + inc
            self.ops[eng].append(("c", fn, sem, inc))
            return Ev(sem, self.semtot[sem])
        self.ops[eng].append(("c", fn, None, 0))
        return None

    def wait_only(self, eng, evs):
        for w in evs:
            self._wait(eng, w)

    def check_deadlock(self):
        ptr = {e: 0 for e in ENGINES}
        cnt = {}
        progress = True
        while progress:
            progress = False
            for e in ENGINES:
                while ptr[e] < len(self.ops[e]):
                    it = self.ops[e][ptr[e]]
                    if it[0] == "w" and cnt.get(it[1], 0) < it[2]:
                        break
                    if it[0] == "c" and it[2] is not None:
                        cnt[it[2]] = cnt.get(it[2], 0) + it[3]
                    ptr[e] += 1
                    progress = True
        stuck = {e: ptr[e] for e in ENGINES if ptr[e] < len(self.ops[e])}
        if stuck:
            detail = {}
            for e in stuck:
                it = self.ops[e][ptr[e]]
                detail[e] = (f"op {ptr[e]}/{len(self.ops[e])} waiting "
                             f"{it[1]}>={it[2]} (now {cnt.get(it[1], 0)})")
            raise RuntimeError(f"build-time deadlock: {detail}")

    def emit(self, eng, engobj, sems):
        for item in self.ops[eng]:
            if item[0] == "w":
                engobj.wait_ge(sems[item[1]], item[2])
            else:
                inst = item[1](engobj)
                if item[2] is not None:
                    inst.then_inc(sems[item[2]], item[3])


class Bank:
    def __init__(self, ap):
        self.ap = ap
        self.wev = None
        self.rev = None


def build_core(taps=None, stop_after="full"):
    taps = taps or []
    tap_out = []

    nc = bass.Bass("TRN2", target_bir_lowering=False, debug=False)

    x_h = nc.dram_tensor("x", [S, D], F32, kind="ExternalInput").ap()
    wqkv_h = nc.dram_tensor("w_qkv", [D, CONV_DIM], F32, kind="ExternalInput").ap()
    wz_h = nc.dram_tensor("w_z", [D, VDIM], F32, kind="ExternalInput").ap()
    wb_h = nc.dram_tensor("w_b", [D, HV], F32, kind="ExternalInput").ap()
    wa_h = nc.dram_tensor("w_a", [D, HV], F32, kind="ExternalInput").ap()
    wout_h = nc.dram_tensor("w_out", [VDIM, D], F32, kind="ExternalInput").ap()
    convw_h = nc.dram_tensor("conv_w", [CONV_DIM, KSIZE], F32, kind="ExternalInput").ap()
    dtb_h = nc.dram_tensor("dt_bias", [HV], F32, kind="ExternalInput").ap()
    alog_h = nc.dram_tensor("A_log", [HV], F32, kind="ExternalInput").ap()
    nw_h = nc.dram_tensor("norm_weight", [DV], F32, kind="ExternalInput").ap()
    w1_h = nc.dram_tensor("w1", [HFF, D], F32, kind="ExternalInput").ap()
    w2_h = nc.dram_tensor("w2", [HFF, D], F32, kind="ExternalInput").ap()
    w3_h = nc.dram_tensor("w3", [D, HFF], F32, kind="ExternalInput").ap()
    y_h = nc.dram_tensor("y", [S, D], F32, kind="ExternalOutput").ap()

    SEG_QQKK = 16 * 512
    SEG_WOUT = 4 * 2048
    SEG_Q12 = 16 * 128
    SEG_Q3 = NHC * 128
    tape_cols = 0
    seg_off = {}

    def add_seg(name, cols):
        nonlocal tape_cols
        seg_off[name] = tape_cols
        tape_cols += cols

    for g in range(NGRP):
        add_seg(f"qqkk{g}", SEG_QQKK)
        add_seg(f"vvvv{g}", SEG_QQKK)
        add_seg(f"zzzz{g}", SEG_QQKK)
        add_seg(f"wout{g}", SEG_WOUT)
    for hc in range(NHC):
        add_seg(f"q1_{hc}", SEG_Q12)
        add_seg(f"q2_{hc}", SEG_Q12)
    for oc in range(NKC):
        add_seg(f"q3_{oc}", SEG_Q3)
    wtape = nc.dram_tensor("wtape", [128, tape_cols], BF16, kind="Internal").ap()

    kb = KB()

    with ExitStack() as ctx:
        def sb(name, shape, dt):
            return ctx.enter_context(nc.sbuf_tensor(name, shape, dt))

        xn_fm = sb("xn_fm", [128, NKC * T], BF16)
        attn_bf = sb("attn_bf", [128, NKC * T], BF16)
        x_tm = sb("x_tm", [128, 2 * NCH * D], BF16)
        work_bf = sb("work_bf", [128, 23296], BF16)
        work_f32 = sb("work_f32", [128, 6720], F32)
        ring = sb("ring", [128, RING_COLS], BF16)
        s_f32 = sb("s_f32", [128, HV * DV], F32)
        wba = sb("wba", [128, NKC * 64], BF16)
        conv_w = sb("conv_w_sb", [128, 64 * KSIZE], F32)
        conv_hist = sb("conv_hist", [128, 64 * (KSIZE - 1)], BF16)
        ident_b = sb("ident_b", [128, 128], BF16)
        ident_f = sb("ident_f", [128, 128], F32)
        two_i = sb("two_i", [128, 128], BF16)
        mask_i = sb("mask_i", [128, 128], F32)
        mask_s = sb("mask_s", [128, 128], F32)
        sel4 = sb("sel4", [4, 4 * 128], F32)
        ones_cb = sb("ones_cb", [128, 1], BF16)
        ones_rb = sb("ones_rb", [1, 128], BF16)
        ones_rf = sb("ones_rf", [1, 128], F32)
        eps_ln = sb("eps_ln", [128, 1], F32)
        eps_l2 = sb("eps_l2", [128, 1], F32)
        eps_rms = sb("eps_rms", [128, 1], F32)
        qsc_b = sb("qsc_b", [128, 1], F32)
        zero_c = sb("zero_c", [128, 1], F32)
        one_c = sb("one_c", [128, 1], F32)
        nw_c = sb("nw_c", [128, 1], F32)
        dtb_c = sb("dtb_c", [64, 1], F32)
        nega_c = sb("nega_c", [64, 1], F32)
        ba_sb = sb("ba_sb", [64, 3 * T], F32)
        gb_tmh = sb("gb_tmh", [128, NCH * 64], F32)
        exps_t = sb("exps_t", [128, NCH * 32], F32)
        bl_t = sb("bl_t", [128, NCH * 32], F32)
        st4 = sb("st4", [128, 16], F32)
        mrow = sb("mrow", [4, 256], F32)
        mbsb = sb("mbsb", [128, NCH * 256], BF16)
        gam_sb = sb("gam_sb", [128, 4], F32)
        nth_sb = sb("nth_sb", [128, 4], F32)
        absc = sb("absc", [128, NHC], F32)
        gred = sb("gred", [128, 128], F32)

        WB = {}
        _o = 0
        for nm, w in [("qg", 2 * T), ("kg", 2 * T), ("vg", 4 * T), ("zg", 4 * T),
                      ("kkkq", 2 * NCH * 256), ("di", 4 * NCH * C),
                      ("wt", 4 * NCH * C), ("tt", 4 * NCH * C), ("xt0", NCH * C),
                      ("vt", 4 * NCH * C), ("kt", 2 * NCH * C),
                      ("at", NCH * C), ("g_nw", NCH * C), ("z1", NCH * C),
                      ("u_sb", 4 * C), ("gtd", 4 * C), ("dl", 4 * C), ("dl2", 4 * C),
                      ("sbf", 4 * C)]:
            WB[nm] = _o
            _o += w
        assert _o <= 23296, _o
        WB["gated"] = WB["kkkq"]
        WB["xt1"] = WB["at"]
        WB["qn"] = WB["u_sb"]
        WB["sqb"] = WB["gtd"]
        WB["cvo"] = WB["dl"]
        WB["mxp"] = WB["dl2"]
        WB_HID = 0
        WB_SIG = 22528
        assert NHC * T <= 22528 + 512

        PRE_Q = 0
        PRE_Q2 = 8192
        PRE_T = 16384
        WF = {}
        _o = 0
        for nm, w in [("e1", NCH * C), ("e2", NCH * C), ("osb", 4 * C), ("g1", 4 * C),
                      ("u1", 4 * C), ("sq", D), ("fsa", 2 * T), ("fsb", 2 * T),
                      ("ms", 8), ("rstd", 8), ("lcc", 16)]:
            WF[nm] = _o
            _o += w
        assert _o <= 6720, _o
        PRE_SQ = 0

        pb = [Bank(ctx.enter_context(nc.psum_tensor(f"pb{i}", [128, 512], F32)).ap())
              for i in range(7)]
        pt = Bank(ctx.enter_context(nc.psum_tensor("pt7", [128, 1024], BF16)).ap())

        sem_names = list(ENG_SEM.values()) + ["dw", "dx", "dy", "dpre"]
        sems = {n: ctx.enter_context(nc.semaphore(n)) for n in sem_names}
        block = ctx.enter_context(nc.Block())

        def mm_group(bank, mm_list, extra_waits=()):
            waits = [w for w in extra_waits if w is not None]
            if bank.rev is not None:
                waits.append(bank.rev)
            n = len(mm_list)
            for i, fn in enumerate(mm_list):
                if i < n - 1:
                    kb.op("pe", fn, waits=waits if i == 0 else ())
                else:
                    bank.wev = kb.op("pe", fn, waits=waits if i == 0 else (), sig=True)
            return bank.wev

        def bank_read(eng, bank, fn, extra_waits=()):
            waits = [bank.wev] + [w for w in extra_waits if w is not None]
            ev = kb.op(eng, fn, waits=waits, sig=True)
            bank.rev = ev
            return ev

        ring_state = {"cursor": 0, "occ": []}

        def ring_push(seg_name, cols):
            cur = ring_state["cursor"]
            if cur + cols > RING_COLS:
                cur = 0
            start, end = cur, cur + cols
            waits, keep = [], []
            for ent in ring_state["occ"]:
                if ent[0] < end and ent[1] > start:
                    assert ent[2] is not None, (
                        f"ring overlap with unreleased segment {ent} pushing "
                        f"{seg_name} at {start}:{end}")
                    waits.append(ent[2])
                else:
                    keep.append(ent)
            ring_state["occ"] = keep
            ent = [start, end, None]
            ring_state["occ"].append(ent)
            toff = seg_off[seg_name]
            ev = kb.op("sp",
                       lambda e, a=toff, n=cols, c=start: e.dma_start(
                           out=ring[:, c:c + n], in_=wtape[:, a:a + n]),
                       waits=waits, dma="dw")
            ring_state["cursor"] = end
            return start, ev, ent

        def ring_release(ent, ev):
            ent[2] = ev

        tap_ev = [None]

        def tap(name, ap_fn, shape, dt, waits):
            if name in taps:
                th = nc.dram_tensor("tap_" + name, shape, dt, kind="ExternalOutput").ap()
                tap_out.append("tap_" + name)
                tap_ev[0] = kb.op("sp", lambda e: e.dma_start(out=th, in_=ap_fn()),
                                  waits=waits, dma="dy")

        def wb(off, cols):
            return work_bf[:, off:off + cols]

        def wf(off, cols):
            return work_f32[:, off:off + cols]

        def wb3(off, n, step, cols):
            a = work_bf.ap()
            return bass.AP(a.tensor, a.offset + off, [a.ap[0], [step, n], [1, cols]])

        def wf3(off, n, step, cols):
            a = work_f32.ap()
            return bass.AP(a.tensor, a.offset + off, [a.ap[0], [step, n], [1, cols]])

        def bcast3(tensor, off, n, step, cols):
            a = tensor.ap()
            return bass.AP(a.tensor, a.offset + off, [a.ap[0], [step, n], [0, cols]])

        def aff(out_ap, fill, pattern, cm, base=0, cmp=ALU.not_equal):
            return lambda e: e.affine_select(out=out_ap, in_=out_ap, compare_op=cmp,
                                             fill=fill, base=base, pattern=pattern,
                                             channel_multiplier=cm)

        kb.op("pool", lambda e: e.memset(ident_b[:], 0.0))
        kb.op("pool", aff(ident_b[:], 1.0, [[-1, 128]], 1))
        kb.op("pool", lambda e: e.memset(ident_f[:], 0.0))
        kb.op("pool", aff(ident_f[:], 1.0, [[-1, 128]], 1))
        kb.op("pool", lambda e: e.memset(two_i[:], 0.0))
        kb.op("pool", aff(two_i[:], 2.0, [[-1, 128]], 1))
        kb.op("pool", lambda e: e.memset(mask_i[:], 0.0))
        kb.op("pool", aff(mask_i[:], -1e9, [[1, 128]], -1, cmp=ALU.is_ge))
        kb.op("pool", lambda e: e.memset(mask_s[:], 0.0))
        kb.op("pool", aff(mask_s[:], 1e9, [[-1, 128]], 1, cmp=ALU.is_gt))
        kb.op("pool", lambda e: e.memset(sel4[:], 0.0))
        kb.op("pool", aff(sel4[:], 1.0, [[-1, 4], [0, 128]], 1))
        for t_, v_ in ((ones_cb, 1.0), (ones_rb, 1.0), (ones_rf, 1.0),
                       (eps_ln, LNEPS), (eps_l2, L2EPS), (eps_rms, RMSEPS),
                       (qsc_b, float(np.log(DK ** -0.5))), (zero_c, 0.0),
                       (one_c, 1.0)):
            kb.op("pool", lambda e, t=t_, v=v_: e.memset(t[:], v))
        kb.op("pool", lambda e: e.memset(conv_hist[:], 0.0))
        kb.op("pool", lambda e: e.memset(s_f32[:], 0.0))
        ev_nw = kb.op("pool", lambda e: e.dma_start(out=nw_c[:], in_=nw_h[:, None]),
                      dma="dpre")
        kb.op("pool", lambda e: e.dma_start(out=dtb_c[32:64, :], in_=dtb_h[:, None]),
              dma="dpre")
        ev_al = kb.op("pool", lambda e: e.dma_start(out=nega_c[32:64, :],
                                                    in_=alog_h[:, None]), dma="dpre")
        ev_cw = kb.op("pool",
                      lambda e: e.dma_start(
                          out=conv_w.ap().rearrange("p (o j) -> p o j", o=64),
                          in_=convw_h.rearrange("(o p) j -> p o j", p=128)),
                      dma="dpre")
        ev_consts = kb.op("pool", lambda e: e.memset(absc[:], 0.0), sig=True)

        ev_na1 = kb.op("act", lambda e: e.activation(out=nega_c[32:64, :],
                                                     in_=nega_c[32:64, :], func=AF.Exp),
                       waits=[ev_al], sig=True)
        ev_nega = kb.op("dve", lambda e: e.tensor_scalar_mul(
            nega_c[32:64, :], nega_c[32:64, :], -1.0), waits=[ev_na1], sig=True)

        for g in range(NGRP if stop_after != "consts" else 0):
            so = seg_off[f"qqkk{g}"]
            for half, c0 in ((0, 2 * g * 128), (1, (16 + 2 * g) * 128)):
                src = bass.AP(wqkv_h.tensor, wqkv_h.offset + c0,
                              [[128 * CONV_DIM, NKC], [CONV_DIM, 128], [1, 256]])
                dst = bass.AP(wtape.tensor, wtape.offset + so + half * 256,
                              [[512, NKC], [tape_cols, 128], [1, 256]])
                kb.op("pool", lambda e, s=src, d=dst: e.dma_start(out=d, in_=s),
                      dma="dpre")
            so = seg_off[f"vvvv{g}"]
            src = bass.AP(wqkv_h.tensor, wqkv_h.offset + (32 + 4 * g) * 128,
                          [[128 * CONV_DIM, NKC], [CONV_DIM, 128], [1, 512]])
            dst = bass.AP(wtape.tensor, wtape.offset + so,
                          [[512, NKC], [tape_cols, 128], [1, 512]])
            kb.op("pool", lambda e, s=src, d=dst: e.dma_start(out=d, in_=s), dma="dpre")
            so = seg_off[f"zzzz{g}"]
            src = bass.AP(wz_h.tensor, wz_h.offset + g * 512,
                          [[128 * VDIM, NKC], [VDIM, 128], [1, 512]])
            dst = bass.AP(wtape.tensor, wtape.offset + so,
                          [[512, NKC], [tape_cols, 128], [1, 512]])
            kb.op("pool", lambda e, s=src, d=dst: e.dma_start(out=d, in_=s), dma="dpre")
        ev_wba = None
        for (wsrc, half) in ((wb_h, 0), (wa_h, 1)):
            src = bass.AP(wsrc.tensor, wsrc.offset,
                          [[HV, 128], [128 * HV, NKC], [1, 32]])
            a = wba.ap()
            dst = bass.AP(a.tensor, a.offset + half * 32,
                          [a.ap[0], [64, NKC], [1, 32]])
            ev_wba = kb.op("pool", lambda e, s=src, d=dst: e.dma_start(out=d, in_=s),
                           dma="dpre")

        ev_prev_tape = None
        for g in range(NGRP if stop_after != "consts" else 0):
            for kc in range(4):
                r0 = g * 512 + kc * 128
                src = bass.AP(wout_h.tensor, wout_h.offset + r0 * D, [[D, 128], [1, D]])
                evl = kb.op("pool", lambda e, s=src: e.dma_start(out=wb(PRE_Q, D), in_=s),
                            waits=[ev_prev_tape] if ev_prev_tape else [], dma="dpre")
                ev_sc = kb.op("dve", lambda e: e.tensor_scalar_mul(
                    wb(PRE_Q, D), wb(PRE_Q, D), nw_c[:]), waits=[evl, ev_nw], sig=True)
                so = seg_off[f"wout{g}"] + kc * 2048
                ev_prev_tape = kb.op("sp", lambda e, a=so: e.dma_start(
                    out=wtape[:, a:a + 2048], in_=wb(PRE_Q, D)),
                    waits=[ev_sc], dma="dw")

        ev_gam_row = {}
        _gl = ((1, w1_h, NHC, D), (2, w2_h, NHC, D), (3, w3_h, NKC, HFF)) \
            if stop_after not in ("consts", "tape") else ()
        for mi, wh, nslab, cols in _gl:
            ev_abs = None
            for sl in range(nslab):
                src = bass.AP(wh.tensor, wh.offset + sl * 128 * cols,
                              [[cols, 128], [1, cols]])
                po = (sl % 2) * 2048 if cols == D else 0
                war = ev_abs if (cols != D or sl >= 2) else None
                evl = kb.op("pool", lambda e, s=src, c=cols, p=po: e.dma_start(
                    out=wf(PRE_SQ + p, c), in_=s),
                    waits=[w for w in (war,) if w], dma="dpre")
                ev_abs = kb.op("act", lambda e, j=sl, c=cols, p=po: e.activation(
                    out=wf(PRE_SQ + p, c), in_=wf(PRE_SQ + p, c), func=AF.Abs,
                    accum_out=absc[:, j:j + 1]), waits=[evl, ev_consts], sig=True)
            ev_red = kb.op("act", lambda e, m=mi, n=nslab: e.activation(
                out=gred[:, 0:n], in_=absc[:, 0:n], func=AF.Copy,
                accum_out=st4[:, 12 + m:13 + m]), waits=[ev_abs], sig=True)
            bnk = pb[2]
            mm_group(bnk, [lambda t, m=mi, b=bnk: t.matmul(
                b.ap[0:1, 0:128], st4[:, 12 + m:13 + m], ident_f[:],
                is_transpose=True, start=True, stop=True)],
                extra_waits=[ev_red])
            ev_gam_row[mi] = bank_read("act", bnk, lambda e, m=mi, b=bnk: e.activation(
                out=gred[0:1, 0:128], in_=b.ap[0:1, 0:128], func=AF.Copy,
                accum_out=st4[0:1, 8 + m:9 + m]))
        ev_th = None
        for mi in ((1, 2, 3) if stop_after not in ("consts", "tape") else ()):
            numel = HFF * D
            ev1 = kb.op("dve", lambda e, m=mi, n=numel: e.tensor_scalar(
                out=st4[0:1, 8 + m:9 + m], in0=st4[0:1, 8 + m:9 + m],
                scalar1=0.5 / n, scalar2=0.5 * FFEPS, op0=ALU.mult, op1=ALU.add),
                waits=[ev_gam_row[mi]], sig=True)
            bnk = pb[2]
            mm_group(bnk, [lambda t, m=mi, b=bnk: t.matmul(
                b.ap[:, 0:1], ones_rf[:], st4[0:1, 8 + m:9 + m],
                start=True, stop=True)], extra_waits=[ev1])
            ev2 = bank_read("dve", bnk, lambda e, m=mi, b=bnk: e.tensor_copy(
                gam_sb[:, m:m + 1], b.ap[:, 0:1]))
            ev_th = kb.op("dve", lambda e, m=mi: e.tensor_scalar_mul(
                nth_sb[:, m:m + 1], gam_sb[:, m:m + 1], -1.0), waits=[ev2], sig=True)
        tap("gamma_th", lambda: gam_sb[:, 0:4], [128, 4], F32, [ev_th])
        tap("pre_st4", lambda: st4[:, 0:16], [128, 16], F32, [ev_th])
        tap("pre_absc", lambda: absc[:, 0:NHC], [128, NHC], F32, [ev_th])

        ev_presq_free = [ev_gam_row.get(3), ev_gam_row.get(3)]
        ev_stage_pe = [None, None]
        ev_tape_slot = [None, None]
        par_ct = [0]

        def quant_and_transpose(src_ap, cols, mi, nblk, seg):
            nonlocal ev_prev_tape
            if cols == D:
                par = par_ct[0] % 2
                par_ct[0] += 1
                oq, oq2, ot = PRE_Q + par * 2048, PRE_Q2 + par * 2048, \
                    PRE_T + par * 2560
                presq = PRE_SQ + par * 2048
            else:
                par = 0
                oq, oq2, ot = PRE_Q, PRE_Q2, PRE_T
                presq = PRE_SQ
            waitl = [ev_presq_free[par]]
            if cols != D:
                waitl += [ev_presq_free[1 - par], ev_stage_pe[1 - par],
                          ev_tape_slot[1 - par]]
            evl = kb.op("pool", lambda e, s=src_ap, c=cols, o=presq: e.dma_start(
                out=wf(o, c), in_=s),
                waits=[w for w in waitl if w], dma="dpre")
            kb.op("dve", lambda e, c=cols, m=mi, o=presq, q=oq:
                  e.tensor_single_scalar(
                      wb(q, c), wf(o, c), gam_sb[:, m:m + 1], ALU.is_gt),
                  waits=[w for w in (evl, ev_th, ev_stage_pe[par]) if w])
            ev_presq_free[par] = kb.op(
                "dve", lambda e, c=cols, m=mi, o=presq, q=oq2:
                e.tensor_single_scalar(
                    wb(q, c), wf(o, c), nth_sb[:, m:m + 1], ALU.is_lt), sig=True)
            ev_q = kb.op("dve", lambda e, c=cols, q=oq, q2=oq2: e.tensor_tensor(
                out=wb(q, c), in0=wb(q, c), in1=wb(q2, c),
                op=ALU.subtract), sig=True)
            ev_ev = None
            tape_war = ev_tape_slot[par]
            for g0 in range(0, nblk, 8):
                gn = min(8, nblk - g0)
                mml = []
                for j in range(gn):
                    blkc = oq + (g0 + j) * 128
                    mml.append(lambda t, bc=blkc, jj=j, gn=gn: t.matmul(
                        pt.ap[:, jj * 128:(jj + 1) * 128], wb(bc, 128), ident_b[:],
                        is_transpose=True, start=(jj == 0), stop=(jj == gn - 1)))
                mm_group(pt, mml, extra_waits=[ev_q, ev_consts])
                ev_ev = bank_read("dve", pt, lambda e, g0=g0, gn=gn, o=ot:
                                  e.tensor_copy(
                                      wb(o + g0 * 128, gn * 128),
                                      pt.ap[:, 0:gn * 128]),
                                  extra_waits=[tape_war] if g0 == 0 else ())
            ev_stage_pe[par] = kb.now("pe")
            ev = kb.op("sp", lambda e, a=seg_off[seg], n=nblk, o=ot: e.dma_start(
                out=wtape[:, a:a + n * 128], in_=wb(o, n * 128)),
                waits=[ev_ev], dma="dw")
            ev_tape_slot[par] = ev
            ev_prev_tape = ev

        _ql = ((1, w1_h), (2, w2_h)) if stop_after not in ("consts", "tape") else ()
        for mi, wh in _ql:
            for hb in range(NHC):
                src = bass.AP(wh.tensor, wh.offset + hb * 128 * D, [[D, 128], [1, D]])
                quant_and_transpose(src, D, mi, NKC, f"q{mi}_{hb}")
        for ob in range(NKC if stop_after not in ("consts", "tape") else 0):
            src = bass.AP(w3_h.tensor, w3_h.offset + ob * 128 * HFF,
                          [[HFF, 128], [1, HFF]])
            quant_and_transpose(src, HFF, 3, NHC, f"q3_{ob}")

        ev_tape_done = Ev("dw", kb.semtot.get("dw", 0))
        ev_pre_dve = kb.now("dve")
        ev_pre_pe = kb.now("pe")
        ev_pre_act = kb.now("act")

        x_free = [None, None]

        _nt = 0 if stop_after in ("consts", "tape", "pre") else NT
        if stop_after in ("xload", "lnstats", "lnT", "ln1",
                          "ba1", "ba2", "ba3"):
            _nt = 1
        for ti in range(_nt):
            ring_state["cursor"] = 0
            png = ti % 2
            xoff = png * NCH * D
            a = x_tm.ap()
            dst_x = bass.AP(a.tensor, a.offset + xoff, [a.ap[0], [D, NCH], [1, D]])
            src_x = bass.AP(x_h.tensor, x_h.offset + ti * T * D,
                            [[D, 128], [128 * D, NCH], [1, D]])
            wv = [x_free[png]] if x_free[png] else []
            ev_x = kb.op("pool", lambda e, s=src_x, d=dst_x: e.dma_start(out=d, in_=s),
                         waits=wv, dma="dx")
            if ti == 0:
                tap("x_tm", lambda: x_tm[:, 0:NCH * D], [128, NCH * D], BF16, [ev_x])
            if stop_after == "xload":
                break

            ev_st = None
            first_w = [ev_x]
            if ti == 0:
                first_w += [ev_pre_act, ev_pre_dve, ev_pre_pe, ev_consts]
            else:
                first_w += [kb.now("dve"), kb.now("pe")]
            for tc in range(NCH):
                kb.op("act", lambda e, tc=tc, xo=xoff: e.activation(
                    out=wf(WF["sq"], D), in_=x_tm[:, xo + tc * D:xo + (tc + 1) * D],
                    func=AF.Square, accum_out=st4[:, 4 + tc:5 + tc]),
                    waits=first_w if tc == 0 else ())
                ev_st = kb.op("act", lambda e, tc=tc, xo=xoff: e.activation(
                    out=wf(WF["sq"], D), in_=x_tm[:, xo + tc * D:xo + (tc + 1) * D],
                    func=AF.Copy, accum_out=st4[:, tc:tc + 1]), sig=True)
            e1w = [ev_st]
            if ti == 0:
                e1w += [ev_pre_pe, ev_tape_done]
            e1 = kb.op("dve", lambda e: e.tensor_scalar_mul(st4[:, 0:8], st4[:, 0:8],
                                                            1.0 / D),
                       waits=e1w, sig=True)
            e2 = kb.op("dve", lambda e: e.tensor_tensor(
                out=st4[:, 8:12], in0=st4[:, 0:4], in1=st4[:, 0:4], op=ALU.mult),
                waits=[e1], sig=True)
            e3 = kb.op("dve", lambda e: e.tensor_tensor(
                out=st4[:, 8:12], in0=st4[:, 4:8], in1=st4[:, 8:12], op=ALU.subtract),
                waits=[e2], sig=True)
            e4 = kb.op("act", lambda e: e.activation(
                out=st4[:, 8:12], in_=st4[:, 8:12], func=AF.Sqrt, bias=eps_ln[:]),
                waits=[e3], sig=True)
            e5 = kb.op("dve", lambda e: e.reciprocal(st4[:, 12:16], st4[:, 8:12]),
                       waits=[e4], sig=True)
            ev_xn_tm = []
            for tc in range(NCH):
                ev = kb.op("dve", lambda e, tc=tc, xo=xoff: e.tensor_scalar(
                    out=x_tm[:, xo + tc * D:xo + (tc + 1) * D],
                    in0=x_tm[:, xo + tc * D:xo + (tc + 1) * D],
                    scalar1=st4[:, tc:tc + 1], scalar2=st4[:, 12 + tc:13 + tc],
                    op0=ALU.subtract, op1=ALU.mult), waits=[e5], sig=True)
                ev_xn_tm.append(ev)
            if ti == 0:
                tap("xn_tm", lambda: x_tm[:, 0:NCH * D], [128, NCH * D], BF16,
                    [ev_xn_tm[-1]])
            if stop_after == "lnstats":
                break
            mm_group(pb[3], [
                lambda t: t.matmul(pb[3].ap[0:4, 0:128], st4[:, 0:4], ident_f[:],
                                   is_transpose=True, start=True, stop=False),
                lambda t: t.matmul(pb[3].ap[0:4, 128:256], st4[:, 8:12], ident_f[:],
                                   is_transpose=True, start=False, stop=True)],
                extra_waits=[e5])
            ev_mr = bank_read("dve", pb[3], lambda e: e.tensor_copy(
                mrow[:], pb[3].ap[0:4, 0:256]))
            ev_mbsb = None
            for bk, tcs in ((pb[3], (0, 1)), (pb[4], (2, 3))):
                mml = []
                for j, tc in enumerate(tcs):
                    mml.append(lambda t, tc=tc, j=j, b=bk: t.matmul(
                        b.ap[:, j * 256:(j + 1) * 256],
                        sel4[:, tc * 128:(tc + 1) * 128], mrow[:],
                        start=(j == 0), stop=(j == 1)))
                mm_group(bk, mml, extra_waits=[ev_mr])
                ev_mbsb = bank_read("dve", bk, lambda e, b=bk, t0=tcs[0]: e.tensor_copy(
                    mbsb[:, t0 * 256:t0 * 256 + 512], b.ap[:, 0:512]))

            ev_xn = None
            for tc in range(NCH):
                for g0 in range(0, NKC, 8):
                    mml = []
                    for j in range(8):
                        fc = g0 + j
                        mml.append(lambda t, fc=fc, j=j, tc=tc, xo=xoff: t.matmul(
                            pt.ap[:, j * 128:(j + 1) * 128],
                            x_tm[:, xo + tc * D + fc * 128:xo + tc * D + fc * 128 + 128],
                            ident_b[:], is_transpose=True,
                            start=(j == 0), stop=(j == 7)))
                    mm_group(pt, mml, extra_waits=[ev_xn_tm[tc]])
                    an = xn_fm.ap()
                    dst = bass.AP(an.tensor, an.offset + g0 * T + tc * 128,
                                  [an.ap[0], [T, 8], [1, 128]])
                    ev_xn = bank_read("dve", pt, lambda e, d=dst: e.tensor_copy(
                        d, pt.ap.rearrange("p (a b) -> p a b", a=8)))
            x_free[png] = ev_xn
            if ti == 0:
                tap("xn_fm", lambda: xn_fm[:, 0:NKC * T], [128, NKC * T], BF16, [ev_xn])
            if stop_after == "lnT":
                break

            bnk = pb[5]
            mml = []
            for kc in range(NKC):
                mml.append(lambda t, kc=kc, b=bnk: t.matmul(
                    b.ap[0:64, 0:T], wba[:, kc * 64:(kc + 1) * 64],
                    xn_fm[:, kc * T:(kc + 1) * T],
                    start=(kc == 0), stop=(kc == NKC - 1)))
            mm_group(bnk, mml, extra_waits=[ev_xn, ev_wba])
            evb = bank_read("act", bnk, lambda e, b=bnk: e.activation(
                out=ba_sb[0:32, 0:T], in_=b.ap[0:32, 0:T], func=AF.Sigmoid))
            eva1 = kb.op("act", lambda e, b=bnk: e.activation(
                out=ba_sb[32:64, T:2 * T], in_=b.ap[32:64, 0:T], func=AF.Abs,
                bias=dtb_c[32:64, :]), waits=[bnk.wev], sig=True)
            eva2 = kb.op("act", lambda e: e.activation(
                out=ba_sb[32:64, T:2 * T], in_=ba_sb[32:64, T:2 * T], func=AF.Exp,
                scale=-1.0), waits=[eva1], sig=True)
            eva3 = kb.op("act", lambda e: e.activation(
                out=ba_sb[32:64, T:2 * T], in_=ba_sb[32:64, T:2 * T], func=AF.Ln,
                bias=one_c[32:64, :]), waits=[eva2], sig=True)
            eva4 = kb.op("dve", lambda e, b=bnk: e.tensor_scalar(
                out=ba_sb[32:64, 0:T], in0=b.ap[32:64, 0:T],
                scalar1=dtb_c[32:64, :], scalar2=0.0, op0=ALU.add, op1=ALU.max),
                waits=[bnk.wev], sig=True)
            bnk.rev = eva4
            eva5 = kb.op("dve", lambda e: e.tensor_tensor(
                out=ba_sb[32:64, 0:T], in0=ba_sb[32:64, 0:T],
                in1=ba_sb[32:64, T:2 * T], op=ALU.add), waits=[eva3, eva4], sig=True)
            eva6 = kb.op("dve", lambda e: e.tensor_scalar_mul(
                ba_sb[32:64, 0:T], ba_sb[32:64, 0:T], nega_c[32:64, :]),
                waits=[eva5, ev_nega], sig=True)
            if ti == 0:
                tap("ba_g", lambda: ba_sb[0:64, 0:T], [64, T], F32, [eva6, evb])
            if stop_after == "ba1":
                break
            ev_gamma = None
            for cc in range(NCH):
                ev_gamma = kb.op("dve", lambda e, cc=cc: e.tensor_tensor_scan(
                    out=ba_sb[32:64, 2 * T + cc * C:2 * T + (cc + 1) * C],
                    data0=ba_sb[32:64, cc * C:(cc + 1) * C],
                    data1=ba_sb[32:64, cc * C:(cc + 1) * C],
                    initial=0.0, op0=ALU.add, op1=ALU.bypass),
                    waits=[eva6], sig=True)
            if ti == 0:
                tap("ba_gam", lambda: ba_sb[0:64, 2 * T:3 * T], [64, T], F32,
                    [ev_gamma])
            if stop_after == "ba2":
                break
            mml = []
            for cc in range(NCH):
                mml.append(lambda t, cc=cc: t.matmul(
                    pb[5].ap[:, cc * 32:(cc + 1) * 32],
                    ba_sb[32:64, 2 * T + cc * C:2 * T + (cc + 1) * C],
                    ident_f[32:64, 32:64], is_transpose=True,
                    start=(cc == 0), stop=(cc == NCH - 1)))
            mm_group(pb[5], mml, extra_waits=[evb, ev_gamma])
            ev_gb1 = bank_read("dve", pb[5], lambda e: e.tensor_copy(
                bass.AP(gb_tmh.ap().tensor, gb_tmh.ap().offset,
                        [gb_tmh.ap().ap[0], [64, NCH], [1, 32]]),
                pb[5].ap[:, 0:NCH * 32].rearrange("p (a b) -> p a b", a=NCH)))
            mml = []
            for cc in range(NCH):
                mml.append(lambda t, cc=cc: t.matmul(
                    pb[6].ap[:, cc * 32:(cc + 1) * 32],
                    ba_sb[0:32, cc * C:(cc + 1) * C],
                    ident_f[0:32, 0:32], is_transpose=True,
                    start=(cc == 0), stop=(cc == NCH - 1)))
            mm_group(pb[6], mml, extra_waits=[evb, ev_gamma])
            ev_gb2 = bank_read("dve", pb[6], lambda e: e.tensor_copy(
                bass.AP(gb_tmh.ap().tensor, gb_tmh.ap().offset + 32,
                        [gb_tmh.ap().ap[0], [64, NCH], [1, 32]]),
                pb[6].ap[:, 0:NCH * 32].rearrange("p (a b) -> p a b", a=NCH)))
            ev_gb = ev_gb2
            if ti == 0:
                tap("gb_early", lambda: gb_tmh[:, 0:NCH * 64], [128, NCH * 64], F32,
                    [ev_gb])
            if stop_after == "ba3":
                break
            ae, ag = exps_t.ap(), gb_tmh.ap()
            ev_exps = kb.op("act", lambda e: e.activation(
                out=bass.AP(ae.tensor, ae.offset, [ae.ap[0], [32, NCH], [1, 32]]),
                in_=bass.AP(ag.tensor, ag.offset, [ag.ap[0], [64, NCH], [1, 32]]),
                func=AF.Exp), waits=[ev_gb], sig=True)
            ab = bl_t.ap()
            ev_bl = kb.op("dve", lambda e: e.tensor_tensor(
                out=bass.AP(ab.tensor, ab.offset, [ab.ap[0], [32, NCH], [1, 32]]),
                in0=bass.AP(ae.tensor, ae.offset, [ae.ap[0], [32, NCH], [1, 32]]),
                in1=bass.AP(ag.tensor, ag.offset + 32, [ag.ap[0], [64, NCH], [1, 32]]),
                op=ALU.mult), waits=[ev_exps], sig=True)
            if ti == 0:
                tap("gb_tmh", lambda: gb_tmh[:, 0:NCH * 64], [128, NCH * 64], F32,
                    [ev_gb])
            if stop_after in ("ln", "ln1"):
                break

            carry = {"silu_pe": kb.now("pe"),
                     "bcast_pe": kb.now("pe"),
                     "qk_dve": kb.now("dve"),
                     "cvo_act": kb.now("act"),
                     "vg_pe": kb.now("pe"),
                     "qkg_pe": kb.now("pe"),
                     "zg_dve": kb.now("dve"),
                     "sbf_pe": kb.now("pe"),
                     "gated_pe": kb.now("pe")}

            for g in range(NGRP):
                o_qk, ev_qk, ent_qk = ring_push(f"qqkk{g}", SEG_QQKK)
                o_v, ev_v, ent_v = ring_push(f"vvvv{g}", SEG_QQKK)

                def proj_mm_conv(ring_off, ring_sub, qn_off, sq_off, bnk_i,
                                 ready_ev, conv_ch, kind, dst_off=None):
                    bnkp = pb[bnk_i]
                    mml = []
                    for kc in range(NKC):
                        mml.append(lambda t, kc=kc, b=bnkp, ro=ring_off,
                                   rs=ring_sub: t.matmul(
                            b.ap[:, 0:T],
                            ring[:, ro + kc * 512 + rs:ro + kc * 512 + rs + 128],
                            xn_fm[:, kc * T:(kc + 1) * T],
                            start=(kc == 0), stop=(kc == NKC - 1)))
                    mm_group(bnkp, mml, extra_waits=[ready_ev, ev_xn])
                    hco = conv_ch * 3
                    kb.op("dve", lambda e, hco=hco: e.tensor_copy(
                        wb(WB["mxp"], 3), conv_hist[:, hco:hco + 3]),
                        waits=[ev_consts, carry["cvo_act"], carry["sbf_pe"],
                               tap_ev[0]])
                    bank_read("dve", bnkp, lambda e, b=bnkp: e.tensor_copy(
                        wb(WB["mxp"] + 3, T), b.ap[:, 0:T]))
                    kb.op("dve", lambda e, hco=hco: e.tensor_copy(
                        conv_hist[:, hco:hco + 3], wb(WB["mxp"] + T, 3)))
                    if conv_ch == 0:
                        tap("mxp0", lambda: wb(WB["mxp"], T + 3), [128, T + 3],
                            BF16, [kb.now("dve")])
                    kb.op("dve", lambda e, cc_=conv_ch: e.tensor_scalar(
                        out=wb(WB["cvo"], T), in0=wb(WB["mxp"], T),
                        scalar1=conv_w[:, cc_ * 4:cc_ * 4 + 1], scalar2=None,
                        op0=ALU.mult), waits=[ev_cw])
                    for j in (1, 2):
                        kb.op("dve", lambda e, j=j, cc_=conv_ch: e.scalar_tensor_tensor(
                            out=wb(WB["cvo"], T), in0=wb(WB["mxp"] + j, T),
                            scalar=conv_w[:, cc_ * 4 + j:cc_ * 4 + j + 1],
                            in1=wb(WB["cvo"], T), op0=ALU.mult, op1=ALU.add))
                    ev_cv = kb.op("dve", lambda e, cc_=conv_ch: e.scalar_tensor_tensor(
                        out=wb(WB["cvo"], T), in0=wb(WB["mxp"] + 3, T),
                        scalar=conv_w[:, cc_ * 4 + 3:cc_ * 4 + 4],
                        in1=wb(WB["cvo"], T), op0=ALU.mult, op1=ALU.add), sig=True)
                    if kind == "v":
                        ev = kb.op("act", lambda e, d=dst_off: e.activation(
                            out=wb(d, T), in_=wb(WB["cvo"], T), func=AF.Silu),
                            waits=[ev_cv, carry["vg_pe"]], sig=True)
                        carry["cvo_act"] = ev
                        return ev
                    ev_s = kb.op("act", lambda e, q=qn_off: e.activation(
                        out=wb(q, T), in_=wb(WB["cvo"], T), func=AF.Silu),
                        waits=[ev_cv, carry["qk_dve"], carry["silu_pe"],
                               carry["sbf_pe"]], sig=True)
                    carry["cvo_act"] = ev_s
                    ev_sq = kb.op("act", lambda e, q=qn_off, s=sq_off: e.activation(
                        out=wb(s, T), in_=wb(q, T), func=AF.Square),
                        waits=[ev_s], sig=True)
                    return ev_sq

                def proj_l2_finish(qn_off, sq_off, dst_off, kind, ev_sq):
                    b2 = pb[6]
                    mm_group(b2, [lambda t, b=b2, s=sq_off: t.matmul(
                        b.ap[0:1, 0:T], ones_cb[:], wb(s, T),
                        start=True, stop=True)], extra_waits=[ev_sq])
                    carry["silu_pe"] = b2.wev
                    ev_ln1 = bank_read("act", b2, lambda e, b=b2: e.activation(
                        out=wf(WF["u1"], T)[0:1, :], in_=b.ap[0:1, 0:T],
                        func=AF.Ln, bias=eps_l2[0:1, :]))
                    ev_ln2 = kb.op("act", lambda e, k=kind: e.activation(
                        out=wb(WB["z1"], T)[0:1, :], in_=wf(WF["u1"], T)[0:1, :],
                        func=AF.Exp, scale=-0.5,
                        bias=(qsc_b[0:1, :] if k == "q" else zero_c[0:1, :])),
                        waits=[ev_ln1, carry["bcast_pe"]], sig=True)
                    mm_group(b2, [lambda t, b=b2: t.matmul(
                        b.ap[:, 0:T], ones_rb[:], wb(WB["z1"], T)[0:1, :],
                        start=True, stop=True)], extra_waits=[ev_ln2])
                    carry["bcast_pe"] = b2.wev
                    ev = bank_read("dve", b2, lambda e, b=b2, q=qn_off, d=dst_off:
                                   e.tensor_tensor(
                                       out=wb(d, T), in0=wb(q, T), in1=b.ap[:, 0:T],
                                       op=ALU.mult), extra_waits=[carry["qkg_pe"]])
                    carry["qk_dve"] = ev
                    return ev

                ev_qg, ev_kg, ev_vg = [None, None], [None, None], [None] * 4
                qk_phase = []
                for i in range(2):
                    ev_sq = proj_mm_conv(o_qk, i * 128, WB["di"] + i * T,
                                         WB["wt"] + i * T, i % 2, ev_qk,
                                         2 * g + i, "q")
                    qk_phase.append(("q", i, ev_sq))
                for i in range(2):
                    ev_sq = proj_mm_conv(o_qk, 256 + i * 128, WB["di"] + (2 + i) * T,
                                         WB["wt"] + (2 + i) * T, i % 2, ev_qk,
                                         16 + 2 * g + i, "k")
                    qk_phase.append(("k", i, ev_sq))
                ring_release(ent_qk, kb.now("pe"))
                o_z, ev_z, ent_z = ring_push(f"zzzz{g}", SEG_QQKK)
                for kind, i, ev_sq in qk_phase:
                    j = i if kind == "q" else 2 + i
                    dst = (WB["qg"] if kind == "q" else WB["kg"]) + i * T
                    ev = proj_l2_finish(WB["di"] + j * T, WB["wt"] + j * T,
                                        dst, kind, ev_sq)
                    if kind == "q":
                        ev_qg[i] = ev
                    else:
                        ev_kg[i] = ev
                for i in range(4):
                    ev_vg[i] = proj_mm_conv(o_v, i * 128, 0, 0, i % 2, ev_v,
                                            32 + 4 * g + i, "v",
                                            dst_off=WB["vg"] + i * T)
                ring_release(ent_v, kb.now("pe"))
                o_wo, ev_wo, ent_wo = ring_push(f"wout{g}", SEG_WOUT)

                if ti == 0 and g == 0:
                    evq = kb.now("dve")
                    tap("qg0", lambda: wb(WB["qg"], 2 * T), [128, 2 * T], BF16, [evq])
                    tap("kg0", lambda: wb(WB["kg"], 2 * T), [128, 2 * T], BF16, [evq])
                    eva = ev_vg[3]
                    tap("vg0", lambda: wb(WB["vg"], 4 * T), [128, 4 * T], BF16, [eva])

                ev_zg = None
                for cc in range(NCH):
                    bnkz = pb[2 + 2 * (cc % 2)]
                    mml = []
                    for kc in range(NKC):
                        mml.append(lambda t, kc=kc, cc=cc, b=bnkz, oz=o_z: t.matmul(
                            b.ap[:, 0:512],
                            xn_fm[:, kc * T + cc * C:kc * T + (cc + 1) * C],
                            ring[:, oz + kc * 512:oz + (kc + 1) * 512],
                            start=(kc == 0), stop=(kc == NKC - 1)))
                    mm_group(bnkz, mml, extra_waits=[ev_z, ev_xn])
                    ev_zg = bank_read("act", bnkz, lambda e, cc=cc, b=bnkz:
                                      e.activation(
                                          out=wb(WB["zg"] + cc * 512, 512),
                                          in_=b.ap[:, 0:512], func=AF.Silu),
                                      extra_waits=[carry["zg_dve"], tap_ev[0]])
                ring_release(ent_z, kb.now("pe"))

                ev_kkkq = None
                for kl in range(2):
                    for c0 in range(0, NCH, 2):
                        bnkk = pb[2 + 2 * ((c0 // 2) % 2)]
                        mml = []
                        for j, cc in enumerate((c0, c0 + 1)):
                            mml.append(lambda t, cc=cc, kl=kl, j=j, b=bnkk: t.matmul(
                                b.ap[:, j * 256:j * 256 + 128],
                                wb(WB["kg"] + kl * T + cc * C, C),
                                wb(WB["kg"] + kl * T + cc * C, C),
                                start=(j == 0), stop=False))
                            mml.append(lambda t, cc=cc, kl=kl, j=j, b=bnkk: t.matmul(
                                b.ap[:, j * 256 + 128:j * 256 + 256],
                                wb(WB["kg"] + kl * T + cc * C, C),
                                wb(WB["qg"] + kl * T + cc * C, C),
                                start=False, stop=(j == 1)))
                        mm_group(bnkk, mml, extra_waits=[ev_kg[kl], ev_qg[kl]])
                        ev_kkkq = bank_read("dve", bnkk,
                                            lambda e, kl=kl, c0=c0, b=bnkk:
                                            e.tensor_copy(
                                                wb(WB["kkkq"] + kl * NCH * 256 + c0 * 256, 512),
                                                b.ap[:, 0:512]),
                                            extra_waits=[carry["gated_pe"],
                                                         tap_ev[0]])
                carry["qkg_pe"] = kb.now("pe")
                ev_vt = None
                for b0 in range(0, 16, 8):
                    mml = []
                    for j in range(8):
                        hh, cc = divmod(b0 + j, NCH)
                        mml.append(lambda t, hh=hh, cc=cc, j=j: t.matmul(
                            pt.ap[:, j * 128:(j + 1) * 128],
                            wb(WB["vg"] + hh * T + cc * C, C),
                            ident_b[:], is_transpose=True,
                            start=(j == 0), stop=(j == 7)))
                    mm_group(pt, mml, extra_waits=ev_vg)
                    ev_vt = bank_read("dve", pt, lambda e, b0=b0: e.tensor_copy(
                        wb(WB["vt"] + b0 * 128, 1024), pt.ap[:, 0:1024]))
                mml = []
                for j in range(8):
                    kl, cc = divmod(j, NCH)
                    mml.append(lambda t, kl=kl, cc=cc, j=j: t.matmul(
                        pt.ap[:, j * 128:(j + 1) * 128],
                        wb(WB["kg"] + kl * T + cc * C, C),
                        ident_b[:], is_transpose=True, start=(j == 0), stop=(j == 7)))
                mm_group(pt, mml, extra_waits=[ev_kg[0], ev_kg[1]])
                ev_kt = bank_read("dve", pt, lambda e: e.tensor_copy(
                    wb(WB["kt"], 1024), pt.ap[:, 0:1024]))
                carry["vg_pe"] = kb.now("pe")

                ev_lcc = None
                G_SLOT = (WB["g_nw"], WB["dl"])
                Z_SLOT = (WB["z1"], WB["dl2"])
                XT_A = (WB["xt0"], WB["u_sb"])
                XT_B = (WB["at"], WB["gtd"])
                XB = (pb[2], pb[3])
                PBANK = (pb[4], pb[5])
                XTBANK = (pb[6], pb[3])
                ev_zslot = [None] * 4
                ev_slot_pe = None

                def p1_prep(hh):
                    p = hh % 2
                    h32 = (4 * g + hh) % 32
                    kl = hh // 2
                    bnkx = XB[p]
                    mml = []
                    for cc in range(NCH):
                        view = bass.AP(gb_tmh.ap().tensor,
                                       gb_tmh.ap().offset + cc * 64 + h32,
                                       [gb_tmh.ap().ap[0], [0, C]])
                        mml.append(lambda t, v=view, cc=cc, b=bnkx: t.matmul(
                            b.ap[:, cc * C:(cc + 1) * C], v, ident_f[:],
                            is_transpose=True, start=(cc == 0), stop=(cc == NCH - 1)))
                    mm_group(bnkx, mml, extra_waits=[ev_gb])
                    gview = bcast3(gb_tmh, h32, NCH, 64, C)
                    ev_e1 = bank_read("dve", bnkx, lambda e, gv=gview, b=bnkx:
                                      e.tensor_tensor(
                                          out=wf3(WF["e1"], NCH, C, C),
                                          in0=b.ap.rearrange("p (a b) -> p a b", a=NCH),
                                          in1=gv, op=ALU.subtract))
                    mi_v = bass.AP(mask_i.ap().tensor, mask_i.ap().offset,
                                   [mask_i.ap().ap[0], [0, NCH], [1, C]])
                    ev_e2 = kb.op("dve", lambda e, mv=mi_v: e.tensor_tensor(
                        out=wf3(WF["e2"], NCH, C, C), in0=wf3(WF["e1"], NCH, C, C),
                        in1=mv, op=ALU.add), waits=[ev_e1], sig=True)
                    ev_di = kb.op("act", lambda e, hh=hh: e.activation(
                        out=wb(WB["di"] + hh * NCH * C, NCH * C),
                        in_=wf(WF["e2"], NCH * C), func=AF.Exp),
                        waits=[ev_e2, tap_ev[0]], sig=True)
                    ms_v = bass.AP(mask_s.ap().tensor, mask_s.ap().offset,
                                   [mask_s.ap().ap[0], [0, NCH], [1, C]])
                    ev_e3 = kb.op("dve", lambda e, mv=ms_v: e.tensor_tensor(
                        out=wf3(WF["e2"], NCH, C, C), in0=wf3(WF["e1"], NCH, C, C),
                        in1=mv, op=ALU.add), waits=[ev_di], sig=True)
                    gso = G_SLOT[p]
                    ev_ds = kb.op("act", lambda e, go=gso: e.activation(
                        out=wb(go, NCH * C), in_=wf(WF["e2"], NCH * C),
                        func=AF.Exp, scale=-1.0),
                        waits=[ev_e3, ev_slot_pe], sig=True)
                    kk_v = wb3(WB["kkkq"] + kl * NCH * 256, NCH, 256, C)
                    zso = Z_SLOT[p]
                    ev_a1 = kb.op("dve", lambda e, kv=kk_v, go=gso, zo=zso:
                                  e.tensor_tensor(
                                      out=wb3(zo, NCH, C, C), in0=wb3(go, NCH, C, C),
                                      in1=kv, op=ALU.mult),
                                  waits=[ev_ds, ev_kkkq, ev_slot_pe], sig=True)
                    b_v = bcast3(gb_tmh, 32 + h32, NCH, 64, C)
                    ev_a = kb.op("dve", lambda e, bv=b_v, zo=zso: e.tensor_tensor(
                        out=wb3(zo, NCH, C, C), in0=wb3(zo, NCH, C, C),
                        in1=bv, op=ALU.mult), waits=[ev_a1], sig=True)
                    kq_v = wb3(WB["kkkq"] + kl * NCH * 256 + 128, NCH, 256, C)
                    kb.op("dve", lambda e, kv=kq_v, hh=hh: e.tensor_tensor(
                        out=wb3(WB["wt"] + hh * NCH * C, NCH, C, C),
                        in0=wb3(WB["di"] + hh * NCH * C, NCH, C, C),
                        in1=kv, op=ALU.mult), waits=[ev_di])
                    i_v = bass.AP(ident_b.ap().tensor, ident_b.ap().offset,
                                  [ident_b.ap().ap[0], [0, NCH], [1, C]])
                    ev_g = kb.op("dve", lambda e, iv=i_v, go=gso, zo=zso:
                                 e.tensor_tensor(
                                     out=wb3(go, NCH, C, C), in0=wb3(zo, NCH, C, C),
                                     in1=iv, op=ALU.add), waits=[ev_a], sig=True)
                    ev_z1n = kb.op("dve", lambda e, iv=i_v, zo=zso:
                                   e.scalar_tensor_tensor(
                                       out=wb3(zo, NCH, C, C),
                                       in0=wb3(zo, NCH, C, C),
                                       scalar=-1.0, in1=iv, op0=ALU.mult,
                                       op1=ALU.add), waits=[ev_g], sig=True)
                    ev_zslot[hh] = ev_z1n
                    ed_v = bass.AP(work_bf.ap().tensor,
                                   work_bf.ap().offset + WB["di"] + hh * NCH * C + 127,
                                   [work_bf.ap().ap[0], [C, NCH]])
                    ex_v = bass.AP(exps_t.ap().tensor, exps_t.ap().offset + h32,
                                   [exps_t.ap().ap[0], [32, NCH]])
                    nonlocal_lcc = kb.op("dve", lambda e, hh=hh, ev2=ed_v, xv=ex_v:
                                   e.tensor_tensor(
                                       out=bass.AP(work_f32.ap().tensor,
                                                   work_f32.ap().offset + WF["lcc"] + hh * NCH,
                                                   [work_f32.ap().ap[0], [1, NCH]]),
                                       in0=ev2, in1=xv, op=ALU.mult),
                                   waits=[ev_di, ev_exps, ev_bl], sig=True)
                    return nonlocal_lcc

                for hp in (0, 2):
                    for hh in (hp, hp + 1):
                        ev_lcc = p1_prep(hh)
                    xt_cur = [None, None]
                    ev_xt = [None, None]
                    for hh in (hp, hp + 1):
                        p = hh % 2
                        mml = []
                        for cc in range(NCH):
                            mml.append(lambda t, cc=cc, zo=Z_SLOT[p]: t.matmul(
                                pt.ap[:, cc * C:(cc + 1) * C], wb(zo + cc * C, C),
                                ident_b[:], is_transpose=True,
                                start=(cc == 0), stop=(cc == NCH - 1)))
                        mm_group(pt, mml, extra_waits=[ev_zslot[hh]])
                        ev_xt[p] = bank_read("dve", pt,
                                             lambda e, d=XT_A[p]: e.tensor_copy(
                                                 wb(d, NCH * C), pt.ap[:, 0:NCH * C]),
                                             extra_waits=[ev_slot_pe])
                        xt_cur[p] = XT_A[p]
                    for step in range(2):
                        pevs = [None, None]
                        for hh in (hp, hp + 1):
                            p = hh % 2
                            bpn = PBANK[p]
                            mml = []
                            for cc in range(NCH):
                                mml.append(lambda t, cc=cc, b=bpn, c=xt_cur[p],
                                           go=G_SLOT[p]: t.matmul(
                                    b.ap[:, cc * C:(cc + 1) * C], wb(c + cc * C, C),
                                    wb(go + cc * C, C),
                                    start=(cc == 0), stop=(cc == NCH - 1)))
                            mm_group(bpn, mml, extra_waits=[ev_xt[p]])
                            pevs[p] = bpn
                        for hh in (hp, hp + 1):
                            p = hh % 2
                            bpn = pevs[p]
                            t_v = bass.AP(two_i.ap().tensor, two_i.ap().offset,
                                          [two_i.ap().ap[0], [0, NCH], [1, C]])
                            ev_zz = bank_read("dve", bpn,
                                              lambda e, b=bpn, tv=t_v, zo=Z_SLOT[p]:
                                              e.scalar_tensor_tensor(
                                                  out=wb3(zo, NCH, C, C),
                                                  in0=b.ap.rearrange(
                                                      "p (a b) -> p a b", a=NCH),
                                                  scalar=-1.0, in1=tv,
                                                  op0=ALU.mult, op1=ALU.add))
                            bq = XTBANK[p]
                            mml = []
                            for cc in range(NCH):
                                mml.append(lambda t, cc=cc, b=bq, c=xt_cur[p],
                                           zo=Z_SLOT[p]: t.matmul(
                                    b.ap[:, cc * C:(cc + 1) * C],
                                    wb(zo + cc * C, C), wb(c + cc * C, C),
                                    start=(cc == 0), stop=(cc == NCH - 1)))
                            mm_group(bq, mml, extra_waits=[ev_zz])
                            if step < 1:
                                dsto = XT_B[p] if xt_cur[p] == XT_A[p] else XT_A[p]
                            else:
                                dsto = WB["tt"] + hh * NCH * C
                            ev_xt[p] = bank_read("dve", bq,
                                                 lambda e, b=bq, d=dsto: e.tensor_copy(
                                                     wb(d, NCH * C),
                                                     b.ap[:, 0:NCH * C]),
                                                 extra_waits=[tap_ev[0]])
                            xt_cur[p] = dsto
                    ev_slot_pe = kb.now("pe")
                ev_p1 = kb.now("dve")
                if ti == 0 and g == 0:
                    tap("tt0", lambda: wb(WB["tt"], 4 * NCH * C), [128, 4 * NCH * C],
                        BF16, [ev_p1])
                    tap("di0", lambda: wb(WB["di"], 4 * NCH * C), [128, 4 * NCH * C],
                        BF16, [ev_p1])

                prev_ks = prev_qs = None
                for cc in range(NCH):
                    ev_sb = kb.op("dve", lambda e, g=g: e.tensor_copy(
                        wb(WB["sbf"], 512), s_f32[:, g * 512:(g + 1) * 512]),
                        waits=[w for w in (prev_ks, prev_qs, carry["sbf_pe"]) if w],
                        sig=True)
                    bKS, bQS = pb[3], pb[4]
                    mml = []
                    for p in range(2):
                        mml.append(lambda t, p=p, cc=cc, b=bKS: t.matmul(
                            b.ap[:, p * 256:(p + 1) * 256],
                            wb(WB["kg"] + p * T + cc * C, C),
                            wb(WB["sbf"] + p * 256, 256),
                            start=(p == 0), stop=(p == 1)))
                    mm_group(bKS, mml, extra_waits=[ev_sb])
                    prev_ks = bKS.wev
                    mml = []
                    for p in range(2):
                        mml.append(lambda t, p=p, cc=cc, b=bQS: t.matmul(
                            b.ap[:, p * 256:(p + 1) * 256],
                            wb(WB["qg"] + p * T + cc * C, C),
                            wb(WB["sbf"] + p * 256, 256),
                            start=(p == 0), stop=(p == 1)))
                    mm_group(bQS, mml, extra_waits=[ev_sb])
                    prev_qs = bQS.wev
                    bl_v = bcast3(bl_t, cc * 32 + 4 * g - 32 * (g // 8), 4, 1, C)
                    bl_v = bcast3(bl_t, cc * 32 + (4 * g) % 32, 4, 1, C)
                    ev_u1 = bank_read("dve", bKS, lambda e, bv=bl_v, b=bKS:
                                      e.tensor_tensor(
                                          out=wf3(WF["u1"], 4, C, C),
                                          in0=b.ap.rearrange("p (a b) -> p a b", a=4),
                                          in1=bv, op=ALU.mult),
                                      extra_waits=[ev_bl])
                    vt_v = wb3(WB["vt"] + cc * C, 4, NCH * C, C)
                    be_v = bcast3(gb_tmh, cc * 64 + 32 + (4 * g) % 32, 4, 1, C)
                    ev_u2 = kb.op("dve", lambda e, vv=vt_v, bv=be_v: e.tensor_tensor(
                        out=wf3(WF["g1"], 4, C, C), in0=vv, in1=bv, op=ALU.mult),
                        waits=[ev_vt, ev_gb], sig=True)
                    ev_u = kb.op("dve", lambda e: e.tensor_tensor(
                        out=wb(WB["u_sb"], 512), in0=wf(WF["g1"], 512),
                        in1=wf(WF["u1"], 512), op=ALU.subtract),
                        waits=[ev_u1, ev_u2], sig=True)
                    bD = pb[5]
                    mml = []
                    for hh in range(4):
                        mml.append(lambda t, hh=hh, cc=cc, b=bD: t.matmul(
                            b.ap[:, hh * 128:(hh + 1) * 128],
                            wb(WB["tt"] + hh * NCH * C + cc * C, C),
                            wb(WB["u_sb"] + hh * 128, 128),
                            start=(hh == 0), stop=(hh == 3)))
                    mm_group(bD, mml, extra_waits=[ev_u, ev_p1])
                    ev_dl = bank_read("dve", bD, lambda e, b=bD: e.tensor_copy(
                        wb(WB["dl"], 512), b.ap[:, 0:512]))
                    ed_v = bass.AP(work_bf.ap().tensor,
                                   work_bf.ap().offset + WB["di"] + cc * C + 127,
                                   [work_bf.ap().ap[0], [NCH * C, 4], [0, C]])
                    ev_dl2 = kb.op("dve", lambda e, ev2=ed_v, b=bD: e.tensor_tensor(
                        out=wb(WB["dl2"], 512),
                        in0=b.ap.rearrange("p (a b) -> p a b", a=4),
                        in1=ev2, op=ALU.mult), waits=[ev_dl], sig=True)
                    bD.rev = ev_dl2
                    bO = pb[3]
                    mml = []
                    for hh in range(4):
                        mml.append(lambda t, hh=hh, cc=cc, b=bO: t.matmul(
                            b.ap[:, hh * 128:(hh + 1) * 128],
                            wb(WB["wt"] + hh * NCH * C + cc * C, C),
                            wb(WB["dl"] + hh * 128, 128),
                            start=(hh == 0), stop=(hh == 3)))
                    mm_group(bO, mml, extra_waits=[ev_dl])
                    prev_ks = bO.wev
                    eg_v = bcast3(exps_t, cc * 32 + (4 * g) % 32, 4, 1, C)
                    ev_g1 = bank_read("dve", bQS, lambda e, gv=eg_v, b=bQS:
                                      e.tensor_tensor(
                                          out=wf3(WF["g1"], 4, C, C),
                                          in0=b.ap.rearrange("p (a b) -> p a b", a=4),
                                          in1=gv, op=ALU.mult))
                    ev_o = bank_read("dve", bO, lambda e, b=bO: e.tensor_tensor(
                        out=wf(WF["osb"], 512), in0=b.ap[:, 0:512],
                        in1=wf(WF["g1"], 512), op=ALU.add), extra_waits=[ev_g1])
                    ev_ms = None
                    for hh in range(4):
                        ev_ms = kb.op("act", lambda e, hh=hh: e.activation(
                            out=wf(WF["u1"] + hh * C, C),
                            in_=wf(WF["osb"] + hh * C, C),
                            func=AF.Square,
                            accum_out=wf(WF["ms"] + hh, 1)),
                            waits=[ev_o] if hh == 0 else (), sig=(hh == 3))
                    ev_r1 = kb.op("act", lambda e: e.activation(
                        out=wf(WF["rstd"], 4), in_=wf(WF["ms"], 4), func=AF.Ln,
                        scale=1.0 / DV, bias=eps_rms[:]), waits=[ev_ms], sig=True)
                    ev_r2 = kb.op("act", lambda e: e.activation(
                        out=wf(WF["rstd"], 4), in_=wf(WF["rstd"], 4), func=AF.Exp,
                        scale=-0.5), waits=[ev_r1], sig=True)
                    rs_v = bcast3(work_f32, WF["rstd"], 4, 1, C)
                    ev_gt1 = kb.op("dve", lambda e, rv=rs_v: e.tensor_tensor(
                        out=wf3(WF["g1"], 4, C, C), in0=wf3(WF["osb"], 4, C, C),
                        in1=rv, op=ALU.mult), waits=[ev_r2], sig=True)
                    ev_gt = kb.op("dve", lambda e, cc=cc: e.tensor_tensor(
                        out=wb(WB["gtd"], 512), in0=wf(WF["g1"], 512),
                        in1=wb(WB["zg"] + cc * 512, 512), op=ALU.mult),
                        waits=[ev_gt1, ev_zg], sig=True)
                    carry["zg_dve"] = ev_gt
                    mml = []
                    for hh in range(4):
                        mml.append(lambda t, hh=hh: t.matmul(
                            pt.ap[:, hh * 128:(hh + 1) * 128],
                            wb(WB["gtd"] + hh * 128, 128), ident_b[:],
                            is_transpose=True, start=(hh == 0), stop=(hh == 3)))
                    mm_group(pt, mml, extra_waits=[ev_gt])
                    aw = work_bf.ap()
                    dstg = bass.AP(aw.tensor, aw.offset + WB["gated"] + cc * C,
                                   [aw.ap[0], [T, 4], [1, C]])
                    bank_read("dve", pt, lambda e, d=dstg: e.tensor_copy(
                        d, pt.ap[:, 0:512].rearrange("p (a b) -> p a b", a=4)),
                        extra_waits=[carry["gated_pe"], tap_ev[0]])
                    bS = pb[4]
                    mml = []
                    for p in range(2):
                        mml.append(lambda t, p=p, cc=cc, b=bS: t.matmul(
                            b.ap[:, p * 256:(p + 1) * 256],
                            wb(WB["kt"] + (p * NCH + cc) * C, C),
                            wb(WB["dl2"] + p * 256, 256),
                            start=(p == 0), stop=(p == 1)))
                    mm_group(bS, mml, extra_waits=[ev_dl2, ev_kt])
                    prev_qs = bS.wev
                    for hh in range(4):
                        lcc_ap = bass.AP(work_f32.ap().tensor,
                                         work_f32.ap().offset + WF["lcc"] + hh * NCH + cc,
                                         [work_f32.ap().ap[0], [1, 1]])
                        bank_read("dve", bS, lambda e, hh=hh, g=g, sc=lcc_ap:
                                  e.scalar_tensor_tensor(
                                      out=s_f32[:, g * 512 + hh * 128:g * 512 + (hh + 1) * 128],
                                      in0=s_f32[:, g * 512 + hh * 128:g * 512 + (hh + 1) * 128],
                                      scalar=sc,
                                      in1=bS.ap[:, hh * 128:(hh + 1) * 128],
                                      op0=ALU.mult, op1=ALU.add),
                                  extra_waits=[ev_lcc])
                carry["sbf_pe"] = kb.now("pe")
                if ti == 0 and g == 0:
                    ev_gg = kb.now("dve")
                    tap("gated0", lambda: wb(WB["gated"], 4 * T), [128, 4 * T], BF16,
                        [ev_gg])
                    tap("s0", lambda: s_f32[:, 0:512], [128, 512], F32, [ev_gg])

                ev_gated_all = kb.now("dve")
                for oc in range(NKC):
                    bnko = pb[oc % 2]
                    mml = []
                    for kc in range(4):
                        mml.append(lambda t, kc=kc, oc=oc, b=bnko, ow=o_wo: t.matmul(
                            b.ap[:, 0:T],
                            ring[:, ow + kc * 2048 + oc * 128:
                                 ow + kc * 2048 + oc * 128 + 128],
                            wb(WB["gated"] + kc * T, T),
                            start=(kc == 0), stop=(kc == 3)))
                    mm_group(bnko, mml, extra_waits=[ev_wo, ev_gated_all])
                    if g == 0:
                        s_v = bass.AP(mbsb.ap().tensor, mbsb.ap().offset + 128,
                                      [mbsb.ap().ap[0], [256, NCH], [1, 128]])
                        m_v = bass.AP(mbsb.ap().tensor, mbsb.ap().offset,
                                      [mbsb.ap().ap[0], [256, NCH], [1, 128]])
                        an = xn_fm.ap()
                        xn_v = bass.AP(an.tensor, an.offset + oc * T,
                                       [an.ap[0], [C, NCH], [1, C]])
                        kb.op("dve", lambda e, xv=xn_v, sv=s_v: e.tensor_tensor(
                            out=wf3(WF["u1"], NCH, C, C), in0=xv, in1=sv,
                            op=ALU.mult), waits=[ev_mbsb, ev_xn])
                        kb.op("dve", lambda e, mv=m_v: e.tensor_tensor(
                            out=wf3(WF["u1"], NCH, C, C), in0=wf3(WF["u1"], NCH, C, C),
                            in1=mv, op=ALU.add))
                        bank_read("dve", bnko, lambda e, oc=oc, b=bnko: e.tensor_tensor(
                            out=attn_bf[:, oc * T:(oc + 1) * T], in0=b.ap[:, 0:T],
                            in1=wf(WF["u1"], T), op=ALU.add))
                    else:
                        bank_read("dve", bnko, lambda e, oc=oc, b=bnko: e.tensor_tensor(
                            out=attn_bf[:, oc * T:(oc + 1) * T], in0=b.ap[:, 0:T],
                            in1=attn_bf[:, oc * T:(oc + 1) * T], op=ALU.add))
                carry["gated_pe"] = kb.now("pe")
                ring_release(ent_wo, kb.now("pe"))

            ev_attn = kb.now("dve")
            if ti == 0:
                tap("attn0", lambda: attn_bf[:, 0:NKC * T], [128, NKC * T], BF16,
                    [ev_attn])
            if stop_after == "attn":
                break

            ev_hid = None
            prev_hid_pe = None
            for hc in range(NHC):
                o1, e_q1, ent1 = ring_push(f"q1_{hc}", SEG_Q12)
                if ti == 0 and hc == 0:
                    tap("ring_q1", lambda o=o1: ring[:, o:o + SEG_Q12],
                        [128, SEG_Q12], BF16, [e_q1])
                o2, e_q2, ent2 = ring_push(f"q2_{hc}", SEG_Q12)
                b1, b2 = pb[0], pb[1]
                mml = []
                for kc in range(NKC):
                    mml.append(lambda t, kc=kc, o=o1, b=b1: t.matmul(
                        b.ap[:, 0:T], ring[:, o + kc * 128:o + (kc + 1) * 128],
                        xn_fm[:, kc * T:(kc + 1) * T],
                        start=(kc == 0), stop=(kc == NKC - 1)))
                mm_group(b1, mml, extra_waits=[e_q1, ev_xn, ev_attn if hc == 0 else None])
                mml = []
                for kc in range(NKC):
                    mml.append(lambda t, kc=kc, o=o2, b=b2: t.matmul(
                        b.ap[:, 0:T], ring[:, o + kc * 128:o + (kc + 1) * 128],
                        xn_fm[:, kc * T:(kc + 1) * T],
                        start=(kc == 0), stop=(kc == NKC - 1)))
                mm_group(b2, mml, extra_waits=[e_q2])
                ev_sg = bank_read("act", b1, lambda e, b=b1: e.activation(
                    out=wb(WB_SIG, T), in_=b.ap[:, 0:T], func=AF.Sigmoid),
                    extra_waits=[ev_hid])
                if ti == 0 and hc == 0:
                    tap("sig0", lambda: wb(WB_SIG, T), [128, T], BF16, [ev_sg])
                ev_hid = bank_read("dve", b2, lambda e, hc=hc, b=b2: e.tensor_tensor(
                    out=wb(WB_HID + hc * T, T), in0=b.ap[:, 0:T],
                    in1=wb(WB_SIG, T), op=ALU.mult),
                    extra_waits=[ev_sg, tap_ev[0] if hc == 0 else None])
                ring_release(ent1, kb.now("pe"))
                ring_release(ent2, kb.now("pe"))
            if ti == 0:
                tap("hid0", lambda: wb(WB_HID, 4 * T), [128, 4 * T], BF16, [ev_hid])

            if ti == 0:
                fs_dy = {0: None, 1: None}
            for oc in range(NKC):
                o3, e_q3, ent3 = ring_push(f"q3_{oc}", SEG_Q3)
                bnkf = pb[oc % 2]
                mml = []
                for kc in range(NHC):
                    mml.append(lambda t, kc=kc, o=o3, b=bnkf: t.matmul(
                        b.ap[:, 0:T], ring[:, o + kc * 128:o + (kc + 1) * 128],
                        wb(WB_HID + kc * T, T),
                        start=(kc == 0), stop=(kc == NHC - 1)))
                mm_group(bnkf, mml, extra_waits=[e_q3, ev_hid])
                fo = (oc % 2) * T
                ev_f = bank_read("dve", bnkf, lambda e, oc=oc, b=bnkf, fo=fo:
                                 e.tensor_tensor(
                                     out=wf(WF["fsa"] + fo, T), in0=b.ap[:, 0:T],
                                     in1=attn_bf[:, oc * T:(oc + 1) * T], op=ALU.add),
                                 extra_waits=[fs_dy[oc % 2]])
                bt = pb[2 + oc % 2]
                mml = []
                for tc in range(NCH):
                    mml.append(lambda t, tc=tc, b=bt, fo=fo: t.matmul(
                        b.ap[:, tc * 128:(tc + 1) * 128],
                        wf(WF["fsa"] + fo + tc * 128, 128), ident_f[:],
                        is_transpose=True, start=(tc == 0), stop=(tc == NCH - 1)))
                mm_group(bt, mml, extra_waits=[ev_f])
                ev_fb = bank_read("dve", bt, lambda e, b=bt, fo=fo: e.tensor_copy(
                    wf(WF["fsb"] + fo, T), b.ap[:, 0:512]))
                dsty = bass.AP(y_h.tensor, y_h.offset + ti * T * D + oc * 128,
                               [[D, 128], [128 * D, NCH], [1, 128]])
                srcy = bass.AP(work_f32.ap().tensor,
                               work_f32.ap().offset + WF["fsb"] + fo,
                               [work_f32.ap().ap[0], [128, NCH], [1, 128]])
                fs_dy[oc % 2] = kb.op("sp", lambda e, d=dsty, s=srcy:
                                      e.dma_start(out=d, in_=s),
                                      waits=[ev_fb], dma="dy")
                ring_release(ent3, kb.now("pe"))

            if stop_after == "tile0":
                break

        kb.wait_only("sp", [Ev("dy", kb.semtot.get("dy", 0))])

        kb.check_deadlock()

        @block.tensor
        def _(eng):
            kb.emit("pe", eng, sems)

        @block.vector
        def _(eng):
            kb.emit("dve", eng, sems)

        @block.scalar
        def _(eng):
            kb.emit("act", eng, sems)

        @block.gpsimd
        def _(eng):
            kb.emit("pool", eng, sems)

        @block.sync
        def _(eng):
            kb.emit("sp", eng, sems)

    return nc, tap_out


def kernel(L0_attn_hidden_states, L0_attn_w_qkv, L0_attn_w_z, L0_attn_w_b,
           L0_attn_w_a, L0_attn_w_out, L0_attn_conv_w, L0_attn_dt_bias,
           L0_attn_A_log, L0_attn_norm_weight, L0_ffn_w1, L0_ffn_w2, L0_ffn_w3):
    x = np.ascontiguousarray(np.asarray(L0_attn_hidden_states, dtype=np.float32))
    common = {
        "w_qkv": np.ascontiguousarray(L0_attn_w_qkv, dtype=np.float32),
        "w_z": np.ascontiguousarray(L0_attn_w_z, dtype=np.float32),
        "w_b": np.ascontiguousarray(L0_attn_w_b, dtype=np.float32),
        "w_a": np.ascontiguousarray(L0_attn_w_a, dtype=np.float32),
        "w_out": np.ascontiguousarray(L0_attn_w_out, dtype=np.float32),
        "conv_w": np.ascontiguousarray(L0_attn_conv_w, dtype=np.float32),
        "dt_bias": np.ascontiguousarray(L0_attn_dt_bias, dtype=np.float32),
        "A_log": np.ascontiguousarray(L0_attn_A_log, dtype=np.float32),
        "norm_weight": np.ascontiguousarray(L0_attn_norm_weight, dtype=np.float32),
        "w1": np.ascontiguousarray(L0_ffn_w1, dtype=np.float32),
        "w2": np.ascontiguousarray(L0_ffn_w2, dtype=np.float32),
        "w3": np.ascontiguousarray(L0_ffn_w3, dtype=np.float32),
    }
    nc, _ = build_core()
    in_maps = [dict(common, x=np.ascontiguousarray(x[b])) for b in range(B)]
    res = run_bass_kernel_spmd(nc, in_maps, core_ids=list(range(B)))
    return np.stack([res.results[b]["y"] for b in range(B)], axis=0)


# version 11
# speedup vs baseline: 1.0184x; 1.0046x over previous
import numpy as np
from contextlib import ExitStack

import concourse.bass as bass
import concourse.mybir as mybir
from concourse.bass_utils import run_bass_kernel_spmd

F32 = mybir.dt.float32
BF16 = mybir.dt.bfloat16
AF = mybir.ActivationFunctionType
ALU = mybir.AluOpType

B, S, D = 8, 4096, 2048
HK, HV = 16, 32
DK = DV = 128
KDIM, VDIM = HK * DK, HV * DV
CONV_DIM, KSIZE = 2 * KDIM + VDIM, 4
HFF = 5632
NHC = HFF // 128
T = 512
NT = S // T
C = 128
NCH = T // C
NGRP = 8
NKC = D // 128
LNEPS, L2EPS, RMSEPS, FFEPS = 1e-5, 1e-6, 1e-6, 1e-8
RING_COLS = 18176

ENGINES = ("pe", "dve", "act", "pool", "sp")
ENG_SEM = {e: "s_" + e for e in ENGINES}


class Ev:
    __slots__ = ("sem", "val")

    def __init__(self, sem, val):
        self.sem, self.val = sem, val


class KB:
    def __init__(self):
        self.ops = {e: [] for e in ENGINES}
        self.semtot = {}
        self.waited = {e: {} for e in ENGINES}

    def now(self, eng):
        return Ev(ENG_SEM[eng], self.semtot.get(ENG_SEM[eng], 0))

    def _wait(self, eng, ev):
        if ev is None or ev.val <= 0:
            return
        cur = self.waited[eng].get(ev.sem, 0)
        if ev.val > cur:
            self.ops[eng].append(("w", ev.sem, ev.val))
            self.waited[eng][ev.sem] = ev.val

    def op(self, eng, fn, waits=(), sig=False, dma=None):
        for w in waits:
            self._wait(eng, w)
        sem = inc = None
        if dma is not None:
            sem, inc = dma, 16
        elif sig:
            sem, inc = ENG_SEM[eng], 1
        if sem is not None:
            self.semtot[sem] = self.semtot.get(sem, 0) + inc
            self.ops[eng].append(("c", fn, sem, inc))
            return Ev(sem, self.semtot[sem])
        self.ops[eng].append(("c", fn, None, 0))
        return None

    def wait_only(self, eng, evs):
        for w in evs:
            self._wait(eng, w)

    def check_deadlock(self):
        ptr = {e: 0 for e in ENGINES}
        cnt = {}
        progress = True
        while progress:
            progress = False
            for e in ENGINES:
                while ptr[e] < len(self.ops[e]):
                    it = self.ops[e][ptr[e]]
                    if it[0] == "w" and cnt.get(it[1], 0) < it[2]:
                        break
                    if it[0] == "c" and it[2] is not None:
                        cnt[it[2]] = cnt.get(it[2], 0) + it[3]
                    ptr[e] += 1
                    progress = True
        stuck = {e: ptr[e] for e in ENGINES if ptr[e] < len(self.ops[e])}
        if stuck:
            detail = {}
            for e in stuck:
                it = self.ops[e][ptr[e]]
                detail[e] = (f"op {ptr[e]}/{len(self.ops[e])} waiting "
                             f"{it[1]}>={it[2]} (now {cnt.get(it[1], 0)})")
            raise RuntimeError(f"build-time deadlock: {detail}")

    def emit(self, eng, engobj, sems):
        for item in self.ops[eng]:
            if item[0] == "w":
                engobj.wait_ge(sems[item[1]], item[2])
            else:
                inst = item[1](engobj)
                if item[2] is not None:
                    inst.then_inc(sems[item[2]], item[3])


class Bank:
    def __init__(self, ap):
        self.ap = ap
        self.wev = None
        self.rev = None


def build_core(taps=None, stop_after="full"):
    taps = taps or []
    tap_out = []

    nc = bass.Bass("TRN2", target_bir_lowering=False, debug=False)

    x_h = nc.dram_tensor("x", [S, D], F32, kind="ExternalInput").ap()
    wqkv_h = nc.dram_tensor("w_qkv", [D, CONV_DIM], F32, kind="ExternalInput").ap()
    wz_h = nc.dram_tensor("w_z", [D, VDIM], F32, kind="ExternalInput").ap()
    wb_h = nc.dram_tensor("w_b", [D, HV], F32, kind="ExternalInput").ap()
    wa_h = nc.dram_tensor("w_a", [D, HV], F32, kind="ExternalInput").ap()
    wout_h = nc.dram_tensor("w_out", [VDIM, D], F32, kind="ExternalInput").ap()
    convw_h = nc.dram_tensor("conv_w", [CONV_DIM, KSIZE], F32, kind="ExternalInput").ap()
    dtb_h = nc.dram_tensor("dt_bias", [HV], F32, kind="ExternalInput").ap()
    alog_h = nc.dram_tensor("A_log", [HV], F32, kind="ExternalInput").ap()
    nw_h = nc.dram_tensor("norm_weight", [DV], F32, kind="ExternalInput").ap()
    w1_h = nc.dram_tensor("w1", [HFF, D], F32, kind="ExternalInput").ap()
    w2_h = nc.dram_tensor("w2", [HFF, D], F32, kind="ExternalInput").ap()
    w3_h = nc.dram_tensor("w3", [D, HFF], F32, kind="ExternalInput").ap()
    y_h = nc.dram_tensor("y", [S, D], F32, kind="ExternalOutput").ap()

    SEG_QQKK = 16 * 512
    SEG_WOUT = 4 * 2048
    SEG_Q12 = 16 * 128
    SEG_Q3 = NHC * 128
    tape_cols = 0
    seg_off = {}

    def add_seg(name, cols):
        nonlocal tape_cols
        seg_off[name] = tape_cols
        tape_cols += cols

    for g in range(NGRP):
        add_seg(f"qqkk{g}", SEG_QQKK)
        add_seg(f"vvvv{g}", SEG_QQKK)
        add_seg(f"zzzz{g}", SEG_QQKK)
        add_seg(f"wout{g}", SEG_WOUT)
    for hc in range(NHC):
        add_seg(f"q1_{hc}", SEG_Q12)
        add_seg(f"q2_{hc}", SEG_Q12)
    for oc in range(NKC):
        add_seg(f"q3_{oc}", SEG_Q3)
    wtape = nc.dram_tensor("wtape", [128, tape_cols], BF16, kind="Internal").ap()

    kb = KB()

    with ExitStack() as ctx:
        def sb(name, shape, dt):
            return ctx.enter_context(nc.sbuf_tensor(name, shape, dt))

        xn_fm = sb("xn_fm", [128, NKC * T], BF16)
        attn_bf = sb("attn_bf", [128, NKC * T], BF16)
        x_tm = sb("x_tm", [128, 2 * NCH * D], BF16)
        work_bf = sb("work_bf", [128, 23296], BF16)
        work_f32 = sb("work_f32", [128, 6720], F32)
        ring = sb("ring", [128, RING_COLS], BF16)
        s_f32 = sb("s_f32", [128, HV * DV], F32)
        wba = sb("wba", [128, NKC * 64], BF16)
        conv_w = sb("conv_w_sb", [128, 64 * KSIZE], F32)
        conv_hist = sb("conv_hist", [128, 64 * (KSIZE - 1)], BF16)
        ident_b = sb("ident_b", [128, 128], BF16)
        ident_f = sb("ident_f", [128, 128], F32)
        two_i = sb("two_i", [128, 128], BF16)
        mask_i = sb("mask_i", [128, 128], F32)
        mask_s = sb("mask_s", [128, 128], F32)
        sel4 = sb("sel4", [4, 4 * 128], F32)
        ones_cb = sb("ones_cb", [128, 1], BF16)
        ones_rb = sb("ones_rb", [1, 128], BF16)
        ones_rf = sb("ones_rf", [1, 128], F32)
        eps_ln = sb("eps_ln", [128, 1], F32)
        eps_l2 = sb("eps_l2", [128, 1], F32)
        eps_rms = sb("eps_rms", [128, 1], F32)
        qsc_b = sb("qsc_b", [128, 1], F32)
        zero_c = sb("zero_c", [128, 1], F32)
        one_c = sb("one_c", [128, 1], F32)
        nw_c = sb("nw_c", [128, 1], F32)
        dtb_c = sb("dtb_c", [64, 1], F32)
        nega_c = sb("nega_c", [64, 1], F32)
        ba_sb = sb("ba_sb", [64, 3 * T], F32)
        gb_tmh = sb("gb_tmh", [128, NCH * 64], F32)
        exps_t = sb("exps_t", [128, NCH * 32], F32)
        bl_t = sb("bl_t", [128, NCH * 32], F32)
        st4 = sb("st4", [128, 16], F32)
        mrow = sb("mrow", [4, 256], F32)
        mbsb = sb("mbsb", [128, NCH * 256], BF16)
        gam_sb = sb("gam_sb", [128, 4], F32)
        nth_sb = sb("nth_sb", [128, 4], F32)
        absc = sb("absc", [128, NHC], F32)
        gred = sb("gred", [128, 128], F32)

        WB = {}
        _o = 0
        for nm, w in [("qg", 2 * T), ("kg", 2 * T), ("vg", 4 * T), ("zg", 4 * T),
                      ("kkkq", 2 * NCH * 256), ("di", 4 * NCH * C),
                      ("wt", 4 * NCH * C), ("tt", 4 * NCH * C), ("xt0", NCH * C),
                      ("vt", 4 * NCH * C), ("kt", 2 * NCH * C),
                      ("at", NCH * C), ("g_nw", NCH * C), ("z1", NCH * C),
                      ("u_sb", 4 * C), ("gtd", 4 * C), ("dl", 4 * C), ("dl2", 4 * C),
                      ("sbf", 4 * C)]:
            WB[nm] = _o
            _o += w
        assert _o <= 23296, _o
        WB["gated"] = WB["kkkq"]
        WB["xt1"] = WB["at"]
        WB["qn"] = WB["u_sb"]
        WB["sqb"] = WB["gtd"]
        WB["cvo"] = WB["dl"]
        WB["mxp"] = WB["dl2"]
        WB_HID = 0
        WB_SIG = 22528
        assert NHC * T <= 22528 + 512

        PRE_Q = 0
        PRE_Q2 = 8192
        PRE_T = 16384
        WF = {}
        _o = 0
        for nm, w in [("e1", NCH * C), ("e2", NCH * C), ("osb", 4 * C), ("g1", 4 * C),
                      ("u1", 4 * C), ("sq", D), ("fsa", 2 * T), ("fsb", 2 * T),
                      ("ms", 8), ("rstd", 8), ("lcc", 16)]:
            WF[nm] = _o
            _o += w
        assert _o <= 6720, _o
        PRE_SQ = 0

        pb = [Bank(ctx.enter_context(nc.psum_tensor(f"pb{i}", [128, 512], F32)).ap())
              for i in range(7)]
        pt = Bank(ctx.enter_context(nc.psum_tensor("pt7", [128, 1024], BF16)).ap())

        sem_names = list(ENG_SEM.values()) + ["dw", "dx", "dy", "dpre"]
        sems = {n: ctx.enter_context(nc.semaphore(n)) for n in sem_names}
        block = ctx.enter_context(nc.Block())

        def mm_group(bank, mm_list, extra_waits=()):
            waits = [w for w in extra_waits if w is not None]
            if bank.rev is not None:
                waits.append(bank.rev)
            n = len(mm_list)
            for i, fn in enumerate(mm_list):
                if i < n - 1:
                    kb.op("pe", fn, waits=waits if i == 0 else ())
                else:
                    bank.wev = kb.op("pe", fn, waits=waits if i == 0 else (), sig=True)
            return bank.wev

        def bank_read(eng, bank, fn, extra_waits=()):
            waits = [bank.wev] + [w for w in extra_waits if w is not None]
            ev = kb.op(eng, fn, waits=waits, sig=True)
            bank.rev = ev
            return ev

        ring_state = {"cursor": 0, "occ": []}

        def ring_push(seg_name, cols):
            cur = ring_state["cursor"]
            if cur + cols > RING_COLS:
                cur = 0
            start, end = cur, cur + cols
            waits, keep = [], []
            for ent in ring_state["occ"]:
                if ent[0] < end and ent[1] > start:
                    assert ent[2] is not None, (
                        f"ring overlap with unreleased segment {ent} pushing "
                        f"{seg_name} at {start}:{end}")
                    waits.append(ent[2])
                else:
                    keep.append(ent)
            ring_state["occ"] = keep
            ent = [start, end, None]
            ring_state["occ"].append(ent)
            toff = seg_off[seg_name]
            ev = kb.op("sp",
                       lambda e, a=toff, n=cols, c=start: e.dma_start(
                           out=ring[:, c:c + n], in_=wtape[:, a:a + n]),
                       waits=waits, dma="dw")
            ring_state["cursor"] = end
            return start, ev, ent

        def ring_release(ent, ev):
            ent[2] = ev

        tap_ev = [None]

        def tap(name, ap_fn, shape, dt, waits):
            if name in taps:
                th = nc.dram_tensor("tap_" + name, shape, dt, kind="ExternalOutput").ap()
                tap_out.append("tap_" + name)
                tap_ev[0] = kb.op("sp", lambda e: e.dma_start(out=th, in_=ap_fn()),
                                  waits=waits, dma="dy")

        def wb(off, cols):
            return work_bf[:, off:off + cols]

        def wf(off, cols):
            return work_f32[:, off:off + cols]

        def wb3(off, n, step, cols):
            a = work_bf.ap()
            return bass.AP(a.tensor, a.offset + off, [a.ap[0], [step, n], [1, cols]])

        def wf3(off, n, step, cols):
            a = work_f32.ap()
            return bass.AP(a.tensor, a.offset + off, [a.ap[0], [step, n], [1, cols]])

        def bcast3(tensor, off, n, step, cols):
            a = tensor.ap()
            return bass.AP(a.tensor, a.offset + off, [a.ap[0], [step, n], [0, cols]])

        def aff(out_ap, fill, pattern, cm, base=0, cmp=ALU.not_equal):
            return lambda e: e.affine_select(out=out_ap, in_=out_ap, compare_op=cmp,
                                             fill=fill, base=base, pattern=pattern,
                                             channel_multiplier=cm)

        kb.op("pool", lambda e: e.memset(ident_b[:], 0.0))
        kb.op("pool", aff(ident_b[:], 1.0, [[-1, 128]], 1))
        kb.op("pool", lambda e: e.memset(ident_f[:], 0.0))
        kb.op("pool", aff(ident_f[:], 1.0, [[-1, 128]], 1))
        kb.op("pool", lambda e: e.memset(two_i[:], 0.0))
        kb.op("pool", aff(two_i[:], 2.0, [[-1, 128]], 1))
        kb.op("pool", lambda e: e.memset(mask_i[:], 0.0))
        kb.op("pool", aff(mask_i[:], -1e9, [[1, 128]], -1, cmp=ALU.is_ge))
        kb.op("pool", lambda e: e.memset(mask_s[:], 0.0))
        kb.op("pool", aff(mask_s[:], 1e9, [[-1, 128]], 1, cmp=ALU.is_gt))
        kb.op("pool", lambda e: e.memset(sel4[:], 0.0))
        kb.op("pool", aff(sel4[:], 1.0, [[-1, 4], [0, 128]], 1))
        for t_, v_ in ((ones_cb, 1.0), (ones_rb, 1.0), (ones_rf, 1.0),
                       (eps_ln, LNEPS), (eps_l2, L2EPS), (eps_rms, RMSEPS),
                       (qsc_b, float(np.log(DK ** -0.5))), (zero_c, 0.0),
                       (one_c, 1.0)):
            kb.op("pool", lambda e, t=t_, v=v_: e.memset(t[:], v))
        kb.op("pool", lambda e: e.memset(conv_hist[:], 0.0))
        kb.op("pool", lambda e: e.memset(s_f32[:], 0.0))
        ev_nw = kb.op("pool", lambda e: e.dma_start(out=nw_c[:], in_=nw_h[:, None]),
                      dma="dpre")
        kb.op("pool", lambda e: e.dma_start(out=dtb_c[32:64, :], in_=dtb_h[:, None]),
              dma="dpre")
        ev_al = kb.op("pool", lambda e: e.dma_start(out=nega_c[32:64, :],
                                                    in_=alog_h[:, None]), dma="dpre")
        ev_cw = kb.op("pool",
                      lambda e: e.dma_start(
                          out=conv_w.ap().rearrange("p (o j) -> p o j", o=64),
                          in_=convw_h.rearrange("(o p) j -> p o j", p=128)),
                      dma="dpre")
        ev_consts = kb.op("pool", lambda e: e.memset(absc[:], 0.0), sig=True)

        ev_na1 = kb.op("act", lambda e: e.activation(out=nega_c[32:64, :],
                                                     in_=nega_c[32:64, :], func=AF.Exp),
                       waits=[ev_al], sig=True)
        ev_nega = kb.op("dve", lambda e: e.tensor_scalar_mul(
            nega_c[32:64, :], nega_c[32:64, :], -1.0), waits=[ev_na1], sig=True)

        for g in range(NGRP if stop_after != "consts" else 0):
            so = seg_off[f"qqkk{g}"]
            for half, c0 in ((0, 2 * g * 128), (1, (16 + 2 * g) * 128)):
                src = bass.AP(wqkv_h.tensor, wqkv_h.offset + c0,
                              [[128 * CONV_DIM, NKC], [CONV_DIM, 128], [1, 256]])
                dst = bass.AP(wtape.tensor, wtape.offset + so + half * 256,
                              [[512, NKC], [tape_cols, 128], [1, 256]])
                kb.op("pool", lambda e, s=src, d=dst: e.dma_start(out=d, in_=s),
                      dma="dpre")
            so = seg_off[f"vvvv{g}"]
            src = bass.AP(wqkv_h.tensor, wqkv_h.offset + (32 + 4 * g) * 128,
                          [[128 * CONV_DIM, NKC], [CONV_DIM, 128], [1, 512]])
            dst = bass.AP(wtape.tensor, wtape.offset + so,
                          [[512, NKC], [tape_cols, 128], [1, 512]])
            kb.op("pool", lambda e, s=src, d=dst: e.dma_start(out=d, in_=s), dma="dpre")
            so = seg_off[f"zzzz{g}"]
            src = bass.AP(wz_h.tensor, wz_h.offset + g * 512,
                          [[128 * VDIM, NKC], [VDIM, 128], [1, 512]])
            dst = bass.AP(wtape.tensor, wtape.offset + so,
                          [[512, NKC], [tape_cols, 128], [1, 512]])
            kb.op("pool", lambda e, s=src, d=dst: e.dma_start(out=d, in_=s), dma="dpre")
        ev_wba = None
        for (wsrc, half) in ((wb_h, 0), (wa_h, 1)):
            src = bass.AP(wsrc.tensor, wsrc.offset,
                          [[HV, 128], [128 * HV, NKC], [1, 32]])
            a = wba.ap()
            dst = bass.AP(a.tensor, a.offset + half * 32,
                          [a.ap[0], [64, NKC], [1, 32]])
            ev_wba = kb.op("pool", lambda e, s=src, d=dst: e.dma_start(out=d, in_=s),
                           dma="dpre")

        ev_prev_tape = None
        for g in range(NGRP if stop_after != "consts" else 0):
            for kc in range(4):
                r0 = g * 512 + kc * 128
                src = bass.AP(wout_h.tensor, wout_h.offset + r0 * D, [[D, 128], [1, D]])
                evl = kb.op("pool", lambda e, s=src: e.dma_start(out=wb(PRE_Q, D), in_=s),
                            waits=[ev_prev_tape] if ev_prev_tape else [], dma="dpre")
                ev_sc = kb.op("dve", lambda e: e.tensor_scalar_mul(
                    wb(PRE_Q, D), wb(PRE_Q, D), nw_c[:]), waits=[evl, ev_nw], sig=True)
                so = seg_off[f"wout{g}"] + kc * 2048
                ev_prev_tape = kb.op("sp", lambda e, a=so: e.dma_start(
                    out=wtape[:, a:a + 2048], in_=wb(PRE_Q, D)),
                    waits=[ev_sc], dma="dw")

        ev_gam_row = {}
        _gl = ((1, w1_h, NHC, D), (2, w2_h, NHC, D), (3, w3_h, NKC, HFF)) \
            if stop_after not in ("consts", "tape") else ()
        for mi, wh, nslab, cols in _gl:
            ev_abs = None
            for sl in range(nslab):
                src = bass.AP(wh.tensor, wh.offset + sl * 128 * cols,
                              [[cols, 128], [1, cols]])
                po = (sl % 2) * 2048 if cols == D else 0
                war = ev_abs if (cols != D or sl >= 2) else None
                evl = kb.op("pool", lambda e, s=src, c=cols, p=po: e.dma_start(
                    out=wf(PRE_SQ + p, c), in_=s),
                    waits=[w for w in (war,) if w], dma="dpre")
                ev_abs = kb.op("act", lambda e, j=sl, c=cols, p=po: e.activation(
                    out=wf(PRE_SQ + p, c), in_=wf(PRE_SQ + p, c), func=AF.Abs,
                    accum_out=absc[:, j:j + 1]), waits=[evl, ev_consts], sig=True)
            ev_red = kb.op("act", lambda e, m=mi, n=nslab: e.activation(
                out=gred[:, 0:n], in_=absc[:, 0:n], func=AF.Copy,
                accum_out=st4[:, 12 + m:13 + m]), waits=[ev_abs], sig=True)
            bnk = pb[2]
            mm_group(bnk, [lambda t, m=mi, b=bnk: t.matmul(
                b.ap[0:1, 0:128], st4[:, 12 + m:13 + m], ident_f[:],
                is_transpose=True, start=True, stop=True)],
                extra_waits=[ev_red])
            ev_gam_row[mi] = bank_read("act", bnk, lambda e, m=mi, b=bnk: e.activation(
                out=gred[0:1, 0:128], in_=b.ap[0:1, 0:128], func=AF.Copy,
                accum_out=st4[0:1, 8 + m:9 + m]))
        ev_th = None
        for mi in ((1, 2, 3) if stop_after not in ("consts", "tape") else ()):
            numel = HFF * D
            ev1 = kb.op("dve", lambda e, m=mi, n=numel: e.tensor_scalar(
                out=st4[0:1, 8 + m:9 + m], in0=st4[0:1, 8 + m:9 + m],
                scalar1=0.5 / n, scalar2=0.5 * FFEPS, op0=ALU.mult, op1=ALU.add),
                waits=[ev_gam_row[mi]], sig=True)
            bnk = pb[2]
            mm_group(bnk, [lambda t, m=mi, b=bnk: t.matmul(
                b.ap[:, 0:1], ones_rf[:], st4[0:1, 8 + m:9 + m],
                start=True, stop=True)], extra_waits=[ev1])
            ev2 = bank_read("dve", bnk, lambda e, m=mi, b=bnk: e.tensor_copy(
                gam_sb[:, m:m + 1], b.ap[:, 0:1]))
            ev_th = kb.op("dve", lambda e, m=mi: e.tensor_scalar_mul(
                nth_sb[:, m:m + 1], gam_sb[:, m:m + 1], -1.0), waits=[ev2], sig=True)
        tap("gamma_th", lambda: gam_sb[:, 0:4], [128, 4], F32, [ev_th])
        tap("pre_st4", lambda: st4[:, 0:16], [128, 16], F32, [ev_th])
        tap("pre_absc", lambda: absc[:, 0:NHC], [128, NHC], F32, [ev_th])

        ev_presq_free = [ev_gam_row.get(3), ev_gam_row.get(3)]
        ev_stage_pe = [None, None]
        ev_tape_slot = [None, None]
        par_ct = [0]

        def quant_and_transpose(src_ap, cols, mi, nblk, seg):
            nonlocal ev_prev_tape
            if cols == D:
                par = par_ct[0] % 2
                par_ct[0] += 1
                oq, oq2, ot = PRE_Q + par * 2048, PRE_Q2 + par * 2048, \
                    PRE_T + par * 2560
                presq = PRE_SQ + par * 2048
            else:
                par = 0
                oq, oq2, ot = PRE_Q, PRE_Q2, PRE_T
                presq = PRE_SQ
            waitl = [ev_presq_free[par]]
            if cols != D:
                waitl += [ev_presq_free[1 - par], ev_stage_pe[1 - par],
                          ev_tape_slot[1 - par]]
            evl = kb.op("pool", lambda e, s=src_ap, c=cols, o=presq: e.dma_start(
                out=wf(o, c), in_=s),
                waits=[w for w in waitl if w], dma="dpre")
            kb.op("dve", lambda e, c=cols, m=mi, o=presq, q=oq:
                  e.tensor_single_scalar(
                      wb(q, c), wf(o, c), gam_sb[:, m:m + 1], ALU.is_gt),
                  waits=[w for w in (evl, ev_th, ev_stage_pe[par]) if w])
            ev_presq_free[par] = kb.op(
                "dve", lambda e, c=cols, m=mi, o=presq, q=oq2:
                e.tensor_single_scalar(
                    wb(q, c), wf(o, c), nth_sb[:, m:m + 1], ALU.is_lt), sig=True)
            ev_q = kb.op("dve", lambda e, c=cols, q=oq, q2=oq2: e.tensor_tensor(
                out=wb(q, c), in0=wb(q, c), in1=wb(q2, c),
                op=ALU.subtract), sig=True)
            ev_ev = None
            tape_war = ev_tape_slot[par]
            for g0 in range(0, nblk, 8):
                gn = min(8, nblk - g0)
                mml = []
                for j in range(gn):
                    blkc = oq + (g0 + j) * 128
                    mml.append(lambda t, bc=blkc, jj=j, gn=gn: t.matmul(
                        pt.ap[:, jj * 128:(jj + 1) * 128], wb(bc, 128), ident_b[:],
                        is_transpose=True, start=(jj == 0), stop=(jj == gn - 1)))
                mm_group(pt, mml, extra_waits=[ev_q, ev_consts])
                ev_ev = bank_read("dve", pt, lambda e, g0=g0, gn=gn, o=ot:
                                  e.tensor_copy(
                                      wb(o + g0 * 128, gn * 128),
                                      pt.ap[:, 0:gn * 128]),
                                  extra_waits=[tape_war] if g0 == 0 else ())
            ev_stage_pe[par] = kb.now("pe")
            ev = kb.op("sp", lambda e, a=seg_off[seg], n=nblk, o=ot: e.dma_start(
                out=wtape[:, a:a + n * 128], in_=wb(o, n * 128)),
                waits=[ev_ev], dma="dw")
            ev_tape_slot[par] = ev
            ev_prev_tape = ev

        _ql = ((1, w1_h), (2, w2_h)) if stop_after not in ("consts", "tape") else ()
        for mi, wh in _ql:
            for hb in range(NHC):
                src = bass.AP(wh.tensor, wh.offset + hb * 128 * D, [[D, 128], [1, D]])
                quant_and_transpose(src, D, mi, NKC, f"q{mi}_{hb}")
        for ob in range(NKC if stop_after not in ("consts", "tape") else 0):
            src = bass.AP(w3_h.tensor, w3_h.offset + ob * 128 * HFF,
                          [[HFF, 128], [1, HFF]])
            quant_and_transpose(src, HFF, 3, NHC, f"q3_{ob}")

        ev_tape_done = Ev("dw", kb.semtot.get("dw", 0))
        ev_pre_dve = kb.now("dve")
        ev_pre_pe = kb.now("pe")
        ev_pre_act = kb.now("act")

        x_free = [None, None]

        _nt = 0 if stop_after in ("consts", "tape", "pre") else NT
        if stop_after in ("xload", "lnstats", "lnT", "ln1",
                          "ba1", "ba2", "ba3"):
            _nt = 1
        for ti in range(_nt):
            ring_state["cursor"] = 0
            png = ti % 2
            xoff = png * NCH * D
            a = x_tm.ap()
            dst_x = bass.AP(a.tensor, a.offset + xoff, [a.ap[0], [D, NCH], [1, D]])
            src_x = bass.AP(x_h.tensor, x_h.offset + ti * T * D,
                            [[D, 128], [128 * D, NCH], [1, D]])
            wv = [x_free[png]] if x_free[png] else []
            ev_x = kb.op("pool", lambda e, s=src_x, d=dst_x: e.dma_start(out=d, in_=s),
                         waits=wv, dma="dx")
            if ti == 0:
                tap("x_tm", lambda: x_tm[:, 0:NCH * D], [128, NCH * D], BF16, [ev_x])
            if stop_after == "xload":
                break

            ev_st = None
            first_w = [ev_x]
            if ti == 0:
                first_w += [ev_pre_act, ev_pre_dve, ev_pre_pe, ev_consts]
            else:
                first_w += [kb.now("dve"), kb.now("pe")]
            for tc in range(NCH):
                kb.op("act", lambda e, tc=tc, xo=xoff: e.activation(
                    out=wf(WF["sq"], D), in_=x_tm[:, xo + tc * D:xo + (tc + 1) * D],
                    func=AF.Square, accum_out=st4[:, 4 + tc:5 + tc]),
                    waits=first_w if tc == 0 else ())
                ev_st = kb.op("act", lambda e, tc=tc, xo=xoff: e.activation(
                    out=wf(WF["sq"], D), in_=x_tm[:, xo + tc * D:xo + (tc + 1) * D],
                    func=AF.Copy, accum_out=st4[:, tc:tc + 1]), sig=True)
            e1w = [ev_st]
            if ti == 0:
                e1w += [ev_pre_pe, ev_tape_done]
            e1 = kb.op("dve", lambda e: e.tensor_scalar_mul(st4[:, 0:8], st4[:, 0:8],
                                                            1.0 / D),
                       waits=e1w, sig=True)
            e2 = kb.op("dve", lambda e: e.tensor_tensor(
                out=st4[:, 8:12], in0=st4[:, 0:4], in1=st4[:, 0:4], op=ALU.mult),
                waits=[e1], sig=True)
            e3 = kb.op("dve", lambda e: e.tensor_tensor(
                out=st4[:, 8:12], in0=st4[:, 4:8], in1=st4[:, 8:12], op=ALU.subtract),
                waits=[e2], sig=True)
            e4 = kb.op("act", lambda e: e.activation(
                out=st4[:, 8:12], in_=st4[:, 8:12], func=AF.Sqrt, bias=eps_ln[:]),
                waits=[e3], sig=True)
            e5 = kb.op("dve", lambda e: e.reciprocal(st4[:, 12:16], st4[:, 8:12]),
                       waits=[e4], sig=True)
            ev_xn_tm = []
            for tc in range(NCH):
                ev = kb.op("dve", lambda e, tc=tc, xo=xoff: e.tensor_scalar(
                    out=x_tm[:, xo + tc * D:xo + (tc + 1) * D],
                    in0=x_tm[:, xo + tc * D:xo + (tc + 1) * D],
                    scalar1=st4[:, tc:tc + 1], scalar2=st4[:, 12 + tc:13 + tc],
                    op0=ALU.subtract, op1=ALU.mult), waits=[e5], sig=True)
                ev_xn_tm.append(ev)
            if ti == 0:
                tap("xn_tm", lambda: x_tm[:, 0:NCH * D], [128, NCH * D], BF16,
                    [ev_xn_tm[-1]])
            if stop_after == "lnstats":
                break
            mm_group(pb[3], [
                lambda t: t.matmul(pb[3].ap[0:4, 0:128], st4[:, 0:4], ident_f[:],
                                   is_transpose=True, start=True, stop=False),
                lambda t: t.matmul(pb[3].ap[0:4, 128:256], st4[:, 8:12], ident_f[:],
                                   is_transpose=True, start=False, stop=True)],
                extra_waits=[e5])
            ev_mr = bank_read("dve", pb[3], lambda e: e.tensor_copy(
                mrow[:], pb[3].ap[0:4, 0:256]))
            ev_mbsb = None
            for bk, tcs in ((pb[3], (0, 1)), (pb[4], (2, 3))):
                mml = []
                for j, tc in enumerate(tcs):
                    mml.append(lambda t, tc=tc, j=j, b=bk: t.matmul(
                        b.ap[:, j * 256:(j + 1) * 256],
                        sel4[:, tc * 128:(tc + 1) * 128], mrow[:],
                        start=(j == 0), stop=(j == 1)))
                mm_group(bk, mml, extra_waits=[ev_mr])
                ev_mbsb = bank_read("dve", bk, lambda e, b=bk, t0=tcs[0]: e.tensor_copy(
                    mbsb[:, t0 * 256:t0 * 256 + 512], b.ap[:, 0:512]))

            ev_xn = None
            for tc in range(NCH):
                for g0 in range(0, NKC, 8):
                    mml = []
                    for j in range(8):
                        fc = g0 + j
                        mml.append(lambda t, fc=fc, j=j, tc=tc, xo=xoff: t.matmul(
                            pt.ap[:, j * 128:(j + 1) * 128],
                            x_tm[:, xo + tc * D + fc * 128:xo + tc * D + fc * 128 + 128],
                            ident_b[:], is_transpose=True,
                            start=(j == 0), stop=(j == 7)))
                    mm_group(pt, mml, extra_waits=[ev_xn_tm[tc]])
                    an = xn_fm.ap()
                    dst = bass.AP(an.tensor, an.offset + g0 * T + tc * 128,
                                  [an.ap[0], [T, 8], [1, 128]])
                    ev_xn = bank_read("dve", pt, lambda e, d=dst: e.tensor_copy(
                        d, pt.ap.rearrange("p (a b) -> p a b", a=8)))
            x_free[png] = ev_xn
            if ti == 0:
                tap("xn_fm", lambda: xn_fm[:, 0:NKC * T], [128, NKC * T], BF16, [ev_xn])
            if stop_after == "lnT":
                break

            bnk = pb[5]
            mml = []
            for kc in range(NKC):
                mml.append(lambda t, kc=kc, b=bnk: t.matmul(
                    b.ap[0:64, 0:T], wba[:, kc * 64:(kc + 1) * 64],
                    xn_fm[:, kc * T:(kc + 1) * T],
                    start=(kc == 0), stop=(kc == NKC - 1)))
            mm_group(bnk, mml, extra_waits=[ev_xn, ev_wba])
            evb = bank_read("act", bnk, lambda e, b=bnk: e.activation(
                out=ba_sb[0:32, 0:T], in_=b.ap[0:32, 0:T], func=AF.Sigmoid))
            eva1 = kb.op("act", lambda e, b=bnk: e.activation(
                out=ba_sb[32:64, T:2 * T], in_=b.ap[32:64, 0:T], func=AF.Abs,
                bias=dtb_c[32:64, :]), waits=[bnk.wev], sig=True)
            eva2 = kb.op("act", lambda e: e.activation(
                out=ba_sb[32:64, T:2 * T], in_=ba_sb[32:64, T:2 * T], func=AF.Exp,
                scale=-1.0), waits=[eva1], sig=True)
            eva3 = kb.op("act", lambda e: e.activation(
                out=ba_sb[32:64, T:2 * T], in_=ba_sb[32:64, T:2 * T], func=AF.Ln,
                bias=one_c[32:64, :]), waits=[eva2], sig=True)
            eva4 = kb.op("dve", lambda e, b=bnk: e.tensor_scalar(
                out=ba_sb[32:64, 0:T], in0=b.ap[32:64, 0:T],
                scalar1=dtb_c[32:64, :], scalar2=0.0, op0=ALU.add, op1=ALU.max),
                waits=[bnk.wev], sig=True)
            bnk.rev = eva4
            eva5 = kb.op("dve", lambda e: e.tensor_tensor(
                out=ba_sb[32:64, 0:T], in0=ba_sb[32:64, 0:T],
                in1=ba_sb[32:64, T:2 * T], op=ALU.add), waits=[eva3, eva4], sig=True)
            eva6 = kb.op("dve", lambda e: e.tensor_scalar_mul(
                ba_sb[32:64, 0:T], ba_sb[32:64, 0:T], nega_c[32:64, :]),
                waits=[eva5, ev_nega], sig=True)
            if ti == 0:
                tap("ba_g", lambda: ba_sb[0:64, 0:T], [64, T], F32, [eva6, evb])
            if stop_after == "ba1":
                break
            ev_gamma = None
            for cc in range(NCH):
                ev_gamma = kb.op("dve", lambda e, cc=cc: e.tensor_tensor_scan(
                    out=ba_sb[32:64, 2 * T + cc * C:2 * T + (cc + 1) * C],
                    data0=ba_sb[32:64, cc * C:(cc + 1) * C],
                    data1=ba_sb[32:64, cc * C:(cc + 1) * C],
                    initial=0.0, op0=ALU.add, op1=ALU.bypass),
                    waits=[eva6], sig=True)
            if ti == 0:
                tap("ba_gam", lambda: ba_sb[0:64, 2 * T:3 * T], [64, T], F32,
                    [ev_gamma])
            if stop_after == "ba2":
                break
            mml = []
            for cc in range(NCH):
                mml.append(lambda t, cc=cc: t.matmul(
                    pb[5].ap[:, cc * 32:(cc + 1) * 32],
                    ba_sb[32:64, 2 * T + cc * C:2 * T + (cc + 1) * C],
                    ident_f[32:64, 32:64], is_transpose=True,
                    start=(cc == 0), stop=(cc == NCH - 1)))
            mm_group(pb[5], mml, extra_waits=[evb, ev_gamma])
            ev_gb1 = bank_read("dve", pb[5], lambda e: e.tensor_copy(
                bass.AP(gb_tmh.ap().tensor, gb_tmh.ap().offset,
                        [gb_tmh.ap().ap[0], [64, NCH], [1, 32]]),
                pb[5].ap[:, 0:NCH * 32].rearrange("p (a b) -> p a b", a=NCH)))
            mml = []
            for cc in range(NCH):
                mml.append(lambda t, cc=cc: t.matmul(
                    pb[6].ap[:, cc * 32:(cc + 1) * 32],
                    ba_sb[0:32, cc * C:(cc + 1) * C],
                    ident_f[0:32, 0:32], is_transpose=True,
                    start=(cc == 0), stop=(cc == NCH - 1)))
            mm_group(pb[6], mml, extra_waits=[evb, ev_gamma])
            ev_gb2 = bank_read("dve", pb[6], lambda e: e.tensor_copy(
                bass.AP(gb_tmh.ap().tensor, gb_tmh.ap().offset + 32,
                        [gb_tmh.ap().ap[0], [64, NCH], [1, 32]]),
                pb[6].ap[:, 0:NCH * 32].rearrange("p (a b) -> p a b", a=NCH)))
            ev_gb = ev_gb2
            if ti == 0:
                tap("gb_early", lambda: gb_tmh[:, 0:NCH * 64], [128, NCH * 64], F32,
                    [ev_gb])
            if stop_after == "ba3":
                break
            ae, ag = exps_t.ap(), gb_tmh.ap()
            ev_exps = kb.op("act", lambda e: e.activation(
                out=bass.AP(ae.tensor, ae.offset, [ae.ap[0], [32, NCH], [1, 32]]),
                in_=bass.AP(ag.tensor, ag.offset, [ag.ap[0], [64, NCH], [1, 32]]),
                func=AF.Exp), waits=[ev_gb], sig=True)
            ab = bl_t.ap()
            ev_bl = kb.op("dve", lambda e: e.tensor_tensor(
                out=bass.AP(ab.tensor, ab.offset, [ab.ap[0], [32, NCH], [1, 32]]),
                in0=bass.AP(ae.tensor, ae.offset, [ae.ap[0], [32, NCH], [1, 32]]),
                in1=bass.AP(ag.tensor, ag.offset + 32, [ag.ap[0], [64, NCH], [1, 32]]),
                op=ALU.mult), waits=[ev_exps], sig=True)
            if ti == 0:
                tap("gb_tmh", lambda: gb_tmh[:, 0:NCH * 64], [128, NCH * 64], F32,
                    [ev_gb])
            if stop_after in ("ln", "ln1"):
                break

            carry = {"silu_pe": kb.now("pe"),
                     "bcast_pe": kb.now("pe"),
                     "qk_dve": kb.now("dve"),
                     "cvo_act": kb.now("act"),
                     "vg_pe": kb.now("pe"),
                     "qkg_pe": kb.now("pe"),
                     "zg_dve": kb.now("dve"),
                     "sbf_pe": kb.now("pe"),
                     "gated_pe": kb.now("pe")}

            for g in range(NGRP):
                o_qk, ev_qk, ent_qk = ring_push(f"qqkk{g}", SEG_QQKK)
                o_v, ev_v, ent_v = ring_push(f"vvvv{g}", SEG_QQKK)

                def proj_mm_conv(ring_off, ring_sub, qn_off, sq_off, bnk_i,
                                 ready_ev, conv_ch, kind, dst_off=None):
                    bnkp = pb[bnk_i]
                    mml = []
                    for kc in range(NKC):
                        mml.append(lambda t, kc=kc, b=bnkp, ro=ring_off,
                                   rs=ring_sub: t.matmul(
                            b.ap[:, 0:T],
                            ring[:, ro + kc * 512 + rs:ro + kc * 512 + rs + 128],
                            xn_fm[:, kc * T:(kc + 1) * T],
                            start=(kc == 0), stop=(kc == NKC - 1)))
                    mm_group(bnkp, mml, extra_waits=[ready_ev, ev_xn])
                    hco = conv_ch * 3
                    kb.op("dve", lambda e, hco=hco: e.tensor_copy(
                        wb(WB["mxp"], 3), conv_hist[:, hco:hco + 3]),
                        waits=[ev_consts, carry["cvo_act"], carry["sbf_pe"],
                               tap_ev[0]])
                    bank_read("dve", bnkp, lambda e, b=bnkp: e.tensor_copy(
                        wb(WB["mxp"] + 3, T), b.ap[:, 0:T]))
                    kb.op("dve", lambda e, hco=hco: e.tensor_copy(
                        conv_hist[:, hco:hco + 3], wb(WB["mxp"] + T, 3)))
                    if conv_ch == 0:
                        tap("mxp0", lambda: wb(WB["mxp"], T + 3), [128, T + 3],
                            BF16, [kb.now("dve")])
                    kb.op("dve", lambda e, cc_=conv_ch: e.tensor_scalar(
                        out=wb(WB["cvo"], T), in0=wb(WB["mxp"], T),
                        scalar1=conv_w[:, cc_ * 4:cc_ * 4 + 1], scalar2=None,
                        op0=ALU.mult), waits=[ev_cw])
                    for j in (1, 2):
                        kb.op("dve", lambda e, j=j, cc_=conv_ch: e.scalar_tensor_tensor(
                            out=wb(WB["cvo"], T), in0=wb(WB["mxp"] + j, T),
                            scalar=conv_w[:, cc_ * 4 + j:cc_ * 4 + j + 1],
                            in1=wb(WB["cvo"], T), op0=ALU.mult, op1=ALU.add))
                    ev_cv = kb.op("dve", lambda e, cc_=conv_ch: e.scalar_tensor_tensor(
                        out=wb(WB["cvo"], T), in0=wb(WB["mxp"] + 3, T),
                        scalar=conv_w[:, cc_ * 4 + 3:cc_ * 4 + 4],
                        in1=wb(WB["cvo"], T), op0=ALU.mult, op1=ALU.add), sig=True)
                    if kind == "v":
                        ev = kb.op("act", lambda e, d=dst_off: e.activation(
                            out=wb(d, T), in_=wb(WB["cvo"], T), func=AF.Silu),
                            waits=[ev_cv, carry["vg_pe"]], sig=True)
                        carry["cvo_act"] = ev
                        return ev
                    ev_s = kb.op("act", lambda e, q=qn_off: e.activation(
                        out=wb(q, T), in_=wb(WB["cvo"], T), func=AF.Silu),
                        waits=[ev_cv, carry["qk_dve"], carry["silu_pe"],
                               carry["sbf_pe"]], sig=True)
                    carry["cvo_act"] = ev_s
                    ev_sq = kb.op("act", lambda e, q=qn_off, s=sq_off: e.activation(
                        out=wb(s, T), in_=wb(q, T), func=AF.Square),
                        waits=[ev_s], sig=True)
                    return ev_sq

                def proj_l2_finish(qn_off, sq_off, dst_off, kind, ev_sq):
                    b2 = pb[6]
                    mm_group(b2, [lambda t, b=b2, s=sq_off: t.matmul(
                        b.ap[0:1, 0:T], ones_cb[:], wb(s, T),
                        start=True, stop=True)], extra_waits=[ev_sq])
                    carry["silu_pe"] = b2.wev
                    ev_ln1 = bank_read("act", b2, lambda e, b=b2: e.activation(
                        out=wf(WF["u1"], T)[0:1, :], in_=b.ap[0:1, 0:T],
                        func=AF.Ln, bias=eps_l2[0:1, :]))
                    ev_ln2 = kb.op("act", lambda e, k=kind: e.activation(
                        out=wb(WB["z1"], T)[0:1, :], in_=wf(WF["u1"], T)[0:1, :],
                        func=AF.Exp, scale=-0.5,
                        bias=(qsc_b[0:1, :] if k == "q" else zero_c[0:1, :])),
                        waits=[ev_ln1, carry["bcast_pe"]], sig=True)
                    mm_group(b2, [lambda t, b=b2: t.matmul(
                        b.ap[:, 0:T], ones_rb[:], wb(WB["z1"], T)[0:1, :],
                        start=True, stop=True)], extra_waits=[ev_ln2])
                    carry["bcast_pe"] = b2.wev
                    ev = bank_read("dve", b2, lambda e, b=b2, q=qn_off, d=dst_off:
                                   e.tensor_tensor(
                                       out=wb(d, T), in0=wb(q, T), in1=b.ap[:, 0:T],
                                       op=ALU.mult), extra_waits=[carry["qkg_pe"]])
                    carry["qk_dve"] = ev
                    return ev

                ev_qg, ev_kg, ev_vg = [None, None], [None, None], [None] * 4
                qk_phase = []
                for i in range(2):
                    ev_sq = proj_mm_conv(o_qk, i * 128, WB["di"] + i * T,
                                         WB["wt"] + i * T, i % 2, ev_qk,
                                         2 * g + i, "q")
                    qk_phase.append(("q", i, ev_sq))
                for i in range(2):
                    ev_sq = proj_mm_conv(o_qk, 256 + i * 128, WB["di"] + (2 + i) * T,
                                         WB["wt"] + (2 + i) * T, i % 2, ev_qk,
                                         16 + 2 * g + i, "k")
                    qk_phase.append(("k", i, ev_sq))
                ring_release(ent_qk, kb.now("pe"))
                o_z, ev_z, ent_z = ring_push(f"zzzz{g}", SEG_QQKK)
                for kind, i, ev_sq in qk_phase:
                    j = i if kind == "q" else 2 + i
                    dst = (WB["qg"] if kind == "q" else WB["kg"]) + i * T
                    ev = proj_l2_finish(WB["di"] + j * T, WB["wt"] + j * T,
                                        dst, kind, ev_sq)
                    if kind == "q":
                        ev_qg[i] = ev
                    else:
                        ev_kg[i] = ev
                for i in range(4):
                    ev_vg[i] = proj_mm_conv(o_v, i * 128, 0, 0, i % 2, ev_v,
                                            32 + 4 * g + i, "v",
                                            dst_off=WB["vg"] + i * T)
                ring_release(ent_v, kb.now("pe"))
                o_wo, ev_wo, ent_wo = ring_push(f"wout{g}", SEG_WOUT)

                if ti == 0 and g == 0:
                    evq = kb.now("dve")
                    tap("qg0", lambda: wb(WB["qg"], 2 * T), [128, 2 * T], BF16, [evq])
                    tap("kg0", lambda: wb(WB["kg"], 2 * T), [128, 2 * T], BF16, [evq])
                    eva = ev_vg[3]
                    tap("vg0", lambda: wb(WB["vg"], 4 * T), [128, 4 * T], BF16, [eva])

                ev_zg = None
                for cc in range(NCH):
                    bnkz = pb[2 + 2 * (cc % 2)]
                    mml = []
                    for kc in range(NKC):
                        mml.append(lambda t, kc=kc, cc=cc, b=bnkz, oz=o_z: t.matmul(
                            b.ap[:, 0:512],
                            xn_fm[:, kc * T + cc * C:kc * T + (cc + 1) * C],
                            ring[:, oz + kc * 512:oz + (kc + 1) * 512],
                            start=(kc == 0), stop=(kc == NKC - 1)))
                    mm_group(bnkz, mml, extra_waits=[ev_z, ev_xn])
                    ev_zg = bank_read("act", bnkz, lambda e, cc=cc, b=bnkz:
                                      e.activation(
                                          out=wb(WB["zg"] + cc * 512, 512),
                                          in_=b.ap[:, 0:512], func=AF.Silu),
                                      extra_waits=[carry["zg_dve"], tap_ev[0]])
                ring_release(ent_z, kb.now("pe"))

                ev_kkkq = None
                for kl in range(2):
                    for c0 in range(0, NCH, 2):
                        bnkk = pb[2 + 2 * ((c0 // 2) % 2)]
                        mml = []
                        for j, cc in enumerate((c0, c0 + 1)):
                            mml.append(lambda t, cc=cc, kl=kl, j=j, b=bnkk: t.matmul(
                                b.ap[:, j * 256:j * 256 + 128],
                                wb(WB["kg"] + kl * T + cc * C, C),
                                wb(WB["kg"] + kl * T + cc * C, C),
                                start=(j == 0), stop=False))
                            mml.append(lambda t, cc=cc, kl=kl, j=j, b=bnkk: t.matmul(
                                b.ap[:, j * 256 + 128:j * 256 + 256],
                                wb(WB["kg"] + kl * T + cc * C, C),
                                wb(WB["qg"] + kl * T + cc * C, C),
                                start=False, stop=(j == 1)))
                        mm_group(bnkk, mml, extra_waits=[ev_kg[kl], ev_qg[kl]])
                        ev_kkkq = bank_read("act", bnkk,
                                            lambda e, kl=kl, c0=c0, b=bnkk:
                                            e.copy(
                                                wb(WB["kkkq"] + kl * NCH * 256 + c0 * 256, 512),
                                                b.ap[:, 0:512]),
                                            extra_waits=[carry["gated_pe"],
                                                         tap_ev[0]])
                carry["qkg_pe"] = kb.now("pe")
                ev_vt = None
                for b0 in range(0, 16, 8):
                    mml = []
                    for j in range(8):
                        hh, cc = divmod(b0 + j, NCH)
                        mml.append(lambda t, hh=hh, cc=cc, j=j: t.matmul(
                            pt.ap[:, j * 128:(j + 1) * 128],
                            wb(WB["vg"] + hh * T + cc * C, C),
                            ident_b[:], is_transpose=True,
                            start=(j == 0), stop=(j == 7)))
                    mm_group(pt, mml, extra_waits=ev_vg)
                    ev_vt = bank_read("act", pt, lambda e, b0=b0: e.copy(
                        wb(WB["vt"] + b0 * 128, 1024), pt.ap[:, 0:1024]))
                mml = []
                for j in range(8):
                    kl, cc = divmod(j, NCH)
                    mml.append(lambda t, kl=kl, cc=cc, j=j: t.matmul(
                        pt.ap[:, j * 128:(j + 1) * 128],
                        wb(WB["kg"] + kl * T + cc * C, C),
                        ident_b[:], is_transpose=True, start=(j == 0), stop=(j == 7)))
                mm_group(pt, mml, extra_waits=[ev_kg[0], ev_kg[1]])
                ev_kt = bank_read("act", pt, lambda e: e.copy(
                    wb(WB["kt"], 1024), pt.ap[:, 0:1024]))
                carry["vg_pe"] = kb.now("pe")

                ev_lcc = None
                G_SLOT = (WB["g_nw"], WB["dl"])
                Z_SLOT = (WB["z1"], WB["dl2"])
                XT_A = (WB["xt0"], WB["u_sb"])
                XT_B = (WB["at"], WB["gtd"])
                XB = (pb[2], pb[3])
                PBANK = (pb[4], pb[5])
                XTBANK = (pb[6], pb[3])
                ev_zslot = [None] * 4
                ev_slot_pe = None

                def p1_prep(hh):
                    p = hh % 2
                    h32 = (4 * g + hh) % 32
                    kl = hh // 2
                    bnkx = XB[p]
                    mml = []
                    for cc in range(NCH):
                        view = bass.AP(gb_tmh.ap().tensor,
                                       gb_tmh.ap().offset + cc * 64 + h32,
                                       [gb_tmh.ap().ap[0], [0, C]])
                        mml.append(lambda t, v=view, cc=cc, b=bnkx: t.matmul(
                            b.ap[:, cc * C:(cc + 1) * C], v, ident_f[:],
                            is_transpose=True, start=(cc == 0), stop=(cc == NCH - 1)))
                    mm_group(bnkx, mml, extra_waits=[ev_gb])
                    gview = bcast3(gb_tmh, h32, NCH, 64, C)
                    ev_e1 = bank_read("dve", bnkx, lambda e, gv=gview, b=bnkx:
                                      e.tensor_tensor(
                                          out=wf3(WF["e1"], NCH, C, C),
                                          in0=b.ap.rearrange("p (a b) -> p a b", a=NCH),
                                          in1=gv, op=ALU.subtract))
                    mi_v = bass.AP(mask_i.ap().tensor, mask_i.ap().offset,
                                   [mask_i.ap().ap[0], [0, NCH], [1, C]])
                    ev_e2 = kb.op("dve", lambda e, mv=mi_v: e.tensor_tensor(
                        out=wf3(WF["e2"], NCH, C, C), in0=wf3(WF["e1"], NCH, C, C),
                        in1=mv, op=ALU.add), waits=[ev_e1], sig=True)
                    ev_di = kb.op("act", lambda e, hh=hh: e.activation(
                        out=wb(WB["di"] + hh * NCH * C, NCH * C),
                        in_=wf(WF["e2"], NCH * C), func=AF.Exp),
                        waits=[ev_e2, tap_ev[0]], sig=True)
                    ms_v = bass.AP(mask_s.ap().tensor, mask_s.ap().offset,
                                   [mask_s.ap().ap[0], [0, NCH], [1, C]])
                    ev_e3 = kb.op("dve", lambda e, mv=ms_v: e.tensor_tensor(
                        out=wf3(WF["e2"], NCH, C, C), in0=wf3(WF["e1"], NCH, C, C),
                        in1=mv, op=ALU.add), waits=[ev_di], sig=True)
                    gso = G_SLOT[p]
                    ev_ds = kb.op("act", lambda e, go=gso: e.activation(
                        out=wb(go, NCH * C), in_=wf(WF["e2"], NCH * C),
                        func=AF.Exp, scale=-1.0),
                        waits=[ev_e3, ev_slot_pe], sig=True)
                    kk_v = wb3(WB["kkkq"] + kl * NCH * 256, NCH, 256, C)
                    zso = Z_SLOT[p]
                    ev_a1 = kb.op("dve", lambda e, kv=kk_v, go=gso, zo=zso:
                                  e.tensor_tensor(
                                      out=wb3(zo, NCH, C, C), in0=wb3(go, NCH, C, C),
                                      in1=kv, op=ALU.mult),
                                  waits=[ev_ds, ev_kkkq, ev_slot_pe], sig=True)
                    b_v = bcast3(gb_tmh, 32 + h32, NCH, 64, C)
                    ev_a = kb.op("dve", lambda e, bv=b_v, zo=zso: e.tensor_tensor(
                        out=wb3(zo, NCH, C, C), in0=wb3(zo, NCH, C, C),
                        in1=bv, op=ALU.mult), waits=[ev_a1], sig=True)
                    kq_v = wb3(WB["kkkq"] + kl * NCH * 256 + 128, NCH, 256, C)
                    kb.op("dve", lambda e, kv=kq_v, hh=hh: e.tensor_tensor(
                        out=wb3(WB["wt"] + hh * NCH * C, NCH, C, C),
                        in0=wb3(WB["di"] + hh * NCH * C, NCH, C, C),
                        in1=kv, op=ALU.mult), waits=[ev_di])
                    i_v = bass.AP(ident_b.ap().tensor, ident_b.ap().offset,
                                  [ident_b.ap().ap[0], [0, NCH], [1, C]])
                    ev_g = kb.op("dve", lambda e, iv=i_v, go=gso, zo=zso:
                                 e.tensor_tensor(
                                     out=wb3(go, NCH, C, C), in0=wb3(zo, NCH, C, C),
                                     in1=iv, op=ALU.add), waits=[ev_a], sig=True)
                    ev_z1n = kb.op("dve", lambda e, iv=i_v, zo=zso:
                                   e.scalar_tensor_tensor(
                                       out=wb3(zo, NCH, C, C),
                                       in0=wb3(zo, NCH, C, C),
                                       scalar=-1.0, in1=iv, op0=ALU.mult,
                                       op1=ALU.add), waits=[ev_g], sig=True)
                    ev_zslot[hh] = ev_z1n
                    ed_v = bass.AP(work_bf.ap().tensor,
                                   work_bf.ap().offset + WB["di"] + hh * NCH * C + 127,
                                   [work_bf.ap().ap[0], [C, NCH]])
                    ex_v = bass.AP(exps_t.ap().tensor, exps_t.ap().offset + h32,
                                   [exps_t.ap().ap[0], [32, NCH]])
                    nonlocal_lcc = kb.op("dve", lambda e, hh=hh, ev2=ed_v, xv=ex_v:
                                   e.tensor_tensor(
                                       out=bass.AP(work_f32.ap().tensor,
                                                   work_f32.ap().offset + WF["lcc"] + hh * NCH,
                                                   [work_f32.ap().ap[0], [1, NCH]]),
                                       in0=ev2, in1=xv, op=ALU.mult),
                                   waits=[ev_di, ev_exps, ev_bl], sig=True)
                    return nonlocal_lcc

                for hp in (0, 2):
                    for hh in (hp, hp + 1):
                        ev_lcc = p1_prep(hh)
                    xt_cur = [None, None]
                    ev_xt = [None, None]
                    for hh in (hp, hp + 1):
                        p = hh % 2
                        mml = []
                        for cc in range(NCH):
                            mml.append(lambda t, cc=cc, zo=Z_SLOT[p]: t.matmul(
                                pt.ap[:, cc * C:(cc + 1) * C], wb(zo + cc * C, C),
                                ident_b[:], is_transpose=True,
                                start=(cc == 0), stop=(cc == NCH - 1)))
                        mm_group(pt, mml, extra_waits=[ev_zslot[hh]])
                        ev_xt[p] = bank_read("act", pt,
                                             lambda e, d=XT_A[p]: e.copy(
                                                 wb(d, NCH * C), pt.ap[:, 0:NCH * C]),
                                             extra_waits=[ev_slot_pe])
                        xt_cur[p] = XT_A[p]
                    for step in range(2):
                        pevs = [None, None]
                        for hh in (hp, hp + 1):
                            p = hh % 2
                            bpn = PBANK[p]
                            mml = []
                            for cc in range(NCH):
                                mml.append(lambda t, cc=cc, b=bpn, c=xt_cur[p],
                                           go=G_SLOT[p]: t.matmul(
                                    b.ap[:, cc * C:(cc + 1) * C], wb(c + cc * C, C),
                                    wb(go + cc * C, C),
                                    start=(cc == 0), stop=(cc == NCH - 1)))
                            mm_group(bpn, mml, extra_waits=[ev_xt[p]])
                            pevs[p] = bpn
                        for hh in (hp, hp + 1):
                            p = hh % 2
                            bpn = pevs[p]
                            t_v = bass.AP(two_i.ap().tensor, two_i.ap().offset,
                                          [two_i.ap().ap[0], [0, NCH], [1, C]])
                            ev_zz = bank_read("dve", bpn,
                                              lambda e, b=bpn, tv=t_v, zo=Z_SLOT[p]:
                                              e.scalar_tensor_tensor(
                                                  out=wb3(zo, NCH, C, C),
                                                  in0=b.ap.rearrange(
                                                      "p (a b) -> p a b", a=NCH),
                                                  scalar=-1.0, in1=tv,
                                                  op0=ALU.mult, op1=ALU.add))
                            bq = XTBANK[p]
                            mml = []
                            for cc in range(NCH):
                                mml.append(lambda t, cc=cc, b=bq, c=xt_cur[p],
                                           zo=Z_SLOT[p]: t.matmul(
                                    b.ap[:, cc * C:(cc + 1) * C],
                                    wb(zo + cc * C, C), wb(c + cc * C, C),
                                    start=(cc == 0), stop=(cc == NCH - 1)))
                            mm_group(bq, mml, extra_waits=[ev_zz])
                            if step < 1:
                                dsto = XT_B[p] if xt_cur[p] == XT_A[p] else XT_A[p]
                            else:
                                dsto = WB["tt"] + hh * NCH * C
                            ev_xt[p] = bank_read("act", bq,
                                                 lambda e, b=bq, d=dsto: e.copy(
                                                     wb(d, NCH * C),
                                                     b.ap[:, 0:NCH * C]),
                                                 extra_waits=[tap_ev[0]])
                            xt_cur[p] = dsto
                    ev_slot_pe = kb.now("pe")
                ev_p1 = kb.now("dve")
                if ti == 0 and g == 0:
                    tap("tt0", lambda: wb(WB["tt"], 4 * NCH * C), [128, 4 * NCH * C],
                        BF16, [ev_p1])
                    tap("di0", lambda: wb(WB["di"], 4 * NCH * C), [128, 4 * NCH * C],
                        BF16, [ev_p1])

                prev_ks = prev_qs = None
                for cc in range(NCH):
                    ev_sb = kb.op("dve", lambda e, g=g: e.tensor_copy(
                        wb(WB["sbf"], 512), s_f32[:, g * 512:(g + 1) * 512]),
                        waits=[w for w in (prev_ks, prev_qs, carry["sbf_pe"]) if w],
                        sig=True)
                    bKS, bQS = pb[3], pb[4]
                    mml = []
                    for p in range(2):
                        mml.append(lambda t, p=p, cc=cc, b=bKS: t.matmul(
                            b.ap[:, p * 256:(p + 1) * 256],
                            wb(WB["kg"] + p * T + cc * C, C),
                            wb(WB["sbf"] + p * 256, 256),
                            start=(p == 0), stop=(p == 1)))
                    mm_group(bKS, mml, extra_waits=[ev_sb])
                    prev_ks = bKS.wev
                    mml = []
                    for p in range(2):
                        mml.append(lambda t, p=p, cc=cc, b=bQS: t.matmul(
                            b.ap[:, p * 256:(p + 1) * 256],
                            wb(WB["qg"] + p * T + cc * C, C),
                            wb(WB["sbf"] + p * 256, 256),
                            start=(p == 0), stop=(p == 1)))
                    mm_group(bQS, mml, extra_waits=[ev_sb])
                    prev_qs = bQS.wev
                    bl_v = bcast3(bl_t, cc * 32 + 4 * g - 32 * (g // 8), 4, 1, C)
                    bl_v = bcast3(bl_t, cc * 32 + (4 * g) % 32, 4, 1, C)
                    ev_u1 = bank_read("dve", bKS, lambda e, bv=bl_v, b=bKS:
                                      e.tensor_tensor(
                                          out=wf3(WF["u1"], 4, C, C),
                                          in0=b.ap.rearrange("p (a b) -> p a b", a=4),
                                          in1=bv, op=ALU.mult),
                                      extra_waits=[ev_bl])
                    vt_v = wb3(WB["vt"] + cc * C, 4, NCH * C, C)
                    be_v = bcast3(gb_tmh, cc * 64 + 32 + (4 * g) % 32, 4, 1, C)
                    ev_u2 = kb.op("dve", lambda e, vv=vt_v, bv=be_v: e.tensor_tensor(
                        out=wf3(WF["g1"], 4, C, C), in0=vv, in1=bv, op=ALU.mult),
                        waits=[ev_vt, ev_gb], sig=True)
                    ev_u = kb.op("dve", lambda e: e.tensor_tensor(
                        out=wb(WB["u_sb"], 512), in0=wf(WF["g1"], 512),
                        in1=wf(WF["u1"], 512), op=ALU.subtract),
                        waits=[ev_u1, ev_u2], sig=True)
                    bD = pb[5]
                    mml = []
                    for hh in range(4):
                        mml.append(lambda t, hh=hh, cc=cc, b=bD: t.matmul(
                            b.ap[:, hh * 128:(hh + 1) * 128],
                            wb(WB["tt"] + hh * NCH * C + cc * C, C),
                            wb(WB["u_sb"] + hh * 128, 128),
                            start=(hh == 0), stop=(hh == 3)))
                    mm_group(bD, mml, extra_waits=[ev_u, ev_p1])
                    ev_dl = bank_read("act", bD, lambda e, b=bD: e.copy(
                        wb(WB["dl"], 512), b.ap[:, 0:512]))
                    ed_v = bass.AP(work_bf.ap().tensor,
                                   work_bf.ap().offset + WB["di"] + cc * C + 127,
                                   [work_bf.ap().ap[0], [NCH * C, 4], [0, C]])
                    ev_dl2 = kb.op("dve", lambda e, ev2=ed_v, b=bD: e.tensor_tensor(
                        out=wb(WB["dl2"], 512),
                        in0=b.ap.rearrange("p (a b) -> p a b", a=4),
                        in1=ev2, op=ALU.mult), waits=[ev_dl], sig=True)
                    bD.rev = ev_dl2
                    bO = pb[3]
                    mml = []
                    for hh in range(4):
                        mml.append(lambda t, hh=hh, cc=cc, b=bO: t.matmul(
                            b.ap[:, hh * 128:(hh + 1) * 128],
                            wb(WB["wt"] + hh * NCH * C + cc * C, C),
                            wb(WB["dl"] + hh * 128, 128),
                            start=(hh == 0), stop=(hh == 3)))
                    mm_group(bO, mml, extra_waits=[ev_dl])
                    prev_ks = bO.wev
                    eg_v = bcast3(exps_t, cc * 32 + (4 * g) % 32, 4, 1, C)
                    ev_g1 = bank_read("dve", bQS, lambda e, gv=eg_v, b=bQS:
                                      e.tensor_tensor(
                                          out=wf3(WF["g1"], 4, C, C),
                                          in0=b.ap.rearrange("p (a b) -> p a b", a=4),
                                          in1=gv, op=ALU.mult))
                    ev_o = bank_read("dve", bO, lambda e, b=bO: e.tensor_tensor(
                        out=wf(WF["osb"], 512), in0=b.ap[:, 0:512],
                        in1=wf(WF["g1"], 512), op=ALU.add), extra_waits=[ev_g1])
                    ev_ms = None
                    for hh in range(4):
                        ev_ms = kb.op("act", lambda e, hh=hh: e.activation(
                            out=wf(WF["u1"] + hh * C, C),
                            in_=wf(WF["osb"] + hh * C, C),
                            func=AF.Square,
                            accum_out=wf(WF["ms"] + hh, 1)),
                            waits=[ev_o] if hh == 0 else (), sig=(hh == 3))
                    ev_r1 = kb.op("act", lambda e: e.activation(
                        out=wf(WF["rstd"], 4), in_=wf(WF["ms"], 4), func=AF.Ln,
                        scale=1.0 / DV, bias=eps_rms[:]), waits=[ev_ms], sig=True)
                    ev_r2 = kb.op("act", lambda e: e.activation(
                        out=wf(WF["rstd"], 4), in_=wf(WF["rstd"], 4), func=AF.Exp,
                        scale=-0.5), waits=[ev_r1], sig=True)
                    rs_v = bcast3(work_f32, WF["rstd"], 4, 1, C)
                    ev_gt1 = kb.op("dve", lambda e, rv=rs_v: e.tensor_tensor(
                        out=wf3(WF["g1"], 4, C, C), in0=wf3(WF["osb"], 4, C, C),
                        in1=rv, op=ALU.mult), waits=[ev_r2], sig=True)
                    ev_gt = kb.op("dve", lambda e, cc=cc: e.tensor_tensor(
                        out=wb(WB["gtd"], 512), in0=wf(WF["g1"], 512),
                        in1=wb(WB["zg"] + cc * 512, 512), op=ALU.mult),
                        waits=[ev_gt1, ev_zg], sig=True)
                    carry["zg_dve"] = ev_gt
                    mml = []
                    for hh in range(4):
                        mml.append(lambda t, hh=hh: t.matmul(
                            pt.ap[:, hh * 128:(hh + 1) * 128],
                            wb(WB["gtd"] + hh * 128, 128), ident_b[:],
                            is_transpose=True, start=(hh == 0), stop=(hh == 3)))
                    mm_group(pt, mml, extra_waits=[ev_gt])
                    aw = work_bf.ap()
                    dstg = bass.AP(aw.tensor, aw.offset + WB["gated"] + cc * C,
                                   [aw.ap[0], [T, 4], [1, C]])
                    bank_read("act", pt, lambda e, d=dstg: e.copy(
                        d, pt.ap[:, 0:512].rearrange("p (a b) -> p a b", a=4)),
                        extra_waits=[carry["gated_pe"], tap_ev[0]])
                    bS = pb[4]
                    mml = []
                    for p in range(2):
                        mml.append(lambda t, p=p, cc=cc, b=bS: t.matmul(
                            b.ap[:, p * 256:(p + 1) * 256],
                            wb(WB["kt"] + (p * NCH + cc) * C, C),
                            wb(WB["dl2"] + p * 256, 256),
                            start=(p == 0), stop=(p == 1)))
                    mm_group(bS, mml, extra_waits=[ev_dl2, ev_kt])
                    prev_qs = bS.wev
                    for hh in range(4):
                        lcc_ap = bass.AP(work_f32.ap().tensor,
                                         work_f32.ap().offset + WF["lcc"] + hh * NCH + cc,
                                         [work_f32.ap().ap[0], [1, 1]])
                        bank_read("dve", bS, lambda e, hh=hh, g=g, sc=lcc_ap:
                                  e.scalar_tensor_tensor(
                                      out=s_f32[:, g * 512 + hh * 128:g * 512 + (hh + 1) * 128],
                                      in0=s_f32[:, g * 512 + hh * 128:g * 512 + (hh + 1) * 128],
                                      scalar=sc,
                                      in1=bS.ap[:, hh * 128:(hh + 1) * 128],
                                      op0=ALU.mult, op1=ALU.add),
                                  extra_waits=[ev_lcc])
                carry["sbf_pe"] = kb.now("pe")
                if ti == 0 and g == 0:
                    ev_gg = kb.now("dve")
                    tap("gated0", lambda: wb(WB["gated"], 4 * T), [128, 4 * T], BF16,
                        [ev_gg])
                    tap("s0", lambda: s_f32[:, 0:512], [128, 512], F32, [ev_gg])

                ev_gated_all = kb.now("dve")
                for oc in range(NKC):
                    bnko = pb[oc % 2]
                    mml = []
                    for kc in range(4):
                        mml.append(lambda t, kc=kc, oc=oc, b=bnko, ow=o_wo: t.matmul(
                            b.ap[:, 0:T],
                            ring[:, ow + kc * 2048 + oc * 128:
                                 ow + kc * 2048 + oc * 128 + 128],
                            wb(WB["gated"] + kc * T, T),
                            start=(kc == 0), stop=(kc == 3)))
                    mm_group(bnko, mml, extra_waits=[ev_wo, ev_gated_all])
                    if g == 0:
                        s_v = bass.AP(mbsb.ap().tensor, mbsb.ap().offset + 128,
                                      [mbsb.ap().ap[0], [256, NCH], [1, 128]])
                        m_v = bass.AP(mbsb.ap().tensor, mbsb.ap().offset,
                                      [mbsb.ap().ap[0], [256, NCH], [1, 128]])
                        an = xn_fm.ap()
                        xn_v = bass.AP(an.tensor, an.offset + oc * T,
                                       [an.ap[0], [C, NCH], [1, C]])
                        kb.op("dve", lambda e, xv=xn_v, sv=s_v: e.tensor_tensor(
                            out=wf3(WF["u1"], NCH, C, C), in0=xv, in1=sv,
                            op=ALU.mult), waits=[ev_mbsb, ev_xn])
                        kb.op("dve", lambda e, mv=m_v: e.tensor_tensor(
                            out=wf3(WF["u1"], NCH, C, C), in0=wf3(WF["u1"], NCH, C, C),
                            in1=mv, op=ALU.add))
                        bank_read("dve", bnko, lambda e, oc=oc, b=bnko: e.tensor_tensor(
                            out=attn_bf[:, oc * T:(oc + 1) * T], in0=b.ap[:, 0:T],
                            in1=wf(WF["u1"], T), op=ALU.add))
                    else:
                        bank_read("dve", bnko, lambda e, oc=oc, b=bnko: e.tensor_tensor(
                            out=attn_bf[:, oc * T:(oc + 1) * T], in0=b.ap[:, 0:T],
                            in1=attn_bf[:, oc * T:(oc + 1) * T], op=ALU.add))
                carry["gated_pe"] = kb.now("pe")
                ring_release(ent_wo, kb.now("pe"))

            ev_attn = kb.now("dve")
            if ti == 0:
                tap("attn0", lambda: attn_bf[:, 0:NKC * T], [128, NKC * T], BF16,
                    [ev_attn])
            if stop_after == "attn":
                break

            ev_hid = None
            prev_hid_pe = None
            for hc in range(NHC):
                o1, e_q1, ent1 = ring_push(f"q1_{hc}", SEG_Q12)
                if ti == 0 and hc == 0:
                    tap("ring_q1", lambda o=o1: ring[:, o:o + SEG_Q12],
                        [128, SEG_Q12], BF16, [e_q1])
                o2, e_q2, ent2 = ring_push(f"q2_{hc}", SEG_Q12)
                b1, b2 = pb[0], pb[1]
                mml = []
                for kc in range(NKC):
                    mml.append(lambda t, kc=kc, o=o1, b=b1: t.matmul(
                        b.ap[:, 0:T], ring[:, o + kc * 128:o + (kc + 1) * 128],
                        xn_fm[:, kc * T:(kc + 1) * T],
                        start=(kc == 0), stop=(kc == NKC - 1)))
                mm_group(b1, mml, extra_waits=[e_q1, ev_xn, ev_attn if hc == 0 else None])
                mml = []
                for kc in range(NKC):
                    mml.append(lambda t, kc=kc, o=o2, b=b2: t.matmul(
                        b.ap[:, 0:T], ring[:, o + kc * 128:o + (kc + 1) * 128],
                        xn_fm[:, kc * T:(kc + 1) * T],
                        start=(kc == 0), stop=(kc == NKC - 1)))
                mm_group(b2, mml, extra_waits=[e_q2])
                ev_sg = bank_read("act", b1, lambda e, b=b1: e.activation(
                    out=wb(WB_SIG, T), in_=b.ap[:, 0:T], func=AF.Sigmoid),
                    extra_waits=[ev_hid])
                if ti == 0 and hc == 0:
                    tap("sig0", lambda: wb(WB_SIG, T), [128, T], BF16, [ev_sg])
                ev_hid = bank_read("dve", b2, lambda e, hc=hc, b=b2: e.tensor_tensor(
                    out=wb(WB_HID + hc * T, T), in0=b.ap[:, 0:T],
                    in1=wb(WB_SIG, T), op=ALU.mult),
                    extra_waits=[ev_sg, tap_ev[0] if hc == 0 else None])
                ring_release(ent1, kb.now("pe"))
                ring_release(ent2, kb.now("pe"))
            if ti == 0:
                tap("hid0", lambda: wb(WB_HID, 4 * T), [128, 4 * T], BF16, [ev_hid])

            if ti == 0:
                fs_dy = {0: None, 1: None}
            for oc in range(NKC):
                o3, e_q3, ent3 = ring_push(f"q3_{oc}", SEG_Q3)
                bnkf = pb[oc % 2]
                mml = []
                for kc in range(NHC):
                    mml.append(lambda t, kc=kc, o=o3, b=bnkf: t.matmul(
                        b.ap[:, 0:T], ring[:, o + kc * 128:o + (kc + 1) * 128],
                        wb(WB_HID + kc * T, T),
                        start=(kc == 0), stop=(kc == NHC - 1)))
                mm_group(bnkf, mml, extra_waits=[e_q3, ev_hid])
                fo = (oc % 2) * T
                ev_f = bank_read("dve", bnkf, lambda e, oc=oc, b=bnkf, fo=fo:
                                 e.tensor_tensor(
                                     out=wf(WF["fsa"] + fo, T), in0=b.ap[:, 0:T],
                                     in1=attn_bf[:, oc * T:(oc + 1) * T], op=ALU.add),
                                 extra_waits=[fs_dy[oc % 2]])
                bt = pb[2 + oc % 2]
                mml = []
                for tc in range(NCH):
                    mml.append(lambda t, tc=tc, b=bt, fo=fo: t.matmul(
                        b.ap[:, tc * 128:(tc + 1) * 128],
                        wf(WF["fsa"] + fo + tc * 128, 128), ident_f[:],
                        is_transpose=True, start=(tc == 0), stop=(tc == NCH - 1)))
                mm_group(bt, mml, extra_waits=[ev_f])
                ev_fb = bank_read("dve", bt, lambda e, b=bt, fo=fo: e.tensor_copy(
                    wf(WF["fsb"] + fo, T), b.ap[:, 0:512]))
                dsty = bass.AP(y_h.tensor, y_h.offset + ti * T * D + oc * 128,
                               [[D, 128], [128 * D, NCH], [1, 128]])
                srcy = bass.AP(work_f32.ap().tensor,
                               work_f32.ap().offset + WF["fsb"] + fo,
                               [work_f32.ap().ap[0], [128, NCH], [1, 128]])
                fs_dy[oc % 2] = kb.op("sp", lambda e, d=dsty, s=srcy:
                                      e.dma_start(out=d, in_=s),
                                      waits=[ev_fb], dma="dy")
                ring_release(ent3, kb.now("pe"))

            if stop_after == "tile0":
                break

        kb.wait_only("sp", [Ev("dy", kb.semtot.get("dy", 0))])

        kb.check_deadlock()

        @block.tensor
        def _(eng):
            kb.emit("pe", eng, sems)

        @block.vector
        def _(eng):
            kb.emit("dve", eng, sems)

        @block.scalar
        def _(eng):
            kb.emit("act", eng, sems)

        @block.gpsimd
        def _(eng):
            kb.emit("pool", eng, sems)

        @block.sync
        def _(eng):
            kb.emit("sp", eng, sems)

    return nc, tap_out


def kernel(L0_attn_hidden_states, L0_attn_w_qkv, L0_attn_w_z, L0_attn_w_b,
           L0_attn_w_a, L0_attn_w_out, L0_attn_conv_w, L0_attn_dt_bias,
           L0_attn_A_log, L0_attn_norm_weight, L0_ffn_w1, L0_ffn_w2, L0_ffn_w3):
    x = np.ascontiguousarray(np.asarray(L0_attn_hidden_states, dtype=np.float32))
    common = {
        "w_qkv": np.ascontiguousarray(L0_attn_w_qkv, dtype=np.float32),
        "w_z": np.ascontiguousarray(L0_attn_w_z, dtype=np.float32),
        "w_b": np.ascontiguousarray(L0_attn_w_b, dtype=np.float32),
        "w_a": np.ascontiguousarray(L0_attn_w_a, dtype=np.float32),
        "w_out": np.ascontiguousarray(L0_attn_w_out, dtype=np.float32),
        "conv_w": np.ascontiguousarray(L0_attn_conv_w, dtype=np.float32),
        "dt_bias": np.ascontiguousarray(L0_attn_dt_bias, dtype=np.float32),
        "A_log": np.ascontiguousarray(L0_attn_A_log, dtype=np.float32),
        "norm_weight": np.ascontiguousarray(L0_attn_norm_weight, dtype=np.float32),
        "w1": np.ascontiguousarray(L0_ffn_w1, dtype=np.float32),
        "w2": np.ascontiguousarray(L0_ffn_w2, dtype=np.float32),
        "w3": np.ascontiguousarray(L0_ffn_w3, dtype=np.float32),
    }
    nc, _ = build_core()
    in_maps = [dict(common, x=np.ascontiguousarray(x[b])) for b in range(B)]
    res = run_bass_kernel_spmd(nc, in_maps, core_ids=list(range(B)))
    return np.stack([res.results[b]["y"] for b in range(B)], axis=0)


# version 12
# speedup vs baseline: 1.0311x; 1.0125x over previous
import numpy as np
from contextlib import ExitStack

import concourse.bass as bass
import concourse.mybir as mybir
from concourse.bass_utils import run_bass_kernel_spmd

F32 = mybir.dt.float32
BF16 = mybir.dt.bfloat16
AF = mybir.ActivationFunctionType
ALU = mybir.AluOpType

B, S, D = 8, 4096, 2048
HK, HV = 16, 32
DK = DV = 128
KDIM, VDIM = HK * DK, HV * DV
CONV_DIM, KSIZE = 2 * KDIM + VDIM, 4
HFF = 5632
NHC = HFF // 128
T = 512
NT = S // T
C = 128
NCH = T // C
NGRP = 8
NKC = D // 128
LNEPS, L2EPS, RMSEPS, FFEPS = 1e-5, 1e-6, 1e-6, 1e-8
RING_COLS = 18176

ENGINES = ("pe", "dve", "act", "pool", "sp")
ENG_SEM = {e: "s_" + e for e in ENGINES}


class Ev:
    __slots__ = ("sem", "val")

    def __init__(self, sem, val):
        self.sem, self.val = sem, val


class KB:
    def __init__(self):
        self.ops = {e: [] for e in ENGINES}
        self.semtot = {}
        self.waited = {e: {} for e in ENGINES}

    def now(self, eng):
        return Ev(ENG_SEM[eng], self.semtot.get(ENG_SEM[eng], 0))

    def _wait(self, eng, ev):
        if ev is None or ev.val <= 0:
            return
        cur = self.waited[eng].get(ev.sem, 0)
        if ev.val > cur:
            self.ops[eng].append(("w", ev.sem, ev.val))
            self.waited[eng][ev.sem] = ev.val

    def op(self, eng, fn, waits=(), sig=False, dma=None):
        for w in waits:
            self._wait(eng, w)
        sem = inc = None
        if dma is not None:
            sem, inc = dma, 16
        elif sig:
            sem, inc = ENG_SEM[eng], 1
        if sem is not None:
            self.semtot[sem] = self.semtot.get(sem, 0) + inc
            self.ops[eng].append(("c", fn, sem, inc))
            return Ev(sem, self.semtot[sem])
        self.ops[eng].append(("c", fn, None, 0))
        return None

    def wait_only(self, eng, evs):
        for w in evs:
            self._wait(eng, w)

    def check_deadlock(self):
        ptr = {e: 0 for e in ENGINES}
        cnt = {}
        progress = True
        while progress:
            progress = False
            for e in ENGINES:
                while ptr[e] < len(self.ops[e]):
                    it = self.ops[e][ptr[e]]
                    if it[0] == "w" and cnt.get(it[1], 0) < it[2]:
                        break
                    if it[0] == "c" and it[2] is not None:
                        cnt[it[2]] = cnt.get(it[2], 0) + it[3]
                    ptr[e] += 1
                    progress = True
        stuck = {e: ptr[e] for e in ENGINES if ptr[e] < len(self.ops[e])}
        if stuck:
            detail = {}
            for e in stuck:
                it = self.ops[e][ptr[e]]
                detail[e] = (f"op {ptr[e]}/{len(self.ops[e])} waiting "
                             f"{it[1]}>={it[2]} (now {cnt.get(it[1], 0)})")
            raise RuntimeError(f"build-time deadlock: {detail}")

    def emit(self, eng, engobj, sems):
        for item in self.ops[eng]:
            if item[0] == "w":
                engobj.wait_ge(sems[item[1]], item[2])
            else:
                inst = item[1](engobj)
                if item[2] is not None:
                    inst.then_inc(sems[item[2]], item[3])


class Bank:
    def __init__(self, ap):
        self.ap = ap
        self.wev = None
        self.rev = None


def build_core(taps=None, stop_after="full"):
    taps = taps or []
    tap_out = []

    nc = bass.Bass("TRN2", target_bir_lowering=False, debug=False)

    x_h = nc.dram_tensor("x", [S, D], F32, kind="ExternalInput").ap()
    wqkv_h = nc.dram_tensor("w_qkv", [D, CONV_DIM], F32, kind="ExternalInput").ap()
    wz_h = nc.dram_tensor("w_z", [D, VDIM], F32, kind="ExternalInput").ap()
    wb_h = nc.dram_tensor("w_b", [D, HV], F32, kind="ExternalInput").ap()
    wa_h = nc.dram_tensor("w_a", [D, HV], F32, kind="ExternalInput").ap()
    wout_h = nc.dram_tensor("w_out", [VDIM, D], F32, kind="ExternalInput").ap()
    convw_h = nc.dram_tensor("conv_w", [CONV_DIM, KSIZE], F32, kind="ExternalInput").ap()
    dtb_h = nc.dram_tensor("dt_bias", [HV], F32, kind="ExternalInput").ap()
    alog_h = nc.dram_tensor("A_log", [HV], F32, kind="ExternalInput").ap()
    nw_h = nc.dram_tensor("norm_weight", [DV], F32, kind="ExternalInput").ap()
    w1_h = nc.dram_tensor("w1", [HFF, D], F32, kind="ExternalInput").ap()
    w2_h = nc.dram_tensor("w2", [HFF, D], F32, kind="ExternalInput").ap()
    w3_h = nc.dram_tensor("w3", [D, HFF], F32, kind="ExternalInput").ap()
    y_h = nc.dram_tensor("y", [S, D], F32, kind="ExternalOutput").ap()

    SEG_QQKK = 16 * 512
    SEG_WOUT = 4 * 2048
    SEG_Q12 = 16 * 128
    SEG_Q3 = NHC * 128
    tape_cols = 0
    seg_off = {}

    def add_seg(name, cols):
        nonlocal tape_cols
        seg_off[name] = tape_cols
        tape_cols += cols

    for g in range(NGRP):
        add_seg(f"qqkk{g}", SEG_QQKK)
        add_seg(f"vvvv{g}", SEG_QQKK)
        add_seg(f"zzzz{g}", SEG_QQKK)
        add_seg(f"wout{g}", SEG_WOUT)
    for hc in range(NHC):
        add_seg(f"q1_{hc}", SEG_Q12)
        add_seg(f"q2_{hc}", SEG_Q12)
    for oc in range(NKC):
        add_seg(f"q3_{oc}", SEG_Q3)
    wtape = nc.dram_tensor("wtape", [128, tape_cols], BF16, kind="Internal").ap()

    kb = KB()

    with ExitStack() as ctx:
        def sb(name, shape, dt):
            return ctx.enter_context(nc.sbuf_tensor(name, shape, dt))

        xn_fm = sb("xn_fm", [128, NKC * T], BF16)
        attn_bf = sb("attn_bf", [128, NKC * T], BF16)
        x_tm = sb("x_tm", [128, 2 * NCH * D], BF16)
        work_bf = sb("work_bf", [128, 23296], BF16)
        work_f32 = sb("work_f32", [128, 6720], F32)
        ring = sb("ring", [128, RING_COLS], BF16)
        s_f32 = sb("s_f32", [128, HV * DV], F32)
        wba = sb("wba", [128, NKC * 64], BF16)
        conv_w = sb("conv_w_sb", [128, 64 * KSIZE], F32)
        conv_hist = sb("conv_hist", [128, 64 * (KSIZE - 1)], BF16)
        ident_b = sb("ident_b", [128, 128], BF16)
        ident_f = sb("ident_f", [128, 128], F32)
        two_i = sb("two_i", [128, 128], BF16)
        mask_i = sb("mask_i", [128, 128], F32)
        mask_s = sb("mask_s", [128, 128], F32)
        sel4 = sb("sel4", [4, 4 * 128], F32)
        ones_cb = sb("ones_cb", [128, 1], BF16)
        ones_rb = sb("ones_rb", [1, 128], BF16)
        ones_rf = sb("ones_rf", [1, 128], F32)
        eps_ln = sb("eps_ln", [128, 1], F32)
        eps_l2 = sb("eps_l2", [128, 1], F32)
        eps_rms = sb("eps_rms", [128, 1], F32)
        qsc_b = sb("qsc_b", [128, 1], F32)
        zero_c = sb("zero_c", [128, 1], F32)
        one_c = sb("one_c", [128, 1], F32)
        nw_c = sb("nw_c", [128, 1], F32)
        dtb_c = sb("dtb_c", [64, 1], F32)
        nega_c = sb("nega_c", [64, 1], F32)
        ba_sb = sb("ba_sb", [64, 3 * T], F32)
        gb_tmh = sb("gb_tmh", [128, NCH * 64], F32)
        exps_t = sb("exps_t", [128, NCH * 32], F32)
        bl_t = sb("bl_t", [128, NCH * 32], F32)
        st4 = sb("st4", [128, 16], F32)
        mrow = sb("mrow", [4, 256], F32)
        mbsb = sb("mbsb", [128, NCH * 256], BF16)
        gam_sb = sb("gam_sb", [128, 4], F32)
        nth_sb = sb("nth_sb", [128, 4], F32)
        absc = sb("absc", [128, NHC], F32)
        gred = sb("gred", [128, 128], F32)

        WB = {}
        _o = 0
        for nm, w in [("qg", 2 * T), ("kg", 2 * T), ("vg", 4 * T), ("zg", 4 * T),
                      ("kkkq", 2 * NCH * 256), ("di", 4 * NCH * C),
                      ("wt", 4 * NCH * C), ("tt", 4 * NCH * C), ("xt0", NCH * C),
                      ("vt", 4 * NCH * C), ("kt", 2 * NCH * C),
                      ("at", NCH * C), ("g_nw", NCH * C), ("z1", NCH * C),
                      ("u_sb", 4 * C), ("gtd", 4 * C), ("dl", 4 * C), ("dl2", 4 * C),
                      ("sbf", 4 * C)]:
            WB[nm] = _o
            _o += w
        assert _o <= 23296, _o
        WB["gated"] = WB["kkkq"]
        WB["xt1"] = WB["at"]
        WB["qn"] = WB["u_sb"]
        WB["sqb"] = WB["gtd"]
        WB["cvo"] = WB["dl"]
        WB["mxp"] = WB["dl2"]
        WB_HID = 0
        WB_SIG = 22528
        assert NHC * T <= 22528 + 512

        PRE_Q = 0
        PRE_Q2 = 8192
        PRE_T = 16384
        WF = {}
        _o = 0
        for nm, w in [("e1", NCH * C), ("e2", NCH * C), ("osb", 4 * C), ("g1", 4 * C),
                      ("u1", 4 * C), ("sq", D), ("fsa", 2 * T), ("fsb", 2 * T),
                      ("ms", 8), ("rstd", 8), ("lcc", 16)]:
            WF[nm] = _o
            _o += w
        assert _o <= 6720, _o
        PRE_SQ = 0

        pb = [Bank(ctx.enter_context(nc.psum_tensor(f"pb{i}", [128, 512], F32)).ap())
              for i in range(7)]
        pt = Bank(ctx.enter_context(nc.psum_tensor("pt7", [128, 1024], BF16)).ap())

        sem_names = list(ENG_SEM.values()) + ["dw", "dx", "dy", "dpre"]
        sems = {n: ctx.enter_context(nc.semaphore(n)) for n in sem_names}
        block = ctx.enter_context(nc.Block())

        def mm_group(bank, mm_list, extra_waits=()):
            waits = [w for w in extra_waits if w is not None]
            if bank.rev is not None:
                waits.append(bank.rev)
            n = len(mm_list)
            for i, fn in enumerate(mm_list):
                if i < n - 1:
                    kb.op("pe", fn, waits=waits if i == 0 else ())
                else:
                    bank.wev = kb.op("pe", fn, waits=waits if i == 0 else (), sig=True)
            return bank.wev

        def bank_read(eng, bank, fn, extra_waits=()):
            waits = [bank.wev] + [w for w in extra_waits if w is not None]
            ev = kb.op(eng, fn, waits=waits, sig=True)
            bank.rev = ev
            return ev

        ring_state = {"cursor": 0, "occ": []}

        def ring_push(seg_name, cols):
            cur = ring_state["cursor"]
            if cur + cols > RING_COLS:
                cur = 0
            start, end = cur, cur + cols
            waits, keep = [], []
            for ent in ring_state["occ"]:
                if ent[0] < end and ent[1] > start:
                    assert ent[2] is not None, (
                        f"ring overlap with unreleased segment {ent} pushing "
                        f"{seg_name} at {start}:{end}")
                    waits.append(ent[2])
                else:
                    keep.append(ent)
            ring_state["occ"] = keep
            ent = [start, end, None]
            ring_state["occ"].append(ent)
            toff = seg_off[seg_name]
            ev = kb.op("sp",
                       lambda e, a=toff, n=cols, c=start: e.dma_start(
                           out=ring[:, c:c + n], in_=wtape[:, a:a + n]),
                       waits=waits, dma="dw")
            ring_state["cursor"] = end
            return start, ev, ent

        def ring_release(ent, ev):
            ent[2] = ev

        tap_ev = [None]

        def tap(name, ap_fn, shape, dt, waits):
            if name in taps:
                th = nc.dram_tensor("tap_" + name, shape, dt, kind="ExternalOutput").ap()
                tap_out.append("tap_" + name)
                tap_ev[0] = kb.op("sp", lambda e: e.dma_start(out=th, in_=ap_fn()),
                                  waits=waits, dma="dy")

        def wb(off, cols):
            return work_bf[:, off:off + cols]

        def wf(off, cols):
            return work_f32[:, off:off + cols]

        def wb3(off, n, step, cols):
            a = work_bf.ap()
            return bass.AP(a.tensor, a.offset + off, [a.ap[0], [step, n], [1, cols]])

        def wf3(off, n, step, cols):
            a = work_f32.ap()
            return bass.AP(a.tensor, a.offset + off, [a.ap[0], [step, n], [1, cols]])

        def bcast3(tensor, off, n, step, cols):
            a = tensor.ap()
            return bass.AP(a.tensor, a.offset + off, [a.ap[0], [step, n], [0, cols]])

        def aff(out_ap, fill, pattern, cm, base=0, cmp=ALU.not_equal):
            return lambda e: e.affine_select(out=out_ap, in_=out_ap, compare_op=cmp,
                                             fill=fill, base=base, pattern=pattern,
                                             channel_multiplier=cm)

        kb.op("pool", lambda e: e.memset(ident_b[:], 0.0))
        kb.op("pool", aff(ident_b[:], 1.0, [[-1, 128]], 1))
        kb.op("pool", lambda e: e.memset(ident_f[:], 0.0))
        kb.op("pool", aff(ident_f[:], 1.0, [[-1, 128]], 1))
        kb.op("pool", lambda e: e.memset(two_i[:], 0.0))
        kb.op("pool", aff(two_i[:], 2.0, [[-1, 128]], 1))
        kb.op("pool", lambda e: e.memset(mask_i[:], 0.0))
        kb.op("pool", aff(mask_i[:], -1e9, [[1, 128]], -1, cmp=ALU.is_ge))
        kb.op("pool", lambda e: e.memset(mask_s[:], 0.0))
        kb.op("pool", aff(mask_s[:], 1e9, [[-1, 128]], 1, cmp=ALU.is_gt))
        kb.op("pool", lambda e: e.memset(sel4[:], 0.0))
        kb.op("pool", aff(sel4[:], 1.0, [[-1, 4], [0, 128]], 1))
        for t_, v_ in ((ones_cb, 1.0), (ones_rb, 1.0), (ones_rf, 1.0),
                       (eps_ln, LNEPS), (eps_l2, L2EPS), (eps_rms, RMSEPS),
                       (qsc_b, float(np.log(DK ** -0.5))), (zero_c, 0.0),
                       (one_c, 1.0)):
            kb.op("pool", lambda e, t=t_, v=v_: e.memset(t[:], v))
        kb.op("pool", lambda e: e.memset(conv_hist[:], 0.0))
        kb.op("pool", lambda e: e.memset(s_f32[:], 0.0))
        ev_nw = kb.op("pool", lambda e: e.dma_start(out=nw_c[:], in_=nw_h[:, None]),
                      dma="dpre")
        kb.op("pool", lambda e: e.dma_start(out=dtb_c[32:64, :], in_=dtb_h[:, None]),
              dma="dpre")
        ev_al = kb.op("pool", lambda e: e.dma_start(out=nega_c[32:64, :],
                                                    in_=alog_h[:, None]), dma="dpre")
        ev_cw = kb.op("pool",
                      lambda e: e.dma_start(
                          out=conv_w.ap().rearrange("p (o j) -> p o j", o=64),
                          in_=convw_h.rearrange("(o p) j -> p o j", p=128)),
                      dma="dpre")
        ev_consts = kb.op("pool", lambda e: e.memset(absc[:], 0.0), sig=True)

        ev_na1 = kb.op("act", lambda e: e.activation(out=nega_c[32:64, :],
                                                     in_=nega_c[32:64, :], func=AF.Exp),
                       waits=[ev_al], sig=True)
        ev_nega = kb.op("dve", lambda e: e.tensor_scalar_mul(
            nega_c[32:64, :], nega_c[32:64, :], -1.0), waits=[ev_na1], sig=True)

        for g in range(NGRP if stop_after != "consts" else 0):
            so = seg_off[f"qqkk{g}"]
            for half, c0 in ((0, 2 * g * 128), (1, (16 + 2 * g) * 128)):
                src = bass.AP(wqkv_h.tensor, wqkv_h.offset + c0,
                              [[128 * CONV_DIM, NKC], [CONV_DIM, 128], [1, 256]])
                dst = bass.AP(wtape.tensor, wtape.offset + so + half * 256,
                              [[512, NKC], [tape_cols, 128], [1, 256]])
                kb.op("pool", lambda e, s=src, d=dst: e.dma_start(out=d, in_=s),
                      dma="dpre")
            so = seg_off[f"vvvv{g}"]
            src = bass.AP(wqkv_h.tensor, wqkv_h.offset + (32 + 4 * g) * 128,
                          [[128 * CONV_DIM, NKC], [CONV_DIM, 128], [1, 512]])
            dst = bass.AP(wtape.tensor, wtape.offset + so,
                          [[512, NKC], [tape_cols, 128], [1, 512]])
            kb.op("pool", lambda e, s=src, d=dst: e.dma_start(out=d, in_=s), dma="dpre")
            so = seg_off[f"zzzz{g}"]
            src = bass.AP(wz_h.tensor, wz_h.offset + g * 512,
                          [[128 * VDIM, NKC], [VDIM, 128], [1, 512]])
            dst = bass.AP(wtape.tensor, wtape.offset + so,
                          [[512, NKC], [tape_cols, 128], [1, 512]])
            kb.op("pool", lambda e, s=src, d=dst: e.dma_start(out=d, in_=s), dma="dpre")
        ev_wba = None
        for (wsrc, half) in ((wb_h, 0), (wa_h, 1)):
            src = bass.AP(wsrc.tensor, wsrc.offset,
                          [[HV, 128], [128 * HV, NKC], [1, 32]])
            a = wba.ap()
            dst = bass.AP(a.tensor, a.offset + half * 32,
                          [a.ap[0], [64, NKC], [1, 32]])
            ev_wba = kb.op("pool", lambda e, s=src, d=dst: e.dma_start(out=d, in_=s),
                           dma="dpre")

        ev_prev_tape = None
        for g in range(NGRP if stop_after != "consts" else 0):
            for kc in range(4):
                r0 = g * 512 + kc * 128
                src = bass.AP(wout_h.tensor, wout_h.offset + r0 * D, [[D, 128], [1, D]])
                evl = kb.op("pool", lambda e, s=src: e.dma_start(out=wb(PRE_Q, D), in_=s),
                            waits=[ev_prev_tape] if ev_prev_tape else [], dma="dpre")
                ev_sc = kb.op("dve", lambda e: e.tensor_scalar_mul(
                    wb(PRE_Q, D), wb(PRE_Q, D), nw_c[:]), waits=[evl, ev_nw], sig=True)
                so = seg_off[f"wout{g}"] + kc * 2048
                ev_prev_tape = kb.op("sp", lambda e, a=so: e.dma_start(
                    out=wtape[:, a:a + 2048], in_=wb(PRE_Q, D)),
                    waits=[ev_sc], dma="dw")

        ev_gam_row = {}
        _gl = ((1, w1_h, NHC, D), (2, w2_h, NHC, D), (3, w3_h, NKC, HFF)) \
            if stop_after not in ("consts", "tape") else ()
        for mi, wh, nslab, cols in _gl:
            ev_abs = None
            for sl in range(nslab):
                src = bass.AP(wh.tensor, wh.offset + sl * 128 * cols,
                              [[cols, 128], [1, cols]])
                po = (sl % 2) * 2048 if cols == D else 0
                war = ev_abs if (cols != D or sl >= 2) else None
                evl = kb.op("pool", lambda e, s=src, c=cols, p=po: e.dma_start(
                    out=wf(PRE_SQ + p, c), in_=s),
                    waits=[w for w in (war,) if w], dma="dpre")
                ev_abs = kb.op("act", lambda e, j=sl, c=cols, p=po: e.activation(
                    out=wf(PRE_SQ + p, c), in_=wf(PRE_SQ + p, c), func=AF.Abs,
                    accum_out=absc[:, j:j + 1]), waits=[evl, ev_consts], sig=True)
            ev_red = kb.op("act", lambda e, m=mi, n=nslab: e.activation(
                out=gred[:, 0:n], in_=absc[:, 0:n], func=AF.Copy,
                accum_out=st4[:, 12 + m:13 + m]), waits=[ev_abs], sig=True)
            bnk = pb[2]
            mm_group(bnk, [lambda t, m=mi, b=bnk: t.matmul(
                b.ap[0:1, 0:128], st4[:, 12 + m:13 + m], ident_f[:],
                is_transpose=True, start=True, stop=True)],
                extra_waits=[ev_red])
            ev_gam_row[mi] = bank_read("act", bnk, lambda e, m=mi, b=bnk: e.activation(
                out=gred[0:1, 0:128], in_=b.ap[0:1, 0:128], func=AF.Copy,
                accum_out=st4[0:1, 8 + m:9 + m]))
        ev_th = None
        for mi in ((1, 2, 3) if stop_after not in ("consts", "tape") else ()):
            numel = HFF * D
            ev1 = kb.op("dve", lambda e, m=mi, n=numel: e.tensor_scalar(
                out=st4[0:1, 8 + m:9 + m], in0=st4[0:1, 8 + m:9 + m],
                scalar1=0.5 / n, scalar2=0.5 * FFEPS, op0=ALU.mult, op1=ALU.add),
                waits=[ev_gam_row[mi]], sig=True)
            bnk = pb[2]
            mm_group(bnk, [lambda t, m=mi, b=bnk: t.matmul(
                b.ap[:, 0:1], ones_rf[:], st4[0:1, 8 + m:9 + m],
                start=True, stop=True)], extra_waits=[ev1])
            ev2 = bank_read("dve", bnk, lambda e, m=mi, b=bnk: e.tensor_copy(
                gam_sb[:, m:m + 1], b.ap[:, 0:1]))
            ev_th = kb.op("dve", lambda e, m=mi: e.tensor_scalar_mul(
                nth_sb[:, m:m + 1], gam_sb[:, m:m + 1], -1.0), waits=[ev2], sig=True)
        tap("gamma_th", lambda: gam_sb[:, 0:4], [128, 4], F32, [ev_th])
        tap("pre_st4", lambda: st4[:, 0:16], [128, 16], F32, [ev_th])
        tap("pre_absc", lambda: absc[:, 0:NHC], [128, NHC], F32, [ev_th])

        ev_presq_free = [ev_gam_row.get(3), ev_gam_row.get(3)]
        ev_stage_pe = [None, None]
        ev_tape_slot = [None, None]
        par_ct = [0]

        def quant_and_transpose(src_ap, cols, mi, nblk, seg):
            nonlocal ev_prev_tape
            if cols == D:
                par = par_ct[0] % 2
                par_ct[0] += 1
                oq, oq2, ot = PRE_Q + par * 2048, PRE_Q2 + par * 2048, \
                    PRE_T + par * 2560
                presq = PRE_SQ + par * 2048
            else:
                par = 0
                oq, oq2, ot = PRE_Q, PRE_Q2, PRE_T
                presq = PRE_SQ
            waitl = [ev_presq_free[par]]
            if cols != D:
                waitl += [ev_presq_free[1 - par], ev_stage_pe[1 - par],
                          ev_tape_slot[1 - par]]
            evl = kb.op("pool", lambda e, s=src_ap, c=cols, o=presq: e.dma_start(
                out=wf(o, c), in_=s),
                waits=[w for w in waitl if w], dma="dpre")
            kb.op("dve", lambda e, c=cols, m=mi, o=presq, q=oq:
                  e.tensor_single_scalar(
                      wb(q, c), wf(o, c), gam_sb[:, m:m + 1], ALU.is_gt),
                  waits=[w for w in (evl, ev_th, ev_stage_pe[par]) if w])
            ev_presq_free[par] = kb.op(
                "dve", lambda e, c=cols, m=mi, o=presq, q=oq2:
                e.tensor_single_scalar(
                    wb(q, c), wf(o, c), nth_sb[:, m:m + 1], ALU.is_lt), sig=True)
            ev_q = kb.op("dve", lambda e, c=cols, q=oq, q2=oq2: e.tensor_tensor(
                out=wb(q, c), in0=wb(q, c), in1=wb(q2, c),
                op=ALU.subtract), sig=True)
            ev_ev = None
            tape_war = ev_tape_slot[par]
            for g0 in range(0, nblk, 8):
                gn = min(8, nblk - g0)
                mml = []
                for j in range(gn):
                    blkc = oq + (g0 + j) * 128
                    mml.append(lambda t, bc=blkc, jj=j, gn=gn: t.matmul(
                        pt.ap[:, jj * 128:(jj + 1) * 128], wb(bc, 128), ident_b[:],
                        is_transpose=True, start=(jj == 0), stop=(jj == gn - 1)))
                mm_group(pt, mml, extra_waits=[ev_q, ev_consts])
                ev_ev = bank_read("dve", pt, lambda e, g0=g0, gn=gn, o=ot:
                                  e.tensor_copy(
                                      wb(o + g0 * 128, gn * 128),
                                      pt.ap[:, 0:gn * 128]),
                                  extra_waits=[tape_war] if g0 == 0 else ())
            ev_stage_pe[par] = kb.now("pe")
            ev = kb.op("sp", lambda e, a=seg_off[seg], n=nblk, o=ot: e.dma_start(
                out=wtape[:, a:a + n * 128], in_=wb(o, n * 128)),
                waits=[ev_ev], dma="dw")
            ev_tape_slot[par] = ev
            ev_prev_tape = ev

        _ql = ((1, w1_h), (2, w2_h)) if stop_after not in ("consts", "tape") else ()
        for mi, wh in _ql:
            for hb in range(NHC):
                src = bass.AP(wh.tensor, wh.offset + hb * 128 * D, [[D, 128], [1, D]])
                quant_and_transpose(src, D, mi, NKC, f"q{mi}_{hb}")
        for ob in range(NKC if stop_after not in ("consts", "tape") else 0):
            src = bass.AP(w3_h.tensor, w3_h.offset + ob * 128 * HFF,
                          [[HFF, 128], [1, HFF]])
            quant_and_transpose(src, HFF, 3, NHC, f"q3_{ob}")

        ev_tape_done = Ev("dw", kb.semtot.get("dw", 0))
        ev_pre_dve = kb.now("dve")
        ev_pre_pe = kb.now("pe")
        ev_pre_act = kb.now("act")

        x_free = [None, None]

        _nt = 0 if stop_after in ("consts", "tape", "pre") else NT
        if stop_after in ("xload", "lnstats", "lnT", "ln1",
                          "ba1", "ba2", "ba3"):
            _nt = 1
        for ti in range(_nt):
            ring_state["cursor"] = 0
            png = ti % 2
            xoff = png * NCH * D
            a = x_tm.ap()
            dst_x = bass.AP(a.tensor, a.offset + xoff, [a.ap[0], [D, NCH], [1, D]])
            src_x = bass.AP(x_h.tensor, x_h.offset + ti * T * D,
                            [[D, 128], [128 * D, NCH], [1, D]])
            wv = [x_free[png]] if x_free[png] else []
            ev_x = kb.op("pool", lambda e, s=src_x, d=dst_x: e.dma_start(out=d, in_=s),
                         waits=wv, dma="dx")
            if ti == 0:
                tap("x_tm", lambda: x_tm[:, 0:NCH * D], [128, NCH * D], BF16, [ev_x])
            if stop_after == "xload":
                break

            ev_st = None
            first_w = [ev_x]
            if ti == 0:
                first_w += [ev_pre_act, ev_pre_dve, ev_pre_pe, ev_consts]
            else:
                first_w += [kb.now("dve"), kb.now("pe")]
            for tc in range(NCH):
                kb.op("act", lambda e, tc=tc, xo=xoff: e.activation(
                    out=wf(WF["sq"], D), in_=x_tm[:, xo + tc * D:xo + (tc + 1) * D],
                    func=AF.Square, accum_out=st4[:, 4 + tc:5 + tc]),
                    waits=first_w if tc == 0 else ())
                ev_st = kb.op("act", lambda e, tc=tc, xo=xoff: e.activation(
                    out=wf(WF["sq"], D), in_=x_tm[:, xo + tc * D:xo + (tc + 1) * D],
                    func=AF.Copy, accum_out=st4[:, tc:tc + 1]), sig=True)
            e1w = [ev_st]
            if ti == 0:
                e1w += [ev_pre_pe, ev_tape_done]
            e1 = kb.op("dve", lambda e: e.tensor_scalar_mul(st4[:, 0:8], st4[:, 0:8],
                                                            1.0 / D),
                       waits=e1w, sig=True)
            e2 = kb.op("dve", lambda e: e.tensor_tensor(
                out=st4[:, 8:12], in0=st4[:, 0:4], in1=st4[:, 0:4], op=ALU.mult),
                waits=[e1], sig=True)
            e3 = kb.op("dve", lambda e: e.tensor_tensor(
                out=st4[:, 8:12], in0=st4[:, 4:8], in1=st4[:, 8:12], op=ALU.subtract),
                waits=[e2], sig=True)
            e4 = kb.op("act", lambda e: e.activation(
                out=st4[:, 8:12], in_=st4[:, 8:12], func=AF.Sqrt, bias=eps_ln[:]),
                waits=[e3], sig=True)
            e5 = kb.op("dve", lambda e: e.reciprocal(st4[:, 12:16], st4[:, 8:12]),
                       waits=[e4], sig=True)
            ev_xn_tm = []
            for tc in range(NCH):
                ev = kb.op("dve", lambda e, tc=tc, xo=xoff: e.tensor_scalar(
                    out=x_tm[:, xo + tc * D:xo + (tc + 1) * D],
                    in0=x_tm[:, xo + tc * D:xo + (tc + 1) * D],
                    scalar1=st4[:, tc:tc + 1], scalar2=st4[:, 12 + tc:13 + tc],
                    op0=ALU.subtract, op1=ALU.mult), waits=[e5], sig=True)
                ev_xn_tm.append(ev)
            if ti == 0:
                tap("xn_tm", lambda: x_tm[:, 0:NCH * D], [128, NCH * D], BF16,
                    [ev_xn_tm[-1]])
            if stop_after == "lnstats":
                break
            mm_group(pb[3], [
                lambda t: t.matmul(pb[3].ap[0:4, 0:128], st4[:, 0:4], ident_f[:],
                                   is_transpose=True, start=True, stop=False),
                lambda t: t.matmul(pb[3].ap[0:4, 128:256], st4[:, 8:12], ident_f[:],
                                   is_transpose=True, start=False, stop=True)],
                extra_waits=[e5])
            ev_mr = bank_read("dve", pb[3], lambda e: e.tensor_copy(
                mrow[:], pb[3].ap[0:4, 0:256]))
            ev_mbsb = None
            for bk, tcs in ((pb[3], (0, 1)), (pb[4], (2, 3))):
                mml = []
                for j, tc in enumerate(tcs):
                    mml.append(lambda t, tc=tc, j=j, b=bk: t.matmul(
                        b.ap[:, j * 256:(j + 1) * 256],
                        sel4[:, tc * 128:(tc + 1) * 128], mrow[:],
                        start=(j == 0), stop=(j == 1)))
                mm_group(bk, mml, extra_waits=[ev_mr])
                ev_mbsb = bank_read("dve", bk, lambda e, b=bk, t0=tcs[0]: e.tensor_copy(
                    mbsb[:, t0 * 256:t0 * 256 + 512], b.ap[:, 0:512]))

            ev_xn = None
            for tc in range(NCH):
                for g0 in range(0, NKC, 8):
                    mml = []
                    for j in range(8):
                        fc = g0 + j
                        mml.append(lambda t, fc=fc, j=j, tc=tc, xo=xoff: t.matmul(
                            pt.ap[:, j * 128:(j + 1) * 128],
                            x_tm[:, xo + tc * D + fc * 128:xo + tc * D + fc * 128 + 128],
                            ident_b[:], is_transpose=True,
                            start=(j == 0), stop=(j == 7)))
                    mm_group(pt, mml, extra_waits=[ev_xn_tm[tc]])
                    an = xn_fm.ap()
                    dst = bass.AP(an.tensor, an.offset + g0 * T + tc * 128,
                                  [an.ap[0], [T, 8], [1, 128]])
                    ev_xn = bank_read("dve", pt, lambda e, d=dst: e.tensor_copy(
                        d, pt.ap.rearrange("p (a b) -> p a b", a=8)))
            x_free[png] = ev_xn
            if ti == 0:
                tap("xn_fm", lambda: xn_fm[:, 0:NKC * T], [128, NKC * T], BF16, [ev_xn])
            if stop_after == "lnT":
                break

            bnk = pb[5]
            mml = []
            for kc in range(NKC):
                mml.append(lambda t, kc=kc, b=bnk: t.matmul(
                    b.ap[0:64, 0:T], wba[:, kc * 64:(kc + 1) * 64],
                    xn_fm[:, kc * T:(kc + 1) * T],
                    start=(kc == 0), stop=(kc == NKC - 1)))
            mm_group(bnk, mml, extra_waits=[ev_xn, ev_wba])
            evb = bank_read("act", bnk, lambda e, b=bnk: e.activation(
                out=ba_sb[0:32, 0:T], in_=b.ap[0:32, 0:T], func=AF.Sigmoid))
            eva1 = kb.op("act", lambda e, b=bnk: e.activation(
                out=ba_sb[32:64, T:2 * T], in_=b.ap[32:64, 0:T], func=AF.Abs,
                bias=dtb_c[32:64, :]), waits=[bnk.wev], sig=True)
            eva2 = kb.op("act", lambda e: e.activation(
                out=ba_sb[32:64, T:2 * T], in_=ba_sb[32:64, T:2 * T], func=AF.Exp,
                scale=-1.0), waits=[eva1], sig=True)
            eva3 = kb.op("act", lambda e: e.activation(
                out=ba_sb[32:64, T:2 * T], in_=ba_sb[32:64, T:2 * T], func=AF.Ln,
                bias=one_c[32:64, :]), waits=[eva2], sig=True)
            eva4 = kb.op("dve", lambda e, b=bnk: e.tensor_scalar(
                out=ba_sb[32:64, 0:T], in0=b.ap[32:64, 0:T],
                scalar1=dtb_c[32:64, :], scalar2=0.0, op0=ALU.add, op1=ALU.max),
                waits=[bnk.wev], sig=True)
            bnk.rev = eva4
            eva5 = kb.op("dve", lambda e: e.tensor_tensor(
                out=ba_sb[32:64, 0:T], in0=ba_sb[32:64, 0:T],
                in1=ba_sb[32:64, T:2 * T], op=ALU.add), waits=[eva3, eva4], sig=True)
            eva6 = kb.op("dve", lambda e: e.tensor_scalar_mul(
                ba_sb[32:64, 0:T], ba_sb[32:64, 0:T], nega_c[32:64, :]),
                waits=[eva5, ev_nega], sig=True)
            if ti == 0:
                tap("ba_g", lambda: ba_sb[0:64, 0:T], [64, T], F32, [eva6, evb])
            if stop_after == "ba1":
                break
            ev_gamma = None
            for cc in range(NCH):
                ev_gamma = kb.op("dve", lambda e, cc=cc: e.tensor_tensor_scan(
                    out=ba_sb[32:64, 2 * T + cc * C:2 * T + (cc + 1) * C],
                    data0=ba_sb[32:64, cc * C:(cc + 1) * C],
                    data1=ba_sb[32:64, cc * C:(cc + 1) * C],
                    initial=0.0, op0=ALU.add, op1=ALU.bypass),
                    waits=[eva6], sig=True)
            if ti == 0:
                tap("ba_gam", lambda: ba_sb[0:64, 2 * T:3 * T], [64, T], F32,
                    [ev_gamma])
            if stop_after == "ba2":
                break
            mml = []
            for cc in range(NCH):
                mml.append(lambda t, cc=cc: t.matmul(
                    pb[5].ap[:, cc * 32:(cc + 1) * 32],
                    ba_sb[32:64, 2 * T + cc * C:2 * T + (cc + 1) * C],
                    ident_f[32:64, 32:64], is_transpose=True,
                    start=(cc == 0), stop=(cc == NCH - 1)))
            mm_group(pb[5], mml, extra_waits=[evb, ev_gamma])
            ev_gb1 = bank_read("dve", pb[5], lambda e: e.tensor_copy(
                bass.AP(gb_tmh.ap().tensor, gb_tmh.ap().offset,
                        [gb_tmh.ap().ap[0], [64, NCH], [1, 32]]),
                pb[5].ap[:, 0:NCH * 32].rearrange("p (a b) -> p a b", a=NCH)))
            mml = []
            for cc in range(NCH):
                mml.append(lambda t, cc=cc: t.matmul(
                    pb[6].ap[:, cc * 32:(cc + 1) * 32],
                    ba_sb[0:32, cc * C:(cc + 1) * C],
                    ident_f[0:32, 0:32], is_transpose=True,
                    start=(cc == 0), stop=(cc == NCH - 1)))
            mm_group(pb[6], mml, extra_waits=[evb, ev_gamma])
            ev_gb2 = bank_read("dve", pb[6], lambda e: e.tensor_copy(
                bass.AP(gb_tmh.ap().tensor, gb_tmh.ap().offset + 32,
                        [gb_tmh.ap().ap[0], [64, NCH], [1, 32]]),
                pb[6].ap[:, 0:NCH * 32].rearrange("p (a b) -> p a b", a=NCH)))
            ev_gb = ev_gb2
            if ti == 0:
                tap("gb_early", lambda: gb_tmh[:, 0:NCH * 64], [128, NCH * 64], F32,
                    [ev_gb])
            if stop_after == "ba3":
                break
            ae, ag = exps_t.ap(), gb_tmh.ap()
            ev_exps = kb.op("act", lambda e: e.activation(
                out=bass.AP(ae.tensor, ae.offset, [ae.ap[0], [32, NCH], [1, 32]]),
                in_=bass.AP(ag.tensor, ag.offset, [ag.ap[0], [64, NCH], [1, 32]]),
                func=AF.Exp), waits=[ev_gb], sig=True)
            ab = bl_t.ap()
            ev_bl = kb.op("dve", lambda e: e.tensor_tensor(
                out=bass.AP(ab.tensor, ab.offset, [ab.ap[0], [32, NCH], [1, 32]]),
                in0=bass.AP(ae.tensor, ae.offset, [ae.ap[0], [32, NCH], [1, 32]]),
                in1=bass.AP(ag.tensor, ag.offset + 32, [ag.ap[0], [64, NCH], [1, 32]]),
                op=ALU.mult), waits=[ev_exps], sig=True)
            if ti == 0:
                tap("gb_tmh", lambda: gb_tmh[:, 0:NCH * 64], [128, NCH * 64], F32,
                    [ev_gb])
            if stop_after in ("ln", "ln1"):
                break

            carry = {"silu_pe": kb.now("pe"),
                     "bcast_pe": kb.now("pe"),
                     "qk_dve": kb.now("dve"),
                     "cvo_act": kb.now("act"),
                     "vg_pe": kb.now("pe"),
                     "qkg_pe": kb.now("pe"),
                     "zg_dve": kb.now("dve"),
                     "sbf_pe": kb.now("pe"),
                     "gated_pe": kb.now("pe")}

            for g in range(NGRP):
                o_qk, ev_qk, ent_qk = ring_push(f"qqkk{g}", SEG_QQKK)
                o_v, ev_v, ent_v = ring_push(f"vvvv{g}", SEG_QQKK)

                def proj_mm_conv(ring_off, ring_sub, qn_off, sq_off, bnk_i,
                                 ready_ev, conv_ch, kind, dst_off=None):
                    bnkp = pb[bnk_i]
                    mml = []
                    for kc in range(NKC):
                        mml.append(lambda t, kc=kc, b=bnkp, ro=ring_off,
                                   rs=ring_sub: t.matmul(
                            b.ap[:, 0:T],
                            ring[:, ro + kc * 512 + rs:ro + kc * 512 + rs + 128],
                            xn_fm[:, kc * T:(kc + 1) * T],
                            start=(kc == 0), stop=(kc == NKC - 1)))
                    mm_group(bnkp, mml, extra_waits=[ready_ev, ev_xn])
                    hco = conv_ch * 3
                    kb.op("dve", lambda e, hco=hco: e.tensor_copy(
                        wb(WB["mxp"], 3), conv_hist[:, hco:hco + 3]),
                        waits=[ev_consts, carry["cvo_act"], carry["sbf_pe"],
                               tap_ev[0]])
                    bank_read("dve", bnkp, lambda e, b=bnkp: e.tensor_copy(
                        wb(WB["mxp"] + 3, T), b.ap[:, 0:T]))
                    kb.op("dve", lambda e, hco=hco: e.tensor_copy(
                        conv_hist[:, hco:hco + 3], wb(WB["mxp"] + T, 3)))
                    if conv_ch == 0:
                        tap("mxp0", lambda: wb(WB["mxp"], T + 3), [128, T + 3],
                            BF16, [kb.now("dve")])
                    kb.op("dve", lambda e, cc_=conv_ch: e.tensor_scalar(
                        out=wb(WB["cvo"], T), in0=wb(WB["mxp"], T),
                        scalar1=conv_w[:, cc_ * 4:cc_ * 4 + 1], scalar2=None,
                        op0=ALU.mult), waits=[ev_cw])
                    for j in (1, 2):
                        kb.op("dve", lambda e, j=j, cc_=conv_ch: e.scalar_tensor_tensor(
                            out=wb(WB["cvo"], T), in0=wb(WB["mxp"] + j, T),
                            scalar=conv_w[:, cc_ * 4 + j:cc_ * 4 + j + 1],
                            in1=wb(WB["cvo"], T), op0=ALU.mult, op1=ALU.add))
                    ev_cv = kb.op("dve", lambda e, cc_=conv_ch: e.scalar_tensor_tensor(
                        out=wb(WB["cvo"], T), in0=wb(WB["mxp"] + 3, T),
                        scalar=conv_w[:, cc_ * 4 + 3:cc_ * 4 + 4],
                        in1=wb(WB["cvo"], T), op0=ALU.mult, op1=ALU.add), sig=True)
                    if kind == "v":
                        ev = kb.op("act", lambda e, d=dst_off: e.activation(
                            out=wb(d, T), in_=wb(WB["cvo"], T), func=AF.Silu),
                            waits=[ev_cv, carry["vg_pe"]], sig=True)
                        carry["cvo_act"] = ev
                        return ev
                    ev_s = kb.op("act", lambda e, q=qn_off: e.activation(
                        out=wb(q, T), in_=wb(WB["cvo"], T), func=AF.Silu),
                        waits=[ev_cv, carry["qk_dve"], carry["silu_pe"],
                               carry["sbf_pe"]], sig=True)
                    carry["cvo_act"] = ev_s
                    ev_sq = kb.op("act", lambda e, q=qn_off, s=sq_off: e.activation(
                        out=wb(s, T), in_=wb(q, T), func=AF.Square),
                        waits=[ev_s], sig=True)
                    return ev_sq

                def proj_l2_finish(qn_off, sq_off, dst_off, kind, ev_sq):
                    b2 = pb[6]
                    mm_group(b2, [lambda t, b=b2, s=sq_off: t.matmul(
                        b.ap[0:1, 0:T], ones_cb[:], wb(s, T),
                        start=True, stop=True)], extra_waits=[ev_sq])
                    carry["silu_pe"] = b2.wev
                    ev_ln1 = bank_read("act", b2, lambda e, b=b2: e.activation(
                        out=wf(WF["u1"], T)[0:1, :], in_=b.ap[0:1, 0:T],
                        func=AF.Ln, bias=eps_l2[0:1, :]))
                    ev_ln2 = kb.op("act", lambda e, k=kind: e.activation(
                        out=wb(WB["z1"], T)[0:1, :], in_=wf(WF["u1"], T)[0:1, :],
                        func=AF.Exp, scale=-0.5,
                        bias=(qsc_b[0:1, :] if k == "q" else zero_c[0:1, :])),
                        waits=[ev_ln1, carry["bcast_pe"]], sig=True)
                    mm_group(b2, [lambda t, b=b2: t.matmul(
                        b.ap[:, 0:T], ones_rb[:], wb(WB["z1"], T)[0:1, :],
                        start=True, stop=True)], extra_waits=[ev_ln2])
                    carry["bcast_pe"] = b2.wev
                    ev = bank_read("dve", b2, lambda e, b=b2, q=qn_off, d=dst_off:
                                   e.tensor_tensor(
                                       out=wb(d, T), in0=wb(q, T), in1=b.ap[:, 0:T],
                                       op=ALU.mult), extra_waits=[carry["qkg_pe"]])
                    carry["qk_dve"] = ev
                    return ev

                ev_qg, ev_kg, ev_vg = [None, None], [None, None], [None] * 4
                qk_phase = []
                for i in range(2):
                    ev_sq = proj_mm_conv(o_qk, i * 128, WB["di"] + i * T,
                                         WB["wt"] + i * T, i % 2, ev_qk,
                                         2 * g + i, "q")
                    qk_phase.append(("q", i, ev_sq))
                for i in range(2):
                    ev_sq = proj_mm_conv(o_qk, 256 + i * 128, WB["di"] + (2 + i) * T,
                                         WB["wt"] + (2 + i) * T, i % 2, ev_qk,
                                         16 + 2 * g + i, "k")
                    qk_phase.append(("k", i, ev_sq))
                ring_release(ent_qk, kb.now("pe"))
                o_z, ev_z, ent_z = ring_push(f"zzzz{g}", SEG_QQKK)
                for kind, i, ev_sq in qk_phase:
                    j = i if kind == "q" else 2 + i
                    dst = (WB["qg"] if kind == "q" else WB["kg"]) + i * T
                    ev = proj_l2_finish(WB["di"] + j * T, WB["wt"] + j * T,
                                        dst, kind, ev_sq)
                    if kind == "q":
                        ev_qg[i] = ev
                    else:
                        ev_kg[i] = ev
                for i in range(4):
                    ev_vg[i] = proj_mm_conv(o_v, i * 128, 0, 0, i % 2, ev_v,
                                            32 + 4 * g + i, "v",
                                            dst_off=WB["vg"] + i * T)
                ring_release(ent_v, kb.now("pe"))
                o_wo, ev_wo, ent_wo = ring_push(f"wout{g}", SEG_WOUT)

                if ti == 0 and g == 0:
                    evq = kb.now("dve")
                    tap("qg0", lambda: wb(WB["qg"], 2 * T), [128, 2 * T], BF16, [evq])
                    tap("kg0", lambda: wb(WB["kg"], 2 * T), [128, 2 * T], BF16, [evq])
                    eva = ev_vg[3]
                    tap("vg0", lambda: wb(WB["vg"], 4 * T), [128, 4 * T], BF16, [eva])

                ev_zg = None
                for cc in range(NCH):
                    bnkz = pb[2 + 2 * (cc % 2)]
                    mml = []
                    for kc in range(NKC):
                        mml.append(lambda t, kc=kc, cc=cc, b=bnkz, oz=o_z: t.matmul(
                            b.ap[:, 0:512],
                            xn_fm[:, kc * T + cc * C:kc * T + (cc + 1) * C],
                            ring[:, oz + kc * 512:oz + (kc + 1) * 512],
                            start=(kc == 0), stop=(kc == NKC - 1)))
                    mm_group(bnkz, mml, extra_waits=[ev_z, ev_xn])
                    ev_zg = bank_read("act", bnkz, lambda e, cc=cc, b=bnkz:
                                      e.activation(
                                          out=wb(WB["zg"] + cc * 512, 512),
                                          in_=b.ap[:, 0:512], func=AF.Silu),
                                      extra_waits=[carry["zg_dve"], tap_ev[0]])
                ring_release(ent_z, kb.now("pe"))

                ev_kkkq = None
                for kl in range(2):
                    for c0 in range(0, NCH, 2):
                        bnkk = pb[2 + 2 * ((c0 // 2) % 2)]
                        mml = []
                        for j, cc in enumerate((c0, c0 + 1)):
                            mml.append(lambda t, cc=cc, kl=kl, j=j, b=bnkk: t.matmul(
                                b.ap[:, j * 256:j * 256 + 128],
                                wb(WB["kg"] + kl * T + cc * C, C),
                                wb(WB["kg"] + kl * T + cc * C, C),
                                start=(j == 0), stop=False))
                            mml.append(lambda t, cc=cc, kl=kl, j=j, b=bnkk: t.matmul(
                                b.ap[:, j * 256 + 128:j * 256 + 256],
                                wb(WB["kg"] + kl * T + cc * C, C),
                                wb(WB["qg"] + kl * T + cc * C, C),
                                start=False, stop=(j == 1)))
                        mm_group(bnkk, mml, extra_waits=[ev_kg[kl], ev_qg[kl]])
                        ev_kkkq = bank_read("act", bnkk,
                                            lambda e, kl=kl, c0=c0, b=bnkk:
                                            e.copy(
                                                wb(WB["kkkq"] + kl * NCH * 256 + c0 * 256, 512),
                                                b.ap[:, 0:512]),
                                            extra_waits=[carry["gated_pe"],
                                                         tap_ev[0]])
                carry["qkg_pe"] = kb.now("pe")
                ev_vt = None
                for b0 in range(0, 16, 8):
                    mml = []
                    for j in range(8):
                        hh, cc = divmod(b0 + j, NCH)
                        mml.append(lambda t, hh=hh, cc=cc, j=j: t.matmul(
                            pt.ap[:, j * 128:(j + 1) * 128],
                            wb(WB["vg"] + hh * T + cc * C, C),
                            ident_b[:], is_transpose=True,
                            start=(j == 0), stop=(j == 7)))
                    mm_group(pt, mml, extra_waits=ev_vg)
                    ev_vt = bank_read("act", pt, lambda e, b0=b0: e.copy(
                        wb(WB["vt"] + b0 * 128, 1024), pt.ap[:, 0:1024]))
                mml = []
                for j in range(8):
                    kl, cc = divmod(j, NCH)
                    mml.append(lambda t, kl=kl, cc=cc, j=j: t.matmul(
                        pt.ap[:, j * 128:(j + 1) * 128],
                        wb(WB["kg"] + kl * T + cc * C, C),
                        ident_b[:], is_transpose=True, start=(j == 0), stop=(j == 7)))
                mm_group(pt, mml, extra_waits=[ev_kg[0], ev_kg[1]])
                ev_kt = bank_read("act", pt, lambda e: e.copy(
                    wb(WB["kt"], 1024), pt.ap[:, 0:1024]))
                carry["vg_pe"] = kb.now("pe")

                ev_lcc = None
                G_SLOT = (WB["g_nw"], WB["dl"])
                Z_SLOT = (WB["z1"], WB["dl2"])
                XT_A = (WB["xt0"], WB["u_sb"])
                XT_B = (WB["at"], WB["gtd"])
                XB = (pb[2], pb[3])
                PBANK = (pb[4], pb[5])
                XTBANK = (pb[6], pb[3])
                ev_zslot = [None] * 4
                ev_slot_pe = None

                def p1_prep(hh):
                    p = hh % 2
                    h32 = (4 * g + hh) % 32
                    kl = hh // 2
                    bnkx = XB[p]
                    mml = []
                    for cc in range(NCH):
                        view = bass.AP(gb_tmh.ap().tensor,
                                       gb_tmh.ap().offset + cc * 64 + h32,
                                       [gb_tmh.ap().ap[0], [0, C]])
                        mml.append(lambda t, v=view, cc=cc, b=bnkx: t.matmul(
                            b.ap[:, cc * C:(cc + 1) * C], v, ident_f[:],
                            is_transpose=True, start=(cc == 0), stop=(cc == NCH - 1)))
                    mm_group(bnkx, mml, extra_waits=[ev_gb])
                    gview = bcast3(gb_tmh, h32, NCH, 64, C)
                    ev_e1 = bank_read("dve", bnkx, lambda e, gv=gview, b=bnkx:
                                      e.tensor_tensor(
                                          out=wf3(WF["e1"], NCH, C, C),
                                          in0=b.ap.rearrange("p (a b) -> p a b", a=NCH),
                                          in1=gv, op=ALU.subtract))
                    mi_v = bass.AP(mask_i.ap().tensor, mask_i.ap().offset,
                                   [mask_i.ap().ap[0], [0, NCH], [1, C]])
                    ev_e2 = kb.op("dve", lambda e, mv=mi_v: e.tensor_tensor(
                        out=wf3(WF["e2"], NCH, C, C), in0=wf3(WF["e1"], NCH, C, C),
                        in1=mv, op=ALU.add), waits=[ev_e1], sig=True)
                    ev_di = kb.op("act", lambda e, hh=hh: e.activation(
                        out=wb(WB["di"] + hh * NCH * C, NCH * C),
                        in_=wf(WF["e2"], NCH * C), func=AF.Exp),
                        waits=[ev_e2, tap_ev[0]], sig=True)
                    ms_v = bass.AP(mask_s.ap().tensor, mask_s.ap().offset,
                                   [mask_s.ap().ap[0], [0, NCH], [1, C]])
                    ev_e3 = kb.op("dve", lambda e, mv=ms_v: e.tensor_tensor(
                        out=wf3(WF["e2"], NCH, C, C), in0=wf3(WF["e1"], NCH, C, C),
                        in1=mv, op=ALU.add), waits=[ev_di], sig=True)
                    gso = G_SLOT[p]
                    ev_ds = kb.op("act", lambda e, go=gso: e.activation(
                        out=wb(go, NCH * C), in_=wf(WF["e2"], NCH * C),
                        func=AF.Exp, scale=-1.0),
                        waits=[ev_e3, ev_slot_pe], sig=True)
                    kk_v = wb3(WB["kkkq"] + kl * NCH * 256, NCH, 256, C)
                    zso = Z_SLOT[p]
                    ev_a1 = kb.op("dve", lambda e, kv=kk_v, go=gso, zo=zso:
                                  e.tensor_tensor(
                                      out=wb3(zo, NCH, C, C), in0=wb3(go, NCH, C, C),
                                      in1=kv, op=ALU.mult),
                                  waits=[ev_ds, ev_kkkq, ev_slot_pe], sig=True)
                    b_v = bcast3(gb_tmh, 32 + h32, NCH, 64, C)
                    ev_a = kb.op("dve", lambda e, bv=b_v, zo=zso: e.tensor_tensor(
                        out=wb3(zo, NCH, C, C), in0=wb3(zo, NCH, C, C),
                        in1=bv, op=ALU.mult), waits=[ev_a1], sig=True)
                    kq_v = wb3(WB["kkkq"] + kl * NCH * 256 + 128, NCH, 256, C)
                    kb.op("dve", lambda e, kv=kq_v, hh=hh: e.tensor_tensor(
                        out=wb3(WB["wt"] + hh * NCH * C, NCH, C, C),
                        in0=wb3(WB["di"] + hh * NCH * C, NCH, C, C),
                        in1=kv, op=ALU.mult), waits=[ev_di])
                    i_v = bass.AP(ident_b.ap().tensor, ident_b.ap().offset,
                                  [ident_b.ap().ap[0], [0, NCH], [1, C]])
                    ev_g = kb.op("dve", lambda e, iv=i_v, go=gso, zo=zso:
                                 e.tensor_tensor(
                                     out=wb3(go, NCH, C, C), in0=wb3(zo, NCH, C, C),
                                     in1=iv, op=ALU.add), waits=[ev_a], sig=True)
                    ev_z1n = kb.op("dve", lambda e, iv=i_v, zo=zso:
                                   e.scalar_tensor_tensor(
                                       out=wb3(zo, NCH, C, C),
                                       in0=wb3(zo, NCH, C, C),
                                       scalar=-1.0, in1=iv, op0=ALU.mult,
                                       op1=ALU.add), waits=[ev_g], sig=True)
                    ev_zslot[hh] = ev_z1n
                    ed_v = bass.AP(work_bf.ap().tensor,
                                   work_bf.ap().offset + WB["di"] + hh * NCH * C + 127,
                                   [work_bf.ap().ap[0], [C, NCH]])
                    ex_v = bass.AP(exps_t.ap().tensor, exps_t.ap().offset + h32,
                                   [exps_t.ap().ap[0], [32, NCH]])
                    nonlocal_lcc = kb.op("dve", lambda e, hh=hh, ev2=ed_v, xv=ex_v:
                                   e.tensor_tensor(
                                       out=bass.AP(work_f32.ap().tensor,
                                                   work_f32.ap().offset + WF["lcc"] + hh * NCH,
                                                   [work_f32.ap().ap[0], [1, NCH]]),
                                       in0=ev2, in1=xv, op=ALU.mult),
                                   waits=[ev_di, ev_exps, ev_bl], sig=True)
                    return nonlocal_lcc

                for hp in (0, 2):
                    for hh in (hp, hp + 1):
                        ev_lcc = p1_prep(hh)
                    xt_cur = [None, None]
                    ev_xt = [None, None]
                    for hh in (hp, hp + 1):
                        p = hh % 2
                        mml = []
                        for cc in range(NCH):
                            mml.append(lambda t, cc=cc, zo=Z_SLOT[p]: t.matmul(
                                pt.ap[:, cc * C:(cc + 1) * C], wb(zo + cc * C, C),
                                ident_b[:], is_transpose=True,
                                start=(cc == 0), stop=(cc == NCH - 1)))
                        mm_group(pt, mml, extra_waits=[ev_zslot[hh]])
                        ev_xt[p] = bank_read("act", pt,
                                             lambda e, d=XT_A[p]: e.copy(
                                                 wb(d, NCH * C), pt.ap[:, 0:NCH * C]),
                                             extra_waits=[ev_slot_pe])
                        xt_cur[p] = XT_A[p]
                    for step in range(2):
                        pevs = [None, None]
                        for hh in (hp, hp + 1):
                            p = hh % 2
                            bpn = PBANK[p]
                            mml = []
                            for cc in range(NCH):
                                mml.append(lambda t, cc=cc, b=bpn, c=xt_cur[p],
                                           go=G_SLOT[p]: t.matmul(
                                    b.ap[:, cc * C:(cc + 1) * C], wb(c + cc * C, C),
                                    wb(go + cc * C, C),
                                    start=(cc == 0), stop=(cc == NCH - 1)))
                            mm_group(bpn, mml, extra_waits=[ev_xt[p]])
                            pevs[p] = bpn
                        for hh in (hp, hp + 1):
                            p = hh % 2
                            bpn = pevs[p]
                            t_v = bass.AP(two_i.ap().tensor, two_i.ap().offset,
                                          [two_i.ap().ap[0], [0, NCH], [1, C]])
                            ev_zz = bank_read("dve", bpn,
                                              lambda e, b=bpn, tv=t_v, zo=Z_SLOT[p]:
                                              e.scalar_tensor_tensor(
                                                  out=wb3(zo, NCH, C, C),
                                                  in0=b.ap.rearrange(
                                                      "p (a b) -> p a b", a=NCH),
                                                  scalar=-1.0, in1=tv,
                                                  op0=ALU.mult, op1=ALU.add))
                            bq = XTBANK[p]
                            mml = []
                            for cc in range(NCH):
                                mml.append(lambda t, cc=cc, b=bq, c=xt_cur[p],
                                           zo=Z_SLOT[p]: t.matmul(
                                    b.ap[:, cc * C:(cc + 1) * C],
                                    wb(zo + cc * C, C), wb(c + cc * C, C),
                                    start=(cc == 0), stop=(cc == NCH - 1)))
                            mm_group(bq, mml, extra_waits=[ev_zz])
                            if step < 1:
                                dsto = XT_B[p] if xt_cur[p] == XT_A[p] else XT_A[p]
                            else:
                                dsto = WB["tt"] + hh * NCH * C
                            ev_xt[p] = bank_read("act", bq,
                                                 lambda e, b=bq, d=dsto: e.copy(
                                                     wb(d, NCH * C),
                                                     b.ap[:, 0:NCH * C]),
                                                 extra_waits=[tap_ev[0]])
                            xt_cur[p] = dsto
                    ev_slot_pe = kb.now("pe")
                ev_p1 = kb.now("dve")
                if ti == 0 and g == 0:
                    tap("tt0", lambda: wb(WB["tt"], 4 * NCH * C), [128, 4 * NCH * C],
                        BF16, [ev_p1])
                    tap("di0", lambda: wb(WB["di"], 4 * NCH * C), [128, 4 * NCH * C],
                        BF16, [ev_p1])

                prev_ks = prev_qs = None
                for cc in range(NCH):
                    ev_sb = kb.op("act", lambda e, g=g: e.copy(
                        wb(WB["sbf"], 512), s_f32[:, g * 512:(g + 1) * 512]),
                        waits=[w for w in (prev_ks, prev_qs, carry["sbf_pe"]) if w],
                        sig=True)
                    bKS, bQS = pb[3], pb[4]
                    mml = []
                    for p in range(2):
                        mml.append(lambda t, p=p, cc=cc, b=bKS: t.matmul(
                            b.ap[:, p * 256:(p + 1) * 256],
                            wb(WB["kg"] + p * T + cc * C, C),
                            wb(WB["sbf"] + p * 256, 256),
                            start=(p == 0), stop=(p == 1)))
                    mm_group(bKS, mml, extra_waits=[ev_sb])
                    prev_ks = bKS.wev
                    mml = []
                    for p in range(2):
                        mml.append(lambda t, p=p, cc=cc, b=bQS: t.matmul(
                            b.ap[:, p * 256:(p + 1) * 256],
                            wb(WB["qg"] + p * T + cc * C, C),
                            wb(WB["sbf"] + p * 256, 256),
                            start=(p == 0), stop=(p == 1)))
                    mm_group(bQS, mml, extra_waits=[ev_sb])
                    prev_qs = bQS.wev
                    bl_v = bcast3(bl_t, cc * 32 + 4 * g - 32 * (g // 8), 4, 1, C)
                    bl_v = bcast3(bl_t, cc * 32 + (4 * g) % 32, 4, 1, C)
                    ev_u1 = bank_read("dve", bKS, lambda e, bv=bl_v, b=bKS:
                                      e.tensor_tensor(
                                          out=wf3(WF["u1"], 4, C, C),
                                          in0=b.ap.rearrange("p (a b) -> p a b", a=4),
                                          in1=bv, op=ALU.mult),
                                      extra_waits=[ev_bl])
                    vt_v = wb3(WB["vt"] + cc * C, 4, NCH * C, C)
                    be_v = bcast3(gb_tmh, cc * 64 + 32 + (4 * g) % 32, 4, 1, C)
                    ev_u2 = kb.op("pool", lambda e, vv=vt_v, bv=be_v: e.tensor_tensor(
                        out=wf3(WF["g1"], 4, C, C), in0=vv, in1=bv, op=ALU.mult),
                        waits=[ev_vt, ev_gb], sig=True)
                    ev_u = kb.op("dve", lambda e: e.tensor_tensor(
                        out=wb(WB["u_sb"], 512), in0=wf(WF["g1"], 512),
                        in1=wf(WF["u1"], 512), op=ALU.subtract),
                        waits=[ev_u1, ev_u2], sig=True)
                    bD = pb[5]
                    mml = []
                    for hh in range(4):
                        mml.append(lambda t, hh=hh, cc=cc, b=bD: t.matmul(
                            b.ap[:, hh * 128:(hh + 1) * 128],
                            wb(WB["tt"] + hh * NCH * C + cc * C, C),
                            wb(WB["u_sb"] + hh * 128, 128),
                            start=(hh == 0), stop=(hh == 3)))
                    mm_group(bD, mml, extra_waits=[ev_u, ev_p1])
                    ev_dl = bank_read("act", bD, lambda e, b=bD: e.copy(
                        wb(WB["dl"], 512), b.ap[:, 0:512]))
                    ed_v = bass.AP(work_bf.ap().tensor,
                                   work_bf.ap().offset + WB["di"] + cc * C + 127,
                                   [work_bf.ap().ap[0], [NCH * C, 4], [0, C]])
                    ev_dl2 = kb.op("dve", lambda e, ev2=ed_v, b=bD: e.tensor_tensor(
                        out=wb(WB["dl2"], 512),
                        in0=b.ap.rearrange("p (a b) -> p a b", a=4),
                        in1=ev2, op=ALU.mult), waits=[ev_dl], sig=True)
                    bD.rev = ev_dl2
                    bO = pb[3]
                    mml = []
                    for hh in range(4):
                        mml.append(lambda t, hh=hh, cc=cc, b=bO: t.matmul(
                            b.ap[:, hh * 128:(hh + 1) * 128],
                            wb(WB["wt"] + hh * NCH * C + cc * C, C),
                            wb(WB["dl"] + hh * 128, 128),
                            start=(hh == 0), stop=(hh == 3)))
                    mm_group(bO, mml, extra_waits=[ev_dl])
                    prev_ks = bO.wev
                    eg_v = bcast3(exps_t, cc * 32 + (4 * g) % 32, 4, 1, C)
                    ev_g1 = bank_read("dve", bQS, lambda e, gv=eg_v, b=bQS:
                                      e.tensor_tensor(
                                          out=wf3(WF["g1"], 4, C, C),
                                          in0=b.ap.rearrange("p (a b) -> p a b", a=4),
                                          in1=gv, op=ALU.mult))
                    ev_o = bank_read("dve", bO, lambda e, b=bO: e.tensor_tensor(
                        out=wf(WF["osb"], 512), in0=b.ap[:, 0:512],
                        in1=wf(WF["g1"], 512), op=ALU.add), extra_waits=[ev_g1])
                    ev_ms = None
                    for hh in range(4):
                        ev_ms = kb.op("act", lambda e, hh=hh: e.activation(
                            out=wf(WF["u1"] + hh * C, C),
                            in_=wf(WF["osb"] + hh * C, C),
                            func=AF.Square,
                            accum_out=wf(WF["ms"] + hh, 1)),
                            waits=[ev_o] if hh == 0 else (), sig=(hh == 3))
                    ev_r1 = kb.op("act", lambda e: e.activation(
                        out=wf(WF["rstd"], 4), in_=wf(WF["ms"], 4), func=AF.Ln,
                        scale=1.0 / DV, bias=eps_rms[:]), waits=[ev_ms], sig=True)
                    ev_r2 = kb.op("act", lambda e: e.activation(
                        out=wf(WF["rstd"], 4), in_=wf(WF["rstd"], 4), func=AF.Exp,
                        scale=-0.5), waits=[ev_r1], sig=True)
                    rs_v = bcast3(work_f32, WF["rstd"], 4, 1, C)
                    ev_gt1 = kb.op("pool", lambda e, rv=rs_v: e.tensor_tensor(
                        out=wf3(WF["g1"], 4, C, C), in0=wf3(WF["osb"], 4, C, C),
                        in1=rv, op=ALU.mult), waits=[ev_r2], sig=True)
                    ev_gt = kb.op("pool", lambda e, cc=cc: e.tensor_tensor(
                        out=wb(WB["gtd"], 512), in0=wf(WF["g1"], 512),
                        in1=wb(WB["zg"] + cc * 512, 512), op=ALU.mult),
                        waits=[ev_gt1, ev_zg], sig=True)
                    carry["zg_dve"] = ev_gt
                    mml = []
                    for hh in range(4):
                        mml.append(lambda t, hh=hh: t.matmul(
                            pt.ap[:, hh * 128:(hh + 1) * 128],
                            wb(WB["gtd"] + hh * 128, 128), ident_b[:],
                            is_transpose=True, start=(hh == 0), stop=(hh == 3)))
                    mm_group(pt, mml, extra_waits=[ev_gt])
                    aw = work_bf.ap()
                    dstg = bass.AP(aw.tensor, aw.offset + WB["gated"] + cc * C,
                                   [aw.ap[0], [T, 4], [1, C]])
                    bank_read("act", pt, lambda e, d=dstg: e.copy(
                        d, pt.ap[:, 0:512].rearrange("p (a b) -> p a b", a=4)),
                        extra_waits=[carry["gated_pe"], tap_ev[0]])
                    bS = pb[4]
                    mml = []
                    for p in range(2):
                        mml.append(lambda t, p=p, cc=cc, b=bS: t.matmul(
                            b.ap[:, p * 256:(p + 1) * 256],
                            wb(WB["kt"] + (p * NCH + cc) * C, C),
                            wb(WB["dl2"] + p * 256, 256),
                            start=(p == 0), stop=(p == 1)))
                    mm_group(bS, mml, extra_waits=[ev_dl2, ev_kt])
                    prev_qs = bS.wev
                    for hh in range(4):
                        lcc_ap = bass.AP(work_f32.ap().tensor,
                                         work_f32.ap().offset + WF["lcc"] + hh * NCH + cc,
                                         [work_f32.ap().ap[0], [1, 1]])
                        bank_read("dve", bS, lambda e, hh=hh, g=g, sc=lcc_ap:
                                  e.scalar_tensor_tensor(
                                      out=s_f32[:, g * 512 + hh * 128:g * 512 + (hh + 1) * 128],
                                      in0=s_f32[:, g * 512 + hh * 128:g * 512 + (hh + 1) * 128],
                                      scalar=sc,
                                      in1=bS.ap[:, hh * 128:(hh + 1) * 128],
                                      op0=ALU.mult, op1=ALU.add),
                                  extra_waits=[ev_lcc])
                carry["sbf_pe"] = kb.now("pe")
                if ti == 0 and g == 0:
                    ev_gg = kb.now("dve")
                    tap("gated0", lambda: wb(WB["gated"], 4 * T), [128, 4 * T], BF16,
                        [ev_gg])
                    tap("s0", lambda: s_f32[:, 0:512], [128, 512], F32, [ev_gg])

                ev_gated_all = kb.now("dve")
                for oc in range(NKC):
                    bnko = pb[oc % 2]
                    mml = []
                    for kc in range(4):
                        mml.append(lambda t, kc=kc, oc=oc, b=bnko, ow=o_wo: t.matmul(
                            b.ap[:, 0:T],
                            ring[:, ow + kc * 2048 + oc * 128:
                                 ow + kc * 2048 + oc * 128 + 128],
                            wb(WB["gated"] + kc * T, T),
                            start=(kc == 0), stop=(kc == 3)))
                    mm_group(bnko, mml, extra_waits=[ev_wo, ev_gated_all])
                    if g == 0:
                        s_v = bass.AP(mbsb.ap().tensor, mbsb.ap().offset + 128,
                                      [mbsb.ap().ap[0], [256, NCH], [1, 128]])
                        m_v = bass.AP(mbsb.ap().tensor, mbsb.ap().offset,
                                      [mbsb.ap().ap[0], [256, NCH], [1, 128]])
                        an = xn_fm.ap()
                        xn_v = bass.AP(an.tensor, an.offset + oc * T,
                                       [an.ap[0], [C, NCH], [1, C]])
                        kb.op("dve", lambda e, xv=xn_v, sv=s_v: e.tensor_tensor(
                            out=wf3(WF["u1"], NCH, C, C), in0=xv, in1=sv,
                            op=ALU.mult), waits=[ev_mbsb, ev_xn])
                        kb.op("dve", lambda e, mv=m_v: e.tensor_tensor(
                            out=wf3(WF["u1"], NCH, C, C), in0=wf3(WF["u1"], NCH, C, C),
                            in1=mv, op=ALU.add))
                        bank_read("dve", bnko, lambda e, oc=oc, b=bnko: e.tensor_tensor(
                            out=attn_bf[:, oc * T:(oc + 1) * T], in0=b.ap[:, 0:T],
                            in1=wf(WF["u1"], T), op=ALU.add))
                    else:
                        bank_read("dve", bnko, lambda e, oc=oc, b=bnko: e.tensor_tensor(
                            out=attn_bf[:, oc * T:(oc + 1) * T], in0=b.ap[:, 0:T],
                            in1=attn_bf[:, oc * T:(oc + 1) * T], op=ALU.add))
                carry["gated_pe"] = kb.now("pe")
                ring_release(ent_wo, kb.now("pe"))

            ev_attn = kb.now("dve")
            if ti == 0:
                tap("attn0", lambda: attn_bf[:, 0:NKC * T], [128, NKC * T], BF16,
                    [ev_attn])
            if stop_after == "attn":
                break

            ev_hid = None
            prev_hid_pe = None
            for hc in range(NHC):
                o1, e_q1, ent1 = ring_push(f"q1_{hc}", SEG_Q12)
                if ti == 0 and hc == 0:
                    tap("ring_q1", lambda o=o1: ring[:, o:o + SEG_Q12],
                        [128, SEG_Q12], BF16, [e_q1])
                o2, e_q2, ent2 = ring_push(f"q2_{hc}", SEG_Q12)
                b1, b2 = pb[0], pb[1]
                mml = []
                for kc in range(NKC):
                    mml.append(lambda t, kc=kc, o=o1, b=b1: t.matmul(
                        b.ap[:, 0:T], ring[:, o + kc * 128:o + (kc + 1) * 128],
                        xn_fm[:, kc * T:(kc + 1) * T],
                        start=(kc == 0), stop=(kc == NKC - 1)))
                mm_group(b1, mml, extra_waits=[e_q1, ev_xn, ev_attn if hc == 0 else None])
                mml = []
                for kc in range(NKC):
                    mml.append(lambda t, kc=kc, o=o2, b=b2: t.matmul(
                        b.ap[:, 0:T], ring[:, o + kc * 128:o + (kc + 1) * 128],
                        xn_fm[:, kc * T:(kc + 1) * T],
                        start=(kc == 0), stop=(kc == NKC - 1)))
                mm_group(b2, mml, extra_waits=[e_q2])
                ev_sg = bank_read("act", b1, lambda e, b=b1: e.activation(
                    out=wb(WB_SIG, T), in_=b.ap[:, 0:T], func=AF.Sigmoid),
                    extra_waits=[ev_hid])
                if ti == 0 and hc == 0:
                    tap("sig0", lambda: wb(WB_SIG, T), [128, T], BF16, [ev_sg])
                ev_hid = bank_read("dve", b2, lambda e, hc=hc, b=b2: e.tensor_tensor(
                    out=wb(WB_HID + hc * T, T), in0=b.ap[:, 0:T],
                    in1=wb(WB_SIG, T), op=ALU.mult),
                    extra_waits=[ev_sg, tap_ev[0] if hc == 0 else None])
                ring_release(ent1, kb.now("pe"))
                ring_release(ent2, kb.now("pe"))
            if ti == 0:
                tap("hid0", lambda: wb(WB_HID, 4 * T), [128, 4 * T], BF16, [ev_hid])

            if ti == 0:
                fs_dy = {0: None, 1: None}
            for oc in range(NKC):
                o3, e_q3, ent3 = ring_push(f"q3_{oc}", SEG_Q3)
                bnkf = pb[oc % 2]
                mml = []
                for kc in range(NHC):
                    mml.append(lambda t, kc=kc, o=o3, b=bnkf: t.matmul(
                        b.ap[:, 0:T], ring[:, o + kc * 128:o + (kc + 1) * 128],
                        wb(WB_HID + kc * T, T),
                        start=(kc == 0), stop=(kc == NHC - 1)))
                mm_group(bnkf, mml, extra_waits=[e_q3, ev_hid])
                fo = (oc % 2) * T
                ev_f = bank_read("dve", bnkf, lambda e, oc=oc, b=bnkf, fo=fo:
                                 e.tensor_tensor(
                                     out=wf(WF["fsa"] + fo, T), in0=b.ap[:, 0:T],
                                     in1=attn_bf[:, oc * T:(oc + 1) * T], op=ALU.add),
                                 extra_waits=[fs_dy[oc % 2]])
                bt = pb[2 + oc % 2]
                mml = []
                for tc in range(NCH):
                    mml.append(lambda t, tc=tc, b=bt, fo=fo: t.matmul(
                        b.ap[:, tc * 128:(tc + 1) * 128],
                        wf(WF["fsa"] + fo + tc * 128, 128), ident_f[:],
                        is_transpose=True, start=(tc == 0), stop=(tc == NCH - 1)))
                mm_group(bt, mml, extra_waits=[ev_f])
                ev_fb = bank_read("dve", bt, lambda e, b=bt, fo=fo: e.tensor_copy(
                    wf(WF["fsb"] + fo, T), b.ap[:, 0:512]))
                dsty = bass.AP(y_h.tensor, y_h.offset + ti * T * D + oc * 128,
                               [[D, 128], [128 * D, NCH], [1, 128]])
                srcy = bass.AP(work_f32.ap().tensor,
                               work_f32.ap().offset + WF["fsb"] + fo,
                               [work_f32.ap().ap[0], [128, NCH], [1, 128]])
                fs_dy[oc % 2] = kb.op("sp", lambda e, d=dsty, s=srcy:
                                      e.dma_start(out=d, in_=s),
                                      waits=[ev_fb], dma="dy")
                ring_release(ent3, kb.now("pe"))

            if stop_after == "tile0":
                break

        kb.wait_only("sp", [Ev("dy", kb.semtot.get("dy", 0))])

        kb.check_deadlock()

        @block.tensor
        def _(eng):
            kb.emit("pe", eng, sems)

        @block.vector
        def _(eng):
            kb.emit("dve", eng, sems)

        @block.scalar
        def _(eng):
            kb.emit("act", eng, sems)

        @block.gpsimd
        def _(eng):
            kb.emit("pool", eng, sems)

        @block.sync
        def _(eng):
            kb.emit("sp", eng, sems)

    return nc, tap_out


def kernel(L0_attn_hidden_states, L0_attn_w_qkv, L0_attn_w_z, L0_attn_w_b,
           L0_attn_w_a, L0_attn_w_out, L0_attn_conv_w, L0_attn_dt_bias,
           L0_attn_A_log, L0_attn_norm_weight, L0_ffn_w1, L0_ffn_w2, L0_ffn_w3):
    x = np.ascontiguousarray(np.asarray(L0_attn_hidden_states, dtype=np.float32))
    common = {
        "w_qkv": np.ascontiguousarray(L0_attn_w_qkv, dtype=np.float32),
        "w_z": np.ascontiguousarray(L0_attn_w_z, dtype=np.float32),
        "w_b": np.ascontiguousarray(L0_attn_w_b, dtype=np.float32),
        "w_a": np.ascontiguousarray(L0_attn_w_a, dtype=np.float32),
        "w_out": np.ascontiguousarray(L0_attn_w_out, dtype=np.float32),
        "conv_w": np.ascontiguousarray(L0_attn_conv_w, dtype=np.float32),
        "dt_bias": np.ascontiguousarray(L0_attn_dt_bias, dtype=np.float32),
        "A_log": np.ascontiguousarray(L0_attn_A_log, dtype=np.float32),
        "norm_weight": np.ascontiguousarray(L0_attn_norm_weight, dtype=np.float32),
        "w1": np.ascontiguousarray(L0_ffn_w1, dtype=np.float32),
        "w2": np.ascontiguousarray(L0_ffn_w2, dtype=np.float32),
        "w3": np.ascontiguousarray(L0_ffn_w3, dtype=np.float32),
    }
    nc, _ = build_core()
    in_maps = [dict(common, x=np.ascontiguousarray(x[b])) for b in range(B)]
    res = run_bass_kernel_spmd(nc, in_maps, core_ids=list(range(B)))
    return np.stack([res.results[b]["y"] for b in range(B)], axis=0)


# version 13 (final)
# speedup vs baseline: 1.0791x; 1.0466x over previous
import numpy as np
from contextlib import ExitStack

import concourse.bass as bass
import concourse.mybir as mybir
from concourse.bass_utils import run_bass_kernel_spmd

F32 = mybir.dt.float32
BF16 = mybir.dt.bfloat16
AF = mybir.ActivationFunctionType
ALU = mybir.AluOpType

B, S, D = 8, 4096, 2048
HK, HV = 16, 32
DK = DV = 128
KDIM, VDIM = HK * DK, HV * DV
CONV_DIM, KSIZE = 2 * KDIM + VDIM, 4
HFF = 5632
NHC = HFF // 128
T = 512
NT = S // T
C = 128
NCH = T // C
NGRP = 8
NKC = D // 128
LNEPS, L2EPS, RMSEPS, FFEPS = 1e-5, 1e-6, 1e-6, 1e-8
RING_COLS = 18176

ENGINES = ("pe", "dve", "act", "pool", "sp")
ENG_SEM = {e: "s_" + e for e in ENGINES}


class Ev:
    __slots__ = ("sem", "val")

    def __init__(self, sem, val):
        self.sem, self.val = sem, val


class KB:
    def __init__(self):
        self.ops = {e: [] for e in ENGINES}
        self.semtot = {}
        self.waited = {e: {} for e in ENGINES}

    def now(self, eng):
        return Ev(ENG_SEM[eng], self.semtot.get(ENG_SEM[eng], 0))

    def _wait(self, eng, ev):
        if ev is None or ev.val <= 0:
            return
        cur = self.waited[eng].get(ev.sem, 0)
        if ev.val > cur:
            self.ops[eng].append(("w", ev.sem, ev.val))
            self.waited[eng][ev.sem] = ev.val

    def op(self, eng, fn, waits=(), sig=False, dma=None):
        for w in waits:
            self._wait(eng, w)
        sem = inc = None
        if dma is not None:
            sem, inc = dma, 16
        elif sig:
            sem, inc = ENG_SEM[eng], 1
        if sem is not None:
            self.semtot[sem] = self.semtot.get(sem, 0) + inc
            self.ops[eng].append(("c", fn, sem, inc))
            return Ev(sem, self.semtot[sem])
        self.ops[eng].append(("c", fn, None, 0))
        return None

    def wait_only(self, eng, evs):
        for w in evs:
            self._wait(eng, w)

    def check_deadlock(self):
        ptr = {e: 0 for e in ENGINES}
        cnt = {}
        progress = True
        while progress:
            progress = False
            for e in ENGINES:
                while ptr[e] < len(self.ops[e]):
                    it = self.ops[e][ptr[e]]
                    if it[0] == "w" and cnt.get(it[1], 0) < it[2]:
                        break
                    if it[0] == "c" and it[2] is not None:
                        cnt[it[2]] = cnt.get(it[2], 0) + it[3]
                    ptr[e] += 1
                    progress = True
        stuck = {e: ptr[e] for e in ENGINES if ptr[e] < len(self.ops[e])}
        if stuck:
            detail = {}
            for e in stuck:
                it = self.ops[e][ptr[e]]
                detail[e] = (f"op {ptr[e]}/{len(self.ops[e])} waiting "
                             f"{it[1]}>={it[2]} (now {cnt.get(it[1], 0)})")
            raise RuntimeError(f"build-time deadlock: {detail}")

    def emit(self, eng, engobj, sems):
        for item in self.ops[eng]:
            if item[0] == "w":
                engobj.wait_ge(sems[item[1]], item[2])
            else:
                inst = item[1](engobj)
                if item[2] is not None:
                    inst.then_inc(sems[item[2]], item[3])


class Bank:
    def __init__(self, ap):
        self.ap = ap
        self.wev = None
        self.rev = None


def build_core(taps=None, stop_after="full"):
    taps = taps or []
    tap_out = []

    nc = bass.Bass("TRN2", target_bir_lowering=False, debug=False)

    x_h = nc.dram_tensor("x", [S, D], F32, kind="ExternalInput").ap()
    wqkv_h = nc.dram_tensor("w_qkv", [D, CONV_DIM], F32, kind="ExternalInput").ap()
    wz_h = nc.dram_tensor("w_z", [D, VDIM], F32, kind="ExternalInput").ap()
    wb_h = nc.dram_tensor("w_b", [D, HV], F32, kind="ExternalInput").ap()
    wa_h = nc.dram_tensor("w_a", [D, HV], F32, kind="ExternalInput").ap()
    wout_h = nc.dram_tensor("w_out", [VDIM, D], F32, kind="ExternalInput").ap()
    convw_h = nc.dram_tensor("conv_w", [CONV_DIM, KSIZE], F32, kind="ExternalInput").ap()
    dtb_h = nc.dram_tensor("dt_bias", [HV], F32, kind="ExternalInput").ap()
    alog_h = nc.dram_tensor("A_log", [HV], F32, kind="ExternalInput").ap()
    nw_h = nc.dram_tensor("norm_weight", [DV], F32, kind="ExternalInput").ap()
    w1_h = nc.dram_tensor("w1", [HFF, D], F32, kind="ExternalInput").ap()
    w2_h = nc.dram_tensor("w2", [HFF, D], F32, kind="ExternalInput").ap()
    w3_h = nc.dram_tensor("w3", [D, HFF], F32, kind="ExternalInput").ap()
    y_h = nc.dram_tensor("y", [S, D], F32, kind="ExternalOutput").ap()

    SEG_QQKK = 16 * 512
    SEG_WOUT = 4 * 2048
    SEG_Q12 = 16 * 128
    SEG_Q3 = NHC * 128
    tape_cols = 0
    seg_off = {}

    def add_seg(name, cols):
        nonlocal tape_cols
        seg_off[name] = tape_cols
        tape_cols += cols

    for g in range(NGRP):
        add_seg(f"qqkk{g}", SEG_QQKK)
        add_seg(f"vvvv{g}", SEG_QQKK)
        add_seg(f"zzzz{g}", SEG_QQKK)
        add_seg(f"wout{g}", SEG_WOUT)
    for hc in range(NHC):
        add_seg(f"q1_{hc}", SEG_Q12)
        add_seg(f"q2_{hc}", SEG_Q12)
    for oc in range(NKC):
        add_seg(f"q3_{oc}", SEG_Q3)
    wtape = nc.dram_tensor("wtape", [128, tape_cols], BF16, kind="Internal").ap()

    kb = KB()

    with ExitStack() as ctx:
        def sb(name, shape, dt):
            return ctx.enter_context(nc.sbuf_tensor(name, shape, dt))

        xn_fm = sb("xn_fm", [128, NKC * T], BF16)
        attn_bf = sb("attn_bf", [128, NKC * T], BF16)
        x_tm = sb("x_tm", [128, 2 * NCH * D], BF16)
        work_bf = sb("work_bf", [128, 23296], BF16)
        work_f32 = sb("work_f32", [128, 6720], F32)
        ring = sb("ring", [128, RING_COLS], BF16)
        s_f32 = sb("s_f32", [128, HV * DV], F32)
        wba = sb("wba", [128, NKC * 64], BF16)
        conv_w = sb("conv_w_sb", [128, 64 * KSIZE], F32)
        conv_hist = sb("conv_hist", [128, 64 * (KSIZE - 1)], BF16)
        ident_b = sb("ident_b", [128, 128], BF16)
        ident_f = sb("ident_f", [128, 128], F32)
        two_i = sb("two_i", [128, 128], BF16)
        mask_i = sb("mask_i", [128, 128], F32)
        mask_s = sb("mask_s", [128, 128], F32)
        sel4 = sb("sel4", [4, 4 * 128], F32)
        ones_cb = sb("ones_cb", [128, 1], BF16)
        ones_rb = sb("ones_rb", [1, 128], BF16)
        ones_rf = sb("ones_rf", [1, 128], F32)
        eps_ln = sb("eps_ln", [128, 1], F32)
        eps_l2 = sb("eps_l2", [128, 1], F32)
        eps_rms = sb("eps_rms", [128, 1], F32)
        qsc_b = sb("qsc_b", [128, 1], F32)
        zero_c = sb("zero_c", [128, 1], F32)
        one_c = sb("one_c", [128, 1], F32)
        nw_c = sb("nw_c", [128, 1], F32)
        dtb_c = sb("dtb_c", [64, 1], F32)
        nega_c = sb("nega_c", [64, 1], F32)
        ba_sb = sb("ba_sb", [64, 3 * T], F32)
        gb_tmh = sb("gb_tmh", [128, NCH * 64], F32)
        exps_t = sb("exps_t", [128, NCH * 32], F32)
        bl_t = sb("bl_t", [128, NCH * 32], F32)
        st4 = sb("st4", [128, 16], F32)
        mrow = sb("mrow", [4, 256], F32)
        mbsb = sb("mbsb", [128, NCH * 256], BF16)
        gam_sb = sb("gam_sb", [128, 4], F32)
        nth_sb = sb("nth_sb", [128, 4], F32)
        absc = sb("absc", [128, NHC], F32)
        gred = sb("gred", [128, 128], F32)

        WB = {}
        _o = 0
        for nm, w in [("qg", 2 * T), ("kg", 2 * T), ("vg", 4 * T), ("zg", 4 * T),
                      ("kkkq", 2 * NCH * 256), ("di", 4 * NCH * C),
                      ("wt", 4 * NCH * C), ("tt", 4 * NCH * C), ("xt0", NCH * C),
                      ("vt", 4 * NCH * C), ("kt", 2 * NCH * C),
                      ("at", NCH * C), ("g_nw", NCH * C), ("z1", NCH * C),
                      ("u_sb", 4 * C), ("gtd", 4 * C), ("dl", 4 * C), ("dl2", 4 * C),
                      ("sbf", 4 * C)]:
            WB[nm] = _o
            _o += w
        assert _o <= 23296, _o
        WB["gated"] = WB["kkkq"]
        WB["xt1"] = WB["at"]
        WB["qn"] = WB["u_sb"]
        WB["sqb"] = WB["gtd"]
        WB["cvo"] = WB["dl"]
        WB["mxp"] = WB["dl2"]
        WB_HID = 0
        WB_SIG = 22528
        assert NHC * T <= 22528 + 512

        PRE_Q = 0
        PRE_Q2 = 8192
        PRE_T = 16384
        WF = {}
        _o = 0
        for nm, w in [("e1", NCH * C), ("e2", NCH * C), ("osb", 4 * C), ("g1", 4 * C),
                      ("u1", 4 * C), ("sq", D), ("fsa", 2 * T), ("fsb", 2 * T),
                      ("ms", 8), ("rstd", 8), ("lcc", 16)]:
            WF[nm] = _o
            _o += w
        assert _o <= 6720, _o
        PRE_SQ = 0

        pb = [Bank(ctx.enter_context(nc.psum_tensor(f"pb{i}", [128, 512], F32)).ap())
              for i in range(7)]
        pt = Bank(ctx.enter_context(nc.psum_tensor("pt7", [128, 1024], BF16)).ap())

        sem_names = list(ENG_SEM.values()) + ["dw", "dx", "dy", "dpre"]
        sems = {n: ctx.enter_context(nc.semaphore(n)) for n in sem_names}
        block = ctx.enter_context(nc.Block())

        def mm_group(bank, mm_list, extra_waits=()):
            waits = [w for w in extra_waits if w is not None]
            if bank.rev is not None:
                waits.append(bank.rev)
            n = len(mm_list)
            for i, fn in enumerate(mm_list):
                if i < n - 1:
                    kb.op("pe", fn, waits=waits if i == 0 else ())
                else:
                    bank.wev = kb.op("pe", fn, waits=waits if i == 0 else (), sig=True)
            return bank.wev

        def bank_read(eng, bank, fn, extra_waits=()):
            waits = [bank.wev] + [w for w in extra_waits if w is not None]
            ev = kb.op(eng, fn, waits=waits, sig=True)
            bank.rev = ev
            return ev

        ring_state = {"cursor": 0, "occ": []}

        def ring_push(seg_name, cols):
            cur = ring_state["cursor"]
            if cur + cols > RING_COLS:
                cur = 0
            start, end = cur, cur + cols
            waits, keep = [], []
            for ent in ring_state["occ"]:
                if ent[0] < end and ent[1] > start:
                    assert ent[2] is not None, (
                        f"ring overlap with unreleased segment {ent} pushing "
                        f"{seg_name} at {start}:{end}")
                    waits.append(ent[2])
                else:
                    keep.append(ent)
            ring_state["occ"] = keep
            ent = [start, end, None]
            ring_state["occ"].append(ent)
            toff = seg_off[seg_name]
            ev = kb.op("sp",
                       lambda e, a=toff, n=cols, c=start: e.dma_start(
                           out=ring[:, c:c + n], in_=wtape[:, a:a + n]),
                       waits=waits, dma="dw")
            ring_state["cursor"] = end
            return start, ev, ent

        def ring_release(ent, ev):
            ent[2] = ev

        tap_ev = [None]

        def tap(name, ap_fn, shape, dt, waits):
            if name in taps:
                th = nc.dram_tensor("tap_" + name, shape, dt, kind="ExternalOutput").ap()
                tap_out.append("tap_" + name)
                tap_ev[0] = kb.op("sp", lambda e: e.dma_start(out=th, in_=ap_fn()),
                                  waits=waits, dma="dy")

        def wb(off, cols):
            return work_bf[:, off:off + cols]

        def wf(off, cols):
            return work_f32[:, off:off + cols]

        def wb3(off, n, step, cols):
            a = work_bf.ap()
            return bass.AP(a.tensor, a.offset + off, [a.ap[0], [step, n], [1, cols]])

        def wf3(off, n, step, cols):
            a = work_f32.ap()
            return bass.AP(a.tensor, a.offset + off, [a.ap[0], [step, n], [1, cols]])

        def bcast3(tensor, off, n, step, cols):
            a = tensor.ap()
            return bass.AP(a.tensor, a.offset + off, [a.ap[0], [step, n], [0, cols]])

        def aff(out_ap, fill, pattern, cm, base=0, cmp=ALU.not_equal):
            return lambda e: e.affine_select(out=out_ap, in_=out_ap, compare_op=cmp,
                                             fill=fill, base=base, pattern=pattern,
                                             channel_multiplier=cm)

        kb.op("pool", lambda e: e.memset(ident_b[:], 0.0))
        kb.op("pool", aff(ident_b[:], 1.0, [[-1, 128]], 1))
        kb.op("pool", lambda e: e.memset(ident_f[:], 0.0))
        kb.op("pool", aff(ident_f[:], 1.0, [[-1, 128]], 1))
        kb.op("pool", lambda e: e.memset(two_i[:], 0.0))
        kb.op("pool", aff(two_i[:], 2.0, [[-1, 128]], 1))
        kb.op("pool", lambda e: e.memset(mask_i[:], 0.0))
        kb.op("pool", aff(mask_i[:], -1e9, [[1, 128]], -1, cmp=ALU.is_ge))
        kb.op("pool", lambda e: e.memset(mask_s[:], 0.0))
        kb.op("pool", aff(mask_s[:], 1e9, [[-1, 128]], 1, cmp=ALU.is_gt))
        kb.op("pool", lambda e: e.memset(sel4[:], 0.0))
        kb.op("pool", aff(sel4[:], 1.0, [[-1, 4], [0, 128]], 1))
        for t_, v_ in ((ones_cb, 1.0), (ones_rb, 1.0), (ones_rf, 1.0),
                       (eps_ln, LNEPS), (eps_l2, L2EPS), (eps_rms, RMSEPS),
                       (qsc_b, float(np.log(DK ** -0.5))), (zero_c, 0.0),
                       (one_c, 1.0)):
            kb.op("pool", lambda e, t=t_, v=v_: e.memset(t[:], v))
        kb.op("pool", lambda e: e.memset(conv_hist[:], 0.0))
        kb.op("pool", lambda e: e.memset(s_f32[:], 0.0))
        ev_nw = kb.op("pool", lambda e: e.dma_start(out=nw_c[:], in_=nw_h[:, None]),
                      dma="dpre")
        kb.op("pool", lambda e: e.dma_start(out=dtb_c[32:64, :], in_=dtb_h[:, None]),
              dma="dpre")
        ev_al = kb.op("pool", lambda e: e.dma_start(out=nega_c[32:64, :],
                                                    in_=alog_h[:, None]), dma="dpre")
        ev_cw = kb.op("pool",
                      lambda e: e.dma_start(
                          out=conv_w.ap().rearrange("p (o j) -> p o j", o=64),
                          in_=convw_h.rearrange("(o p) j -> p o j", p=128)),
                      dma="dpre")
        ev_consts = kb.op("pool", lambda e: e.memset(absc[:], 0.0), sig=True)

        ev_na1 = kb.op("act", lambda e: e.activation(out=nega_c[32:64, :],
                                                     in_=nega_c[32:64, :], func=AF.Exp),
                       waits=[ev_al], sig=True)
        ev_nega = kb.op("dve", lambda e: e.tensor_scalar_mul(
            nega_c[32:64, :], nega_c[32:64, :], -1.0), waits=[ev_na1], sig=True)

        for g in range(NGRP if stop_after != "consts" else 0):
            so = seg_off[f"qqkk{g}"]
            for half, c0 in ((0, 2 * g * 128), (1, (16 + 2 * g) * 128)):
                src = bass.AP(wqkv_h.tensor, wqkv_h.offset + c0,
                              [[128 * CONV_DIM, NKC], [CONV_DIM, 128], [1, 256]])
                dst = bass.AP(wtape.tensor, wtape.offset + so + half * 256,
                              [[512, NKC], [tape_cols, 128], [1, 256]])
                kb.op("pool", lambda e, s=src, d=dst: e.dma_start(out=d, in_=s),
                      dma="dpre")
            so = seg_off[f"vvvv{g}"]
            src = bass.AP(wqkv_h.tensor, wqkv_h.offset + (32 + 4 * g) * 128,
                          [[128 * CONV_DIM, NKC], [CONV_DIM, 128], [1, 512]])
            dst = bass.AP(wtape.tensor, wtape.offset + so,
                          [[512, NKC], [tape_cols, 128], [1, 512]])
            kb.op("pool", lambda e, s=src, d=dst: e.dma_start(out=d, in_=s), dma="dpre")
            so = seg_off[f"zzzz{g}"]
            src = bass.AP(wz_h.tensor, wz_h.offset + g * 512,
                          [[128 * VDIM, NKC], [VDIM, 128], [1, 512]])
            dst = bass.AP(wtape.tensor, wtape.offset + so,
                          [[512, NKC], [tape_cols, 128], [1, 512]])
            kb.op("pool", lambda e, s=src, d=dst: e.dma_start(out=d, in_=s), dma="dpre")
        ev_wba = None
        for (wsrc, half) in ((wb_h, 0), (wa_h, 1)):
            src = bass.AP(wsrc.tensor, wsrc.offset,
                          [[HV, 128], [128 * HV, NKC], [1, 32]])
            a = wba.ap()
            dst = bass.AP(a.tensor, a.offset + half * 32,
                          [a.ap[0], [64, NKC], [1, 32]])
            ev_wba = kb.op("pool", lambda e, s=src, d=dst: e.dma_start(out=d, in_=s),
                           dma="dpre")

        ev_prev_tape = None
        for g in range(NGRP if stop_after != "consts" else 0):
            for kc in range(4):
                r0 = g * 512 + kc * 128
                src = bass.AP(wout_h.tensor, wout_h.offset + r0 * D, [[D, 128], [1, D]])
                evl = kb.op("pool", lambda e, s=src: e.dma_start(out=wb(PRE_Q, D), in_=s),
                            waits=[ev_prev_tape] if ev_prev_tape else [], dma="dpre")
                ev_sc = kb.op("dve", lambda e: e.tensor_scalar_mul(
                    wb(PRE_Q, D), wb(PRE_Q, D), nw_c[:]), waits=[evl, ev_nw], sig=True)
                so = seg_off[f"wout{g}"] + kc * 2048
                ev_prev_tape = kb.op("sp", lambda e, a=so: e.dma_start(
                    out=wtape[:, a:a + 2048], in_=wb(PRE_Q, D)),
                    waits=[ev_sc], dma="dw")

        ev_gam_row = {}
        _gl = ((1, w1_h, NHC, D), (2, w2_h, NHC, D), (3, w3_h, NKC, HFF)) \
            if stop_after not in ("consts", "tape") else ()
        for mi, wh, nslab, cols in _gl:
            ev_abs = None
            for sl in range(nslab):
                src = bass.AP(wh.tensor, wh.offset + sl * 128 * cols,
                              [[cols, 128], [1, cols]])
                po = (sl % 2) * 2048 if cols == D else 0
                war = ev_abs if (cols != D or sl >= 2) else None
                evl = kb.op("pool", lambda e, s=src, c=cols, p=po: e.dma_start(
                    out=wf(PRE_SQ + p, c), in_=s),
                    waits=[w for w in (war,) if w], dma="dpre")
                ev_abs = kb.op("act", lambda e, j=sl, c=cols, p=po: e.activation(
                    out=wf(PRE_SQ + p, c), in_=wf(PRE_SQ + p, c), func=AF.Abs,
                    accum_out=absc[:, j:j + 1]), waits=[evl, ev_consts], sig=True)
            ev_red = kb.op("act", lambda e, m=mi, n=nslab: e.activation(
                out=gred[:, 0:n], in_=absc[:, 0:n], func=AF.Copy,
                accum_out=st4[:, 12 + m:13 + m]), waits=[ev_abs], sig=True)
            bnk = pb[2]
            mm_group(bnk, [lambda t, m=mi, b=bnk: t.matmul(
                b.ap[0:1, 0:128], st4[:, 12 + m:13 + m], ident_f[:],
                is_transpose=True, start=True, stop=True)],
                extra_waits=[ev_red])
            ev_gam_row[mi] = bank_read("act", bnk, lambda e, m=mi, b=bnk: e.activation(
                out=gred[0:1, 0:128], in_=b.ap[0:1, 0:128], func=AF.Copy,
                accum_out=st4[0:1, 8 + m:9 + m]))
        ev_th = None
        for mi in ((1, 2, 3) if stop_after not in ("consts", "tape") else ()):
            numel = HFF * D
            ev1 = kb.op("dve", lambda e, m=mi, n=numel: e.tensor_scalar(
                out=st4[0:1, 8 + m:9 + m], in0=st4[0:1, 8 + m:9 + m],
                scalar1=0.5 / n, scalar2=0.5 * FFEPS, op0=ALU.mult, op1=ALU.add),
                waits=[ev_gam_row[mi]], sig=True)
            bnk = pb[2]
            mm_group(bnk, [lambda t, m=mi, b=bnk: t.matmul(
                b.ap[:, 0:1], ones_rf[:], st4[0:1, 8 + m:9 + m],
                start=True, stop=True)], extra_waits=[ev1])
            ev2 = bank_read("dve", bnk, lambda e, m=mi, b=bnk: e.tensor_copy(
                gam_sb[:, m:m + 1], b.ap[:, 0:1]))
            ev_th = kb.op("dve", lambda e, m=mi: e.tensor_scalar_mul(
                nth_sb[:, m:m + 1], gam_sb[:, m:m + 1], -1.0), waits=[ev2], sig=True)
        tap("gamma_th", lambda: gam_sb[:, 0:4], [128, 4], F32, [ev_th])
        tap("pre_st4", lambda: st4[:, 0:16], [128, 16], F32, [ev_th])
        tap("pre_absc", lambda: absc[:, 0:NHC], [128, NHC], F32, [ev_th])

        ev_presq_free = [ev_gam_row.get(3), ev_gam_row.get(3)]
        ev_stage_pe = [None, None]
        ev_tape_slot = [None, None]
        par_ct = [0]

        def quant_and_transpose(src_ap, cols, mi, nblk, seg):
            nonlocal ev_prev_tape
            if cols == D:
                par = par_ct[0] % 2
                par_ct[0] += 1
                oq, oq2, ot = PRE_Q + par * 2048, PRE_Q2 + par * 2048, \
                    PRE_T + par * 2560
                presq = PRE_SQ + par * 2048
            else:
                par = 0
                oq, oq2, ot = PRE_Q, PRE_Q2, PRE_T
                presq = PRE_SQ
            waitl = [ev_presq_free[par]]
            if cols != D:
                waitl += [ev_presq_free[1 - par], ev_stage_pe[1 - par],
                          ev_tape_slot[1 - par]]
            evl = kb.op("pool", lambda e, s=src_ap, c=cols, o=presq: e.dma_start(
                out=wf(o, c), in_=s),
                waits=[w for w in waitl if w], dma="dpre")
            kb.op("dve", lambda e, c=cols, m=mi, o=presq, q=oq:
                  e.tensor_single_scalar(
                      wb(q, c), wf(o, c), gam_sb[:, m:m + 1], ALU.is_gt),
                  waits=[w for w in (evl, ev_th, ev_stage_pe[par]) if w])
            ev_presq_free[par] = kb.op(
                "dve", lambda e, c=cols, m=mi, o=presq, q=oq2:
                e.tensor_single_scalar(
                    wb(q, c), wf(o, c), nth_sb[:, m:m + 1], ALU.is_lt), sig=True)
            ev_q = kb.op("dve", lambda e, c=cols, q=oq, q2=oq2: e.tensor_tensor(
                out=wb(q, c), in0=wb(q, c), in1=wb(q2, c),
                op=ALU.subtract), sig=True)
            ev_ev = None
            tape_war = ev_tape_slot[par]
            for g0 in range(0, nblk, 8):
                gn = min(8, nblk - g0)
                mml = []
                for j in range(gn):
                    blkc = oq + (g0 + j) * 128
                    mml.append(lambda t, bc=blkc, jj=j, gn=gn: t.matmul(
                        pt.ap[:, jj * 128:(jj + 1) * 128], wb(bc, 128), ident_b[:],
                        is_transpose=True, start=(jj == 0), stop=(jj == gn - 1)))
                mm_group(pt, mml, extra_waits=[ev_q, ev_consts])
                ev_ev = bank_read("dve", pt, lambda e, g0=g0, gn=gn, o=ot:
                                  e.tensor_copy(
                                      wb(o + g0 * 128, gn * 128),
                                      pt.ap[:, 0:gn * 128]),
                                  extra_waits=[tape_war] if g0 == 0 else ())
            ev_stage_pe[par] = kb.now("pe")
            ev = kb.op("sp", lambda e, a=seg_off[seg], n=nblk, o=ot: e.dma_start(
                out=wtape[:, a:a + n * 128], in_=wb(o, n * 128)),
                waits=[ev_ev], dma="dw")
            ev_tape_slot[par] = ev
            ev_prev_tape = ev

        _ql = ((1, w1_h), (2, w2_h)) if stop_after not in ("consts", "tape") else ()
        for mi, wh in _ql:
            for hb in range(NHC):
                src = bass.AP(wh.tensor, wh.offset + hb * 128 * D, [[D, 128], [1, D]])
                quant_and_transpose(src, D, mi, NKC, f"q{mi}_{hb}")
        for ob in range(NKC if stop_after not in ("consts", "tape") else 0):
            src = bass.AP(w3_h.tensor, w3_h.offset + ob * 128 * HFF,
                          [[HFF, 128], [1, HFF]])
            quant_and_transpose(src, HFF, 3, NHC, f"q3_{ob}")

        ev_tape_done = Ev("dw", kb.semtot.get("dw", 0))
        ev_pre_dve = kb.now("dve")
        ev_pre_pe = kb.now("pe")
        ev_pre_act = kb.now("act")

        x_free = [None, None]

        _nt = 0 if stop_after in ("consts", "tape", "pre") else NT
        if stop_after in ("xload", "lnstats", "lnT", "ln1",
                          "ba1", "ba2", "ba3"):
            _nt = 1
        for ti in range(_nt):
            ring_state["cursor"] = 0
            png = ti % 2
            xoff = png * NCH * D
            a = x_tm.ap()
            dst_x = bass.AP(a.tensor, a.offset + xoff, [a.ap[0], [D, NCH], [1, D]])
            src_x = bass.AP(x_h.tensor, x_h.offset + ti * T * D,
                            [[D, 128], [128 * D, NCH], [1, D]])
            wv = [x_free[png]] if x_free[png] else []
            ev_x = kb.op("pool", lambda e, s=src_x, d=dst_x: e.dma_start(out=d, in_=s),
                         waits=wv, dma="dx")
            if ti == 0:
                tap("x_tm", lambda: x_tm[:, 0:NCH * D], [128, NCH * D], BF16, [ev_x])
            if stop_after == "xload":
                break

            ev_st = None
            first_w = [ev_x]
            if ti == 0:
                first_w += [ev_pre_act, ev_pre_dve, ev_pre_pe, ev_consts]
            else:
                first_w += [kb.now("dve"), kb.now("pe")]
            for tc in range(NCH):
                kb.op("act", lambda e, tc=tc, xo=xoff: e.activation(
                    out=wf(WF["sq"], D), in_=x_tm[:, xo + tc * D:xo + (tc + 1) * D],
                    func=AF.Square, accum_out=st4[:, 4 + tc:5 + tc]),
                    waits=first_w if tc == 0 else ())
                ev_st = kb.op("act", lambda e, tc=tc, xo=xoff: e.activation(
                    out=wf(WF["sq"], D), in_=x_tm[:, xo + tc * D:xo + (tc + 1) * D],
                    func=AF.Copy, accum_out=st4[:, tc:tc + 1]), sig=True)
            e1w = [ev_st]
            if ti == 0:
                e1w += [ev_pre_pe, ev_tape_done]
            e1 = kb.op("dve", lambda e: e.tensor_scalar_mul(st4[:, 0:8], st4[:, 0:8],
                                                            1.0 / D),
                       waits=e1w, sig=True)
            e2 = kb.op("dve", lambda e: e.tensor_tensor(
                out=st4[:, 8:12], in0=st4[:, 0:4], in1=st4[:, 0:4], op=ALU.mult),
                waits=[e1], sig=True)
            e3 = kb.op("dve", lambda e: e.tensor_tensor(
                out=st4[:, 8:12], in0=st4[:, 4:8], in1=st4[:, 8:12], op=ALU.subtract),
                waits=[e2], sig=True)
            e4 = kb.op("act", lambda e: e.activation(
                out=st4[:, 8:12], in_=st4[:, 8:12], func=AF.Sqrt, bias=eps_ln[:]),
                waits=[e3], sig=True)
            e5 = kb.op("dve", lambda e: e.reciprocal(st4[:, 12:16], st4[:, 8:12]),
                       waits=[e4], sig=True)
            ev_xn_tm = []
            for tc in range(NCH):
                ev = kb.op("dve", lambda e, tc=tc, xo=xoff: e.tensor_scalar(
                    out=x_tm[:, xo + tc * D:xo + (tc + 1) * D],
                    in0=x_tm[:, xo + tc * D:xo + (tc + 1) * D],
                    scalar1=st4[:, tc:tc + 1], scalar2=st4[:, 12 + tc:13 + tc],
                    op0=ALU.subtract, op1=ALU.mult), waits=[e5], sig=True)
                ev_xn_tm.append(ev)
            if ti == 0:
                tap("xn_tm", lambda: x_tm[:, 0:NCH * D], [128, NCH * D], BF16,
                    [ev_xn_tm[-1]])
            if stop_after == "lnstats":
                break
            mm_group(pb[3], [
                lambda t: t.matmul(pb[3].ap[0:4, 0:128], st4[:, 0:4], ident_f[:],
                                   is_transpose=True, start=True, stop=False),
                lambda t: t.matmul(pb[3].ap[0:4, 128:256], st4[:, 8:12], ident_f[:],
                                   is_transpose=True, start=False, stop=True)],
                extra_waits=[e5])
            ev_mr = bank_read("dve", pb[3], lambda e: e.tensor_copy(
                mrow[:], pb[3].ap[0:4, 0:256]))
            ev_mbsb = None
            for bk, tcs in ((pb[3], (0, 1)), (pb[4], (2, 3))):
                mml = []
                for j, tc in enumerate(tcs):
                    mml.append(lambda t, tc=tc, j=j, b=bk: t.matmul(
                        b.ap[:, j * 256:(j + 1) * 256],
                        sel4[:, tc * 128:(tc + 1) * 128], mrow[:],
                        start=(j == 0), stop=(j == 1)))
                mm_group(bk, mml, extra_waits=[ev_mr])
                ev_mbsb = bank_read("dve", bk, lambda e, b=bk, t0=tcs[0]: e.tensor_copy(
                    mbsb[:, t0 * 256:t0 * 256 + 512], b.ap[:, 0:512]))

            ev_xn = None
            for tc in range(NCH):
                for g0 in range(0, NKC, 8):
                    mml = []
                    for j in range(8):
                        fc = g0 + j
                        mml.append(lambda t, fc=fc, j=j, tc=tc, xo=xoff: t.matmul(
                            pt.ap[:, j * 128:(j + 1) * 128],
                            x_tm[:, xo + tc * D + fc * 128:xo + tc * D + fc * 128 + 128],
                            ident_b[:], is_transpose=True,
                            start=(j == 0), stop=(j == 7)))
                    mm_group(pt, mml, extra_waits=[ev_xn_tm[tc]])
                    an = xn_fm.ap()
                    dst = bass.AP(an.tensor, an.offset + g0 * T + tc * 128,
                                  [an.ap[0], [T, 8], [1, 128]])
                    ev_xn = bank_read("dve", pt, lambda e, d=dst: e.tensor_copy(
                        d, pt.ap.rearrange("p (a b) -> p a b", a=8)))
            x_free[png] = ev_xn
            if ti == 0:
                tap("xn_fm", lambda: xn_fm[:, 0:NKC * T], [128, NKC * T], BF16, [ev_xn])
            if stop_after == "lnT":
                break

            bnk = pb[5]
            mml = []
            for kc in range(NKC):
                mml.append(lambda t, kc=kc, b=bnk: t.matmul(
                    b.ap[0:64, 0:T], wba[:, kc * 64:(kc + 1) * 64],
                    xn_fm[:, kc * T:(kc + 1) * T],
                    start=(kc == 0), stop=(kc == NKC - 1)))
            mm_group(bnk, mml, extra_waits=[ev_xn, ev_wba])
            evb = bank_read("act", bnk, lambda e, b=bnk: e.activation(
                out=ba_sb[0:32, 0:T], in_=b.ap[0:32, 0:T], func=AF.Sigmoid))
            eva1 = kb.op("act", lambda e, b=bnk: e.activation(
                out=ba_sb[32:64, T:2 * T], in_=b.ap[32:64, 0:T], func=AF.Abs,
                bias=dtb_c[32:64, :]), waits=[bnk.wev], sig=True)
            eva2 = kb.op("act", lambda e: e.activation(
                out=ba_sb[32:64, T:2 * T], in_=ba_sb[32:64, T:2 * T], func=AF.Exp,
                scale=-1.0), waits=[eva1], sig=True)
            eva3 = kb.op("act", lambda e: e.activation(
                out=ba_sb[32:64, T:2 * T], in_=ba_sb[32:64, T:2 * T], func=AF.Ln,
                bias=one_c[32:64, :]), waits=[eva2], sig=True)
            eva4 = kb.op("dve", lambda e, b=bnk: e.tensor_scalar(
                out=ba_sb[32:64, 0:T], in0=b.ap[32:64, 0:T],
                scalar1=dtb_c[32:64, :], scalar2=0.0, op0=ALU.add, op1=ALU.max),
                waits=[bnk.wev], sig=True)
            bnk.rev = eva4
            eva5 = kb.op("dve", lambda e: e.tensor_tensor(
                out=ba_sb[32:64, 0:T], in0=ba_sb[32:64, 0:T],
                in1=ba_sb[32:64, T:2 * T], op=ALU.add), waits=[eva3, eva4], sig=True)
            eva6 = kb.op("dve", lambda e: e.tensor_scalar_mul(
                ba_sb[32:64, 0:T], ba_sb[32:64, 0:T], nega_c[32:64, :]),
                waits=[eva5, ev_nega], sig=True)
            if ti == 0:
                tap("ba_g", lambda: ba_sb[0:64, 0:T], [64, T], F32, [eva6, evb])
            if stop_after == "ba1":
                break
            ev_gamma = None
            for cc in range(NCH):
                ev_gamma = kb.op("dve", lambda e, cc=cc: e.tensor_tensor_scan(
                    out=ba_sb[32:64, 2 * T + cc * C:2 * T + (cc + 1) * C],
                    data0=ba_sb[32:64, cc * C:(cc + 1) * C],
                    data1=ba_sb[32:64, cc * C:(cc + 1) * C],
                    initial=0.0, op0=ALU.add, op1=ALU.bypass),
                    waits=[eva6], sig=True)
            if ti == 0:
                tap("ba_gam", lambda: ba_sb[0:64, 2 * T:3 * T], [64, T], F32,
                    [ev_gamma])
            if stop_after == "ba2":
                break
            mml = []
            for cc in range(NCH):
                mml.append(lambda t, cc=cc: t.matmul(
                    pb[5].ap[:, cc * 32:(cc + 1) * 32],
                    ba_sb[32:64, 2 * T + cc * C:2 * T + (cc + 1) * C],
                    ident_f[32:64, 32:64], is_transpose=True,
                    start=(cc == 0), stop=(cc == NCH - 1)))
            mm_group(pb[5], mml, extra_waits=[evb, ev_gamma])
            ev_gb1 = bank_read("dve", pb[5], lambda e: e.tensor_copy(
                bass.AP(gb_tmh.ap().tensor, gb_tmh.ap().offset,
                        [gb_tmh.ap().ap[0], [64, NCH], [1, 32]]),
                pb[5].ap[:, 0:NCH * 32].rearrange("p (a b) -> p a b", a=NCH)))
            mml = []
            for cc in range(NCH):
                mml.append(lambda t, cc=cc: t.matmul(
                    pb[6].ap[:, cc * 32:(cc + 1) * 32],
                    ba_sb[0:32, cc * C:(cc + 1) * C],
                    ident_f[0:32, 0:32], is_transpose=True,
                    start=(cc == 0), stop=(cc == NCH - 1)))
            mm_group(pb[6], mml, extra_waits=[evb, ev_gamma])
            ev_gb2 = bank_read("dve", pb[6], lambda e: e.tensor_copy(
                bass.AP(gb_tmh.ap().tensor, gb_tmh.ap().offset + 32,
                        [gb_tmh.ap().ap[0], [64, NCH], [1, 32]]),
                pb[6].ap[:, 0:NCH * 32].rearrange("p (a b) -> p a b", a=NCH)))
            ev_gb = ev_gb2
            if ti == 0:
                tap("gb_early", lambda: gb_tmh[:, 0:NCH * 64], [128, NCH * 64], F32,
                    [ev_gb])
            if stop_after == "ba3":
                break
            ae, ag = exps_t.ap(), gb_tmh.ap()
            ev_exps = kb.op("act", lambda e: e.activation(
                out=bass.AP(ae.tensor, ae.offset, [ae.ap[0], [32, NCH], [1, 32]]),
                in_=bass.AP(ag.tensor, ag.offset, [ag.ap[0], [64, NCH], [1, 32]]),
                func=AF.Exp), waits=[ev_gb], sig=True)
            ab = bl_t.ap()
            ev_bl = kb.op("dve", lambda e: e.tensor_tensor(
                out=bass.AP(ab.tensor, ab.offset, [ab.ap[0], [32, NCH], [1, 32]]),
                in0=bass.AP(ae.tensor, ae.offset, [ae.ap[0], [32, NCH], [1, 32]]),
                in1=bass.AP(ag.tensor, ag.offset + 32, [ag.ap[0], [64, NCH], [1, 32]]),
                op=ALU.mult), waits=[ev_exps], sig=True)
            if ti == 0:
                tap("gb_tmh", lambda: gb_tmh[:, 0:NCH * 64], [128, NCH * 64], F32,
                    [ev_gb])
            if stop_after in ("ln", "ln1"):
                break

            carry = {"silu_pe": kb.now("pe"),
                     "bcast_pe": kb.now("pe"),
                     "qk_dve": kb.now("dve"),
                     "cvo_act": kb.now("act"),
                     "vg_pe": kb.now("pe"),
                     "qkg_pe": kb.now("pe"),
                     "zg_dve": kb.now("dve"),
                     "sbf_pe": kb.now("pe"),
                     "gated_pe": kb.now("pe")}

            for g in range(NGRP):
                o_qk, ev_qk, ent_qk = ring_push(f"qqkk{g}", SEG_QQKK)
                o_v, ev_v, ent_v = ring_push(f"vvvv{g}", SEG_QQKK)

                def proj_mm_conv(ring_off, ring_sub, qn_off, sq_off, bnk_i,
                                 ready_ev, conv_ch, kind, dst_off=None):
                    bnkp = pb[bnk_i]
                    mml = []
                    for kc in range(NKC):
                        mml.append(lambda t, kc=kc, b=bnkp, ro=ring_off,
                                   rs=ring_sub: t.matmul(
                            b.ap[:, 0:T],
                            ring[:, ro + kc * 512 + rs:ro + kc * 512 + rs + 128],
                            xn_fm[:, kc * T:(kc + 1) * T],
                            start=(kc == 0), stop=(kc == NKC - 1)))
                    mm_group(bnkp, mml, extra_waits=[ready_ev, ev_xn])
                    hco = conv_ch * 3
                    kb.op("dve", lambda e, hco=hco: e.tensor_copy(
                        wb(WB["mxp"], 3), conv_hist[:, hco:hco + 3]),
                        waits=[ev_consts, carry["cvo_act"], carry["sbf_pe"],
                               tap_ev[0]])
                    bank_read("dve", bnkp, lambda e, b=bnkp: e.tensor_copy(
                        wb(WB["mxp"] + 3, T), b.ap[:, 0:T]))
                    kb.op("dve", lambda e, hco=hco: e.tensor_copy(
                        conv_hist[:, hco:hco + 3], wb(WB["mxp"] + T, 3)))
                    if conv_ch == 0:
                        tap("mxp0", lambda: wb(WB["mxp"], T + 3), [128, T + 3],
                            BF16, [kb.now("dve")])
                    kb.op("dve", lambda e, cc_=conv_ch: e.tensor_scalar(
                        out=wb(WB["cvo"], T), in0=wb(WB["mxp"], T),
                        scalar1=conv_w[:, cc_ * 4:cc_ * 4 + 1], scalar2=None,
                        op0=ALU.mult), waits=[ev_cw])
                    for j in (1, 2):
                        kb.op("dve", lambda e, j=j, cc_=conv_ch: e.scalar_tensor_tensor(
                            out=wb(WB["cvo"], T), in0=wb(WB["mxp"] + j, T),
                            scalar=conv_w[:, cc_ * 4 + j:cc_ * 4 + j + 1],
                            in1=wb(WB["cvo"], T), op0=ALU.mult, op1=ALU.add))
                    ev_cv = kb.op("dve", lambda e, cc_=conv_ch: e.scalar_tensor_tensor(
                        out=wb(WB["cvo"], T), in0=wb(WB["mxp"] + 3, T),
                        scalar=conv_w[:, cc_ * 4 + 3:cc_ * 4 + 4],
                        in1=wb(WB["cvo"], T), op0=ALU.mult, op1=ALU.add), sig=True)
                    if kind == "v":
                        ev = kb.op("act", lambda e, d=dst_off: e.activation(
                            out=wb(d, T), in_=wb(WB["cvo"], T), func=AF.Silu),
                            waits=[ev_cv, carry["vg_pe"]], sig=True)
                        carry["cvo_act"] = ev
                        return ev
                    ev_s = kb.op("act", lambda e, q=qn_off: e.activation(
                        out=wb(q, T), in_=wb(WB["cvo"], T), func=AF.Silu),
                        waits=[ev_cv, carry["qk_dve"], carry["silu_pe"],
                               carry["sbf_pe"]], sig=True)
                    carry["cvo_act"] = ev_s
                    ev_sq = kb.op("act", lambda e, q=qn_off, s=sq_off: e.activation(
                        out=wb(s, T), in_=wb(q, T), func=AF.Square),
                        waits=[ev_s], sig=True)
                    return ev_sq

                def proj_l2_finish(qn_off, sq_off, dst_off, kind, ev_sq):
                    b2 = pb[6]
                    mm_group(b2, [lambda t, b=b2, s=sq_off: t.matmul(
                        b.ap[0:1, 0:T], ones_cb[:], wb(s, T),
                        start=True, stop=True)], extra_waits=[ev_sq])
                    carry["silu_pe"] = b2.wev
                    ev_ln1 = bank_read("act", b2, lambda e, b=b2: e.activation(
                        out=wf(WF["u1"], T)[0:1, :], in_=b.ap[0:1, 0:T],
                        func=AF.Ln, bias=eps_l2[0:1, :]))
                    ev_ln2 = kb.op("act", lambda e, k=kind: e.activation(
                        out=wb(WB["z1"], T)[0:1, :], in_=wf(WF["u1"], T)[0:1, :],
                        func=AF.Exp, scale=-0.5,
                        bias=(qsc_b[0:1, :] if k == "q" else zero_c[0:1, :])),
                        waits=[ev_ln1, carry["bcast_pe"]], sig=True)
                    mm_group(b2, [lambda t, b=b2: t.matmul(
                        b.ap[:, 0:T], ones_rb[:], wb(WB["z1"], T)[0:1, :],
                        start=True, stop=True)], extra_waits=[ev_ln2])
                    carry["bcast_pe"] = b2.wev
                    ev = bank_read("dve", b2, lambda e, b=b2, q=qn_off, d=dst_off:
                                   e.tensor_tensor(
                                       out=wb(d, T), in0=wb(q, T), in1=b.ap[:, 0:T],
                                       op=ALU.mult), extra_waits=[carry["qkg_pe"]])
                    carry["qk_dve"] = ev
                    return ev

                ev_qg, ev_kg, ev_vg = [None, None], [None, None], [None] * 4
                qk_phase = []
                for i in range(2):
                    ev_sq = proj_mm_conv(o_qk, i * 128, WB["di"] + i * T,
                                         WB["wt"] + i * T, i % 2, ev_qk,
                                         2 * g + i, "q")
                    qk_phase.append(("q", i, ev_sq))
                for i in range(2):
                    ev_sq = proj_mm_conv(o_qk, 256 + i * 128, WB["di"] + (2 + i) * T,
                                         WB["wt"] + (2 + i) * T, i % 2, ev_qk,
                                         16 + 2 * g + i, "k")
                    qk_phase.append(("k", i, ev_sq))
                ring_release(ent_qk, kb.now("pe"))
                o_z, ev_z, ent_z = ring_push(f"zzzz{g}", SEG_QQKK)
                for kind, i, ev_sq in qk_phase:
                    j = i if kind == "q" else 2 + i
                    dst = (WB["qg"] if kind == "q" else WB["kg"]) + i * T
                    ev = proj_l2_finish(WB["di"] + j * T, WB["wt"] + j * T,
                                        dst, kind, ev_sq)
                    if kind == "q":
                        ev_qg[i] = ev
                    else:
                        ev_kg[i] = ev
                for i in range(4):
                    ev_vg[i] = proj_mm_conv(o_v, i * 128, 0, 0, i % 2, ev_v,
                                            32 + 4 * g + i, "v",
                                            dst_off=WB["vg"] + i * T)
                ring_release(ent_v, kb.now("pe"))
                o_wo, ev_wo, ent_wo = ring_push(f"wout{g}", SEG_WOUT)

                if ti == 0 and g == 0:
                    evq = kb.now("dve")
                    tap("qg0", lambda: wb(WB["qg"], 2 * T), [128, 2 * T], BF16, [evq])
                    tap("kg0", lambda: wb(WB["kg"], 2 * T), [128, 2 * T], BF16, [evq])
                    eva = ev_vg[3]
                    tap("vg0", lambda: wb(WB["vg"], 4 * T), [128, 4 * T], BF16, [eva])

                ev_zg = None
                for cc in range(NCH):
                    bnkz = pb[2 + 2 * (cc % 2)]
                    mml = []
                    for kc in range(NKC):
                        mml.append(lambda t, kc=kc, cc=cc, b=bnkz, oz=o_z: t.matmul(
                            b.ap[:, 0:512],
                            xn_fm[:, kc * T + cc * C:kc * T + (cc + 1) * C],
                            ring[:, oz + kc * 512:oz + (kc + 1) * 512],
                            start=(kc == 0), stop=(kc == NKC - 1)))
                    mm_group(bnkz, mml, extra_waits=[ev_z, ev_xn])
                    ev_zg = bank_read("act", bnkz, lambda e, cc=cc, b=bnkz:
                                      e.activation(
                                          out=wb(WB["zg"] + cc * 512, 512),
                                          in_=b.ap[:, 0:512], func=AF.Silu),
                                      extra_waits=[carry["zg_dve"], tap_ev[0]])
                ring_release(ent_z, kb.now("pe"))

                ev_kkkq = None
                for kl in range(2):
                    for c0 in range(0, NCH, 2):
                        bnkk = pb[2 + 2 * ((c0 // 2) % 2)]
                        mml = []
                        for j, cc in enumerate((c0, c0 + 1)):
                            mml.append(lambda t, cc=cc, kl=kl, j=j, b=bnkk: t.matmul(
                                b.ap[:, j * 256:j * 256 + 128],
                                wb(WB["kg"] + kl * T + cc * C, C),
                                wb(WB["kg"] + kl * T + cc * C, C),
                                start=(j == 0), stop=False))
                            mml.append(lambda t, cc=cc, kl=kl, j=j, b=bnkk: t.matmul(
                                b.ap[:, j * 256 + 128:j * 256 + 256],
                                wb(WB["kg"] + kl * T + cc * C, C),
                                wb(WB["qg"] + kl * T + cc * C, C),
                                start=False, stop=(j == 1)))
                        mm_group(bnkk, mml, extra_waits=[ev_kg[kl], ev_qg[kl]])
                        ev_kkkq = bank_read("act", bnkk,
                                            lambda e, kl=kl, c0=c0, b=bnkk:
                                            e.copy(
                                                wb(WB["kkkq"] + kl * NCH * 256 + c0 * 256, 512),
                                                b.ap[:, 0:512]),
                                            extra_waits=[carry["gated_pe"],
                                                         tap_ev[0]])
                carry["qkg_pe"] = kb.now("pe")
                ev_vt = None
                for b0 in range(0, 16, 8):
                    mml = []
                    for j in range(8):
                        hh, cc = divmod(b0 + j, NCH)
                        mml.append(lambda t, hh=hh, cc=cc, j=j: t.matmul(
                            pt.ap[:, j * 128:(j + 1) * 128],
                            wb(WB["vg"] + hh * T + cc * C, C),
                            ident_b[:], is_transpose=True,
                            start=(j == 0), stop=(j == 7)))
                    mm_group(pt, mml, extra_waits=ev_vg)
                    ev_vt = bank_read("act", pt, lambda e, b0=b0: e.copy(
                        wb(WB["vt"] + b0 * 128, 1024), pt.ap[:, 0:1024]))
                mml = []
                for j in range(8):
                    kl, cc = divmod(j, NCH)
                    mml.append(lambda t, kl=kl, cc=cc, j=j: t.matmul(
                        pt.ap[:, j * 128:(j + 1) * 128],
                        wb(WB["kg"] + kl * T + cc * C, C),
                        ident_b[:], is_transpose=True, start=(j == 0), stop=(j == 7)))
                mm_group(pt, mml, extra_waits=[ev_kg[0], ev_kg[1]])
                ev_kt = bank_read("act", pt, lambda e: e.copy(
                    wb(WB["kt"], 1024), pt.ap[:, 0:1024]))
                carry["vg_pe"] = kb.now("pe")

                ev_lcc = None
                G_SLOT = (WB["g_nw"], WB["dl"])
                Z_SLOT = (WB["z1"], WB["dl2"])
                XT_A = (WB["xt0"], WB["u_sb"])
                XT_B = (WB["at"], WB["gtd"])
                XB = (pb[2], pb[3])
                PBANK = (pb[4], pb[5])
                XTBANK = (pb[6], pb[3])
                ev_zslot = [None] * 4
                ev_slot_pe = None

                def p1_prep(hh):
                    p = hh % 2
                    h32 = (4 * g + hh) % 32
                    kl = hh // 2
                    bnkx = XB[p]
                    mml = []
                    for cc in range(NCH):
                        view = bass.AP(gb_tmh.ap().tensor,
                                       gb_tmh.ap().offset + cc * 64 + h32,
                                       [gb_tmh.ap().ap[0], [0, C]])
                        mml.append(lambda t, v=view, cc=cc, b=bnkx: t.matmul(
                            b.ap[:, cc * C:(cc + 1) * C], v, ident_f[:],
                            is_transpose=True, start=(cc == 0), stop=(cc == NCH - 1)))
                    mm_group(bnkx, mml, extra_waits=[ev_gb])
                    gview = bcast3(gb_tmh, h32, NCH, 64, C)
                    ev_e1 = bank_read("dve", bnkx, lambda e, gv=gview, b=bnkx:
                                      e.tensor_tensor(
                                          out=wf3(WF["e1"], NCH, C, C),
                                          in0=b.ap.rearrange("p (a b) -> p a b", a=NCH),
                                          in1=gv, op=ALU.subtract))
                    mi_v = bass.AP(mask_i.ap().tensor, mask_i.ap().offset,
                                   [mask_i.ap().ap[0], [0, NCH], [1, C]])
                    ev_e2 = kb.op("dve", lambda e, mv=mi_v: e.tensor_tensor(
                        out=wf3(WF["e2"], NCH, C, C), in0=wf3(WF["e1"], NCH, C, C),
                        in1=mv, op=ALU.add), waits=[ev_e1], sig=True)
                    ev_di = kb.op("act", lambda e, hh=hh: e.activation(
                        out=wb(WB["di"] + hh * NCH * C, NCH * C),
                        in_=wf(WF["e2"], NCH * C), func=AF.Exp),
                        waits=[ev_e2, tap_ev[0]], sig=True)
                    ms_v = bass.AP(mask_s.ap().tensor, mask_s.ap().offset,
                                   [mask_s.ap().ap[0], [0, NCH], [1, C]])
                    ev_e3 = kb.op("dve", lambda e, mv=ms_v: e.tensor_tensor(
                        out=wf3(WF["e2"], NCH, C, C), in0=wf3(WF["e1"], NCH, C, C),
                        in1=mv, op=ALU.add), waits=[ev_di], sig=True)
                    gso = G_SLOT[p]
                    ev_ds = kb.op("act", lambda e, go=gso: e.activation(
                        out=wb(go, NCH * C), in_=wf(WF["e2"], NCH * C),
                        func=AF.Exp, scale=-1.0),
                        waits=[ev_e3, ev_slot_pe], sig=True)
                    kk_v = wb3(WB["kkkq"] + kl * NCH * 256, NCH, 256, C)
                    zso = Z_SLOT[p]
                    ev_a1 = kb.op("dve", lambda e, kv=kk_v, go=gso, zo=zso:
                                  e.tensor_tensor(
                                      out=wb3(zo, NCH, C, C), in0=wb3(go, NCH, C, C),
                                      in1=kv, op=ALU.mult),
                                  waits=[ev_ds, ev_kkkq, ev_slot_pe], sig=True)
                    b_v = bcast3(gb_tmh, 32 + h32, NCH, 64, C)
                    ev_a = kb.op("dve", lambda e, bv=b_v, zo=zso: e.tensor_tensor(
                        out=wb3(zo, NCH, C, C), in0=wb3(zo, NCH, C, C),
                        in1=bv, op=ALU.mult), waits=[ev_a1], sig=True)
                    kq_v = wb3(WB["kkkq"] + kl * NCH * 256 + 128, NCH, 256, C)
                    kb.op("dve", lambda e, kv=kq_v, hh=hh: e.tensor_tensor(
                        out=wb3(WB["wt"] + hh * NCH * C, NCH, C, C),
                        in0=wb3(WB["di"] + hh * NCH * C, NCH, C, C),
                        in1=kv, op=ALU.mult), waits=[ev_di])
                    i_v = bass.AP(ident_b.ap().tensor, ident_b.ap().offset,
                                  [ident_b.ap().ap[0], [0, NCH], [1, C]])
                    ev_g = kb.op("dve", lambda e, iv=i_v, go=gso, zo=zso:
                                 e.tensor_tensor(
                                     out=wb3(go, NCH, C, C), in0=wb3(zo, NCH, C, C),
                                     in1=iv, op=ALU.add), waits=[ev_a], sig=True)
                    ev_z1n = kb.op("dve", lambda e, iv=i_v, zo=zso:
                                   e.scalar_tensor_tensor(
                                       out=wb3(zo, NCH, C, C),
                                       in0=wb3(zo, NCH, C, C),
                                       scalar=-1.0, in1=iv, op0=ALU.mult,
                                       op1=ALU.add), waits=[ev_g], sig=True)
                    ev_zslot[hh] = ev_z1n
                    ed_v = bass.AP(work_bf.ap().tensor,
                                   work_bf.ap().offset + WB["di"] + hh * NCH * C + 127,
                                   [work_bf.ap().ap[0], [C, NCH]])
                    ex_v = bass.AP(exps_t.ap().tensor, exps_t.ap().offset + h32,
                                   [exps_t.ap().ap[0], [32, NCH]])
                    nonlocal_lcc = kb.op("dve", lambda e, hh=hh, ev2=ed_v, xv=ex_v:
                                   e.tensor_tensor(
                                       out=bass.AP(work_f32.ap().tensor,
                                                   work_f32.ap().offset + WF["lcc"] + hh * NCH,
                                                   [work_f32.ap().ap[0], [1, NCH]]),
                                       in0=ev2, in1=xv, op=ALU.mult),
                                   waits=[ev_di, ev_exps, ev_bl], sig=True)
                    return nonlocal_lcc

                for hp in (0, 2):
                    for hh in (hp, hp + 1):
                        ev_lcc = p1_prep(hh)
                    xt_cur = [None, None]
                    ev_xt = [None, None]
                    for hh in (hp, hp + 1):
                        p = hh % 2
                        mml = []
                        for cc in range(NCH):
                            mml.append(lambda t, cc=cc, zo=Z_SLOT[p]: t.matmul(
                                pt.ap[:, cc * C:(cc + 1) * C], wb(zo + cc * C, C),
                                ident_b[:], is_transpose=True,
                                start=(cc == 0), stop=(cc == NCH - 1)))
                        mm_group(pt, mml, extra_waits=[ev_zslot[hh]])
                        ev_xt[p] = bank_read("act", pt,
                                             lambda e, d=XT_A[p]: e.copy(
                                                 wb(d, NCH * C), pt.ap[:, 0:NCH * C]),
                                             extra_waits=[ev_slot_pe])
                        xt_cur[p] = XT_A[p]
                    for step in range(2):
                        pevs = [None, None]
                        for hh in (hp, hp + 1):
                            p = hh % 2
                            bpn = PBANK[p]
                            mml = []
                            for cc in range(NCH):
                                mml.append(lambda t, cc=cc, b=bpn, c=xt_cur[p],
                                           go=G_SLOT[p]: t.matmul(
                                    b.ap[:, cc * C:(cc + 1) * C], wb(c + cc * C, C),
                                    wb(go + cc * C, C),
                                    start=(cc == 0), stop=(cc == NCH - 1)))
                            mm_group(bpn, mml, extra_waits=[ev_xt[p]])
                            pevs[p] = bpn
                        for hh in (hp, hp + 1):
                            p = hh % 2
                            bpn = pevs[p]
                            t_v = bass.AP(two_i.ap().tensor, two_i.ap().offset,
                                          [two_i.ap().ap[0], [0, NCH], [1, C]])
                            ev_zz = bank_read("dve", bpn,
                                              lambda e, b=bpn, tv=t_v, zo=Z_SLOT[p]:
                                              e.scalar_tensor_tensor(
                                                  out=wb3(zo, NCH, C, C),
                                                  in0=b.ap.rearrange(
                                                      "p (a b) -> p a b", a=NCH),
                                                  scalar=-1.0, in1=tv,
                                                  op0=ALU.mult, op1=ALU.add))
                            bq = XTBANK[p]
                            mml = []
                            for cc in range(NCH):
                                mml.append(lambda t, cc=cc, b=bq, c=xt_cur[p],
                                           zo=Z_SLOT[p]: t.matmul(
                                    b.ap[:, cc * C:(cc + 1) * C],
                                    wb(zo + cc * C, C), wb(c + cc * C, C),
                                    start=(cc == 0), stop=(cc == NCH - 1)))
                            mm_group(bq, mml, extra_waits=[ev_zz])
                            if step < 1:
                                dsto = XT_B[p] if xt_cur[p] == XT_A[p] else XT_A[p]
                            else:
                                dsto = WB["tt"] + hh * NCH * C
                            ev_xt[p] = bank_read("act", bq,
                                                 lambda e, b=bq, d=dsto: e.copy(
                                                     wb(d, NCH * C),
                                                     b.ap[:, 0:NCH * C]),
                                                 extra_waits=[tap_ev[0]])
                            xt_cur[p] = dsto
                    ev_slot_pe = kb.now("pe")
                ev_p1 = kb.now("dve")
                if ti == 0 and g == 0:
                    tap("tt0", lambda: wb(WB["tt"], 4 * NCH * C), [128, 4 * NCH * C],
                        BF16, [ev_p1])
                    tap("di0", lambda: wb(WB["di"], 4 * NCH * C), [128, 4 * NCH * C],
                        BF16, [ev_p1])

                prev_ks = prev_qs = None
                for cc in range(NCH):
                    ev_sb = kb.op("dve", lambda e, g=g: e.tensor_copy(
                        wb(WB["sbf"], 512), s_f32[:, g * 512:(g + 1) * 512]),
                        waits=[w for w in (prev_ks, prev_qs, carry["sbf_pe"]) if w],
                        sig=True)
                    bKS, bQS = pb[3], pb[4]
                    mml = []
                    for p in range(2):
                        mml.append(lambda t, p=p, cc=cc, b=bKS: t.matmul(
                            b.ap[:, p * 256:(p + 1) * 256],
                            wb(WB["kg"] + p * T + cc * C, C),
                            wb(WB["sbf"] + p * 256, 256),
                            start=(p == 0), stop=(p == 1)))
                    mm_group(bKS, mml, extra_waits=[ev_sb])
                    prev_ks = bKS.wev
                    mml = []
                    for p in range(2):
                        mml.append(lambda t, p=p, cc=cc, b=bQS: t.matmul(
                            b.ap[:, p * 256:(p + 1) * 256],
                            wb(WB["qg"] + p * T + cc * C, C),
                            wb(WB["sbf"] + p * 256, 256),
                            start=(p == 0), stop=(p == 1)))
                    mm_group(bQS, mml, extra_waits=[ev_sb])
                    prev_qs = bQS.wev
                    bl_v = bcast3(bl_t, cc * 32 + 4 * g - 32 * (g // 8), 4, 1, C)
                    bl_v = bcast3(bl_t, cc * 32 + (4 * g) % 32, 4, 1, C)
                    ev_u1 = bank_read("dve", bKS, lambda e, bv=bl_v, b=bKS:
                                      e.tensor_tensor(
                                          out=wf3(WF["u1"], 4, C, C),
                                          in0=b.ap.rearrange("p (a b) -> p a b", a=4),
                                          in1=bv, op=ALU.mult),
                                      extra_waits=[ev_bl])
                    vt_v = wb3(WB["vt"] + cc * C, 4, NCH * C, C)
                    be_v = bcast3(gb_tmh, cc * 64 + 32 + (4 * g) % 32, 4, 1, C)
                    ev_u2 = kb.op("dve", lambda e, vv=vt_v, bv=be_v: e.tensor_tensor(
                        out=wf3(WF["g1"], 4, C, C), in0=vv, in1=bv, op=ALU.mult),
                        waits=[ev_vt, ev_gb], sig=True)
                    ev_u = kb.op("dve", lambda e: e.tensor_tensor(
                        out=wb(WB["u_sb"], 512), in0=wf(WF["g1"], 512),
                        in1=wf(WF["u1"], 512), op=ALU.subtract),
                        waits=[ev_u1, ev_u2], sig=True)
                    bD = pb[5]
                    mml = []
                    for hh in range(4):
                        mml.append(lambda t, hh=hh, cc=cc, b=bD: t.matmul(
                            b.ap[:, hh * 128:(hh + 1) * 128],
                            wb(WB["tt"] + hh * NCH * C + cc * C, C),
                            wb(WB["u_sb"] + hh * 128, 128),
                            start=(hh == 0), stop=(hh == 3)))
                    mm_group(bD, mml, extra_waits=[ev_u, ev_p1])
                    ev_dl = bank_read("act", bD, lambda e, b=bD: e.copy(
                        wb(WB["dl"], 512), b.ap[:, 0:512]))
                    ed_v = bass.AP(work_bf.ap().tensor,
                                   work_bf.ap().offset + WB["di"] + cc * C + 127,
                                   [work_bf.ap().ap[0], [NCH * C, 4], [0, C]])
                    ev_dl2 = kb.op("dve", lambda e, ev2=ed_v, b=bD: e.tensor_tensor(
                        out=wb(WB["dl2"], 512),
                        in0=b.ap.rearrange("p (a b) -> p a b", a=4),
                        in1=ev2, op=ALU.mult), waits=[ev_dl], sig=True)
                    bD.rev = ev_dl2
                    bO = pb[3]
                    mml = []
                    for hh in range(4):
                        mml.append(lambda t, hh=hh, cc=cc, b=bO: t.matmul(
                            b.ap[:, hh * 128:(hh + 1) * 128],
                            wb(WB["wt"] + hh * NCH * C + cc * C, C),
                            wb(WB["dl"] + hh * 128, 128),
                            start=(hh == 0), stop=(hh == 3)))
                    mm_group(bO, mml, extra_waits=[ev_dl])
                    prev_ks = bO.wev
                    eg_v = bcast3(exps_t, cc * 32 + (4 * g) % 32, 4, 1, C)
                    ev_g1 = bank_read("dve", bQS, lambda e, gv=eg_v, b=bQS:
                                      e.tensor_tensor(
                                          out=wf3(WF["g1"], 4, C, C),
                                          in0=b.ap.rearrange("p (a b) -> p a b", a=4),
                                          in1=gv, op=ALU.mult))
                    ev_o = bank_read("dve", bO, lambda e, b=bO: e.tensor_tensor(
                        out=wf(WF["osb"], 512), in0=b.ap[:, 0:512],
                        in1=wf(WF["g1"], 512), op=ALU.add), extra_waits=[ev_g1])
                    ev_ms = None
                    for hh in range(4):
                        ev_ms = kb.op("act", lambda e, hh=hh: e.activation(
                            out=wf(WF["u1"] + hh * C, C),
                            in_=wf(WF["osb"] + hh * C, C),
                            func=AF.Square,
                            accum_out=wf(WF["ms"] + hh, 1)),
                            waits=[ev_o] if hh == 0 else (), sig=(hh == 3))
                    ev_r1 = kb.op("act", lambda e: e.activation(
                        out=wf(WF["rstd"], 4), in_=wf(WF["ms"], 4), func=AF.Ln,
                        scale=1.0 / DV, bias=eps_rms[:]), waits=[ev_ms], sig=True)
                    ev_r2 = kb.op("act", lambda e: e.activation(
                        out=wf(WF["rstd"], 4), in_=wf(WF["rstd"], 4), func=AF.Exp,
                        scale=-0.5), waits=[ev_r1], sig=True)
                    rs_v = bcast3(work_f32, WF["rstd"], 4, 1, C)
                    ev_gt1 = kb.op("dve", lambda e, rv=rs_v: e.tensor_tensor(
                        out=wf3(WF["g1"], 4, C, C), in0=wf3(WF["osb"], 4, C, C),
                        in1=rv, op=ALU.mult), waits=[ev_r2], sig=True)
                    ev_gt = kb.op("dve", lambda e, cc=cc: e.tensor_tensor(
                        out=wb(WB["gtd"], 512), in0=wf(WF["g1"], 512),
                        in1=wb(WB["zg"] + cc * 512, 512), op=ALU.mult),
                        waits=[ev_gt1, ev_zg], sig=True)
                    carry["zg_dve"] = ev_gt
                    mml = []
                    for hh in range(4):
                        mml.append(lambda t, hh=hh: t.matmul(
                            pt.ap[:, hh * 128:(hh + 1) * 128],
                            wb(WB["gtd"] + hh * 128, 128), ident_b[:],
                            is_transpose=True, start=(hh == 0), stop=(hh == 3)))
                    mm_group(pt, mml, extra_waits=[ev_gt])
                    aw = work_bf.ap()
                    dstg = bass.AP(aw.tensor, aw.offset + WB["gated"] + cc * C,
                                   [aw.ap[0], [T, 4], [1, C]])
                    bank_read("act", pt, lambda e, d=dstg: e.copy(
                        d, pt.ap[:, 0:512].rearrange("p (a b) -> p a b", a=4)),
                        extra_waits=[carry["gated_pe"], tap_ev[0]])
                    bS = pb[4]
                    mml = []
                    for p in range(2):
                        mml.append(lambda t, p=p, cc=cc, b=bS: t.matmul(
                            b.ap[:, p * 256:(p + 1) * 256],
                            wb(WB["kt"] + (p * NCH + cc) * C, C),
                            wb(WB["dl2"] + p * 256, 256),
                            start=(p == 0), stop=(p == 1)))
                    mm_group(bS, mml, extra_waits=[ev_dl2, ev_kt])
                    prev_qs = bS.wev
                    for hh in range(4):
                        lcc_ap = bass.AP(work_f32.ap().tensor,
                                         work_f32.ap().offset + WF["lcc"] + hh * NCH + cc,
                                         [work_f32.ap().ap[0], [1, 1]])
                        bank_read("dve", bS, lambda e, hh=hh, g=g, sc=lcc_ap:
                                  e.scalar_tensor_tensor(
                                      out=s_f32[:, g * 512 + hh * 128:g * 512 + (hh + 1) * 128],
                                      in0=s_f32[:, g * 512 + hh * 128:g * 512 + (hh + 1) * 128],
                                      scalar=sc,
                                      in1=bS.ap[:, hh * 128:(hh + 1) * 128],
                                      op0=ALU.mult, op1=ALU.add),
                                  extra_waits=[ev_lcc])
                carry["sbf_pe"] = kb.now("pe")
                if ti == 0 and g == 0:
                    ev_gg = kb.now("dve")
                    tap("gated0", lambda: wb(WB["gated"], 4 * T), [128, 4 * T], BF16,
                        [ev_gg])
                    tap("s0", lambda: s_f32[:, 0:512], [128, 512], F32, [ev_gg])

                ev_gated_all = kb.now("dve")
                for oc in range(NKC):
                    bnko = pb[oc % 2]
                    mml = []
                    for kc in range(4):
                        mml.append(lambda t, kc=kc, oc=oc, b=bnko, ow=o_wo: t.matmul(
                            b.ap[:, 0:T],
                            ring[:, ow + kc * 2048 + oc * 128:
                                 ow + kc * 2048 + oc * 128 + 128],
                            wb(WB["gated"] + kc * T, T),
                            start=(kc == 0), stop=(kc == 3)))
                    mm_group(bnko, mml, extra_waits=[ev_wo, ev_gated_all])
                    if g == 0:
                        s_v = bass.AP(mbsb.ap().tensor, mbsb.ap().offset + 128,
                                      [mbsb.ap().ap[0], [256, NCH], [1, 128]])
                        m_v = bass.AP(mbsb.ap().tensor, mbsb.ap().offset,
                                      [mbsb.ap().ap[0], [256, NCH], [1, 128]])
                        an = xn_fm.ap()
                        xn_v = bass.AP(an.tensor, an.offset + oc * T,
                                       [an.ap[0], [C, NCH], [1, C]])
                        kb.op("dve", lambda e, xv=xn_v, sv=s_v: e.tensor_tensor(
                            out=wf3(WF["u1"], NCH, C, C), in0=xv, in1=sv,
                            op=ALU.mult), waits=[ev_mbsb, ev_xn])
                        kb.op("dve", lambda e, mv=m_v: e.tensor_tensor(
                            out=wf3(WF["u1"], NCH, C, C), in0=wf3(WF["u1"], NCH, C, C),
                            in1=mv, op=ALU.add))
                        bank_read("dve", bnko, lambda e, oc=oc, b=bnko: e.tensor_tensor(
                            out=attn_bf[:, oc * T:(oc + 1) * T], in0=b.ap[:, 0:T],
                            in1=wf(WF["u1"], T), op=ALU.add))
                    else:
                        bank_read("dve", bnko, lambda e, oc=oc, b=bnko: e.tensor_tensor(
                            out=attn_bf[:, oc * T:(oc + 1) * T], in0=b.ap[:, 0:T],
                            in1=attn_bf[:, oc * T:(oc + 1) * T], op=ALU.add))
                carry["gated_pe"] = kb.now("pe")
                ring_release(ent_wo, kb.now("pe"))

            ev_attn = kb.now("dve")
            if ti == 0:
                tap("attn0", lambda: attn_bf[:, 0:NKC * T], [128, NKC * T], BF16,
                    [ev_attn])
            if stop_after == "attn":
                break

            ev_hid = None
            prev_hid_pe = None
            for hc in range(NHC):
                o1, e_q1, ent1 = ring_push(f"q1_{hc}", SEG_Q12)
                if ti == 0 and hc == 0:
                    tap("ring_q1", lambda o=o1: ring[:, o:o + SEG_Q12],
                        [128, SEG_Q12], BF16, [e_q1])
                o2, e_q2, ent2 = ring_push(f"q2_{hc}", SEG_Q12)
                b1, b2 = pb[(hc % 2) * 2], pb[(hc % 2) * 2 + 1]
                mml = []
                for kc in range(NKC):
                    mml.append(lambda t, kc=kc, o=o1, b=b1: t.matmul(
                        b.ap[:, 0:T], ring[:, o + kc * 128:o + (kc + 1) * 128],
                        xn_fm[:, kc * T:(kc + 1) * T],
                        start=(kc == 0), stop=(kc == NKC - 1)))
                mm_group(b1, mml, extra_waits=[e_q1, ev_xn, ev_attn if hc == 0 else None])
                mml = []
                for kc in range(NKC):
                    mml.append(lambda t, kc=kc, o=o2, b=b2: t.matmul(
                        b.ap[:, 0:T], ring[:, o + kc * 128:o + (kc + 1) * 128],
                        xn_fm[:, kc * T:(kc + 1) * T],
                        start=(kc == 0), stop=(kc == NKC - 1)))
                mm_group(b2, mml, extra_waits=[e_q2])
                ev_sg = bank_read("act", b1, lambda e, b=b1: e.activation(
                    out=wb(WB_SIG, T), in_=b.ap[:, 0:T], func=AF.Sigmoid),
                    extra_waits=[ev_hid])
                if ti == 0 and hc == 0:
                    tap("sig0", lambda: wb(WB_SIG, T), [128, T], BF16, [ev_sg])
                ev_hid = bank_read("dve", b2, lambda e, hc=hc, b=b2: e.tensor_tensor(
                    out=wb(WB_HID + hc * T, T), in0=b.ap[:, 0:T],
                    in1=wb(WB_SIG, T), op=ALU.mult),
                    extra_waits=[ev_sg, tap_ev[0] if hc == 0 else None])
                ring_release(ent1, kb.now("pe"))
                ring_release(ent2, kb.now("pe"))
            if ti == 0:
                tap("hid0", lambda: wb(WB_HID, 4 * T), [128, 4 * T], BF16, [ev_hid])

            if ti == 0:
                fs_dy = {0: None, 1: None}
            for oc in range(NKC):
                o3, e_q3, ent3 = ring_push(f"q3_{oc}", SEG_Q3)
                bnkf = pb[oc % 2]
                mml = []
                for kc in range(NHC):
                    mml.append(lambda t, kc=kc, o=o3, b=bnkf: t.matmul(
                        b.ap[:, 0:T], ring[:, o + kc * 128:o + (kc + 1) * 128],
                        wb(WB_HID + kc * T, T),
                        start=(kc == 0), stop=(kc == NHC - 1)))
                mm_group(bnkf, mml, extra_waits=[e_q3, ev_hid])
                fo = (oc % 2) * T
                ev_f = bank_read("dve", bnkf, lambda e, oc=oc, b=bnkf, fo=fo:
                                 e.tensor_tensor(
                                     out=wf(WF["fsa"] + fo, T), in0=b.ap[:, 0:T],
                                     in1=attn_bf[:, oc * T:(oc + 1) * T], op=ALU.add),
                                 extra_waits=[fs_dy[oc % 2]])
                bt = pb[2 + oc % 2]
                mml = []
                for tc in range(NCH):
                    mml.append(lambda t, tc=tc, b=bt, fo=fo: t.matmul(
                        b.ap[:, tc * 128:(tc + 1) * 128],
                        wf(WF["fsa"] + fo + tc * 128, 128), ident_f[:],
                        is_transpose=True, start=(tc == 0), stop=(tc == NCH - 1)))
                mm_group(bt, mml, extra_waits=[ev_f])
                ev_fb = bank_read("dve", bt, lambda e, b=bt, fo=fo: e.tensor_copy(
                    wf(WF["fsb"] + fo, T), b.ap[:, 0:512]))
                dsty = bass.AP(y_h.tensor, y_h.offset + ti * T * D + oc * 128,
                               [[D, 128], [128 * D, NCH], [1, 128]])
                srcy = bass.AP(work_f32.ap().tensor,
                               work_f32.ap().offset + WF["fsb"] + fo,
                               [work_f32.ap().ap[0], [128, NCH], [1, 128]])
                fs_dy[oc % 2] = kb.op("act", lambda e, d=dsty, s=srcy:
                                      e.dma_start(out=d, in_=s),
                                      waits=[ev_fb], dma="dy")
                ring_release(ent3, kb.now("pe"))

            if stop_after == "tile0":
                break

        kb.wait_only("sp", [Ev("dy", kb.semtot.get("dy", 0))])

        kb.check_deadlock()

        @block.tensor
        def _(eng):
            kb.emit("pe", eng, sems)

        @block.vector
        def _(eng):
            kb.emit("dve", eng, sems)

        @block.scalar
        def _(eng):
            kb.emit("act", eng, sems)

        @block.gpsimd
        def _(eng):
            kb.emit("pool", eng, sems)

        @block.sync
        def _(eng):
            kb.emit("sp", eng, sems)

    return nc, tap_out


def kernel(L0_attn_hidden_states, L0_attn_w_qkv, L0_attn_w_z, L0_attn_w_b,
           L0_attn_w_a, L0_attn_w_out, L0_attn_conv_w, L0_attn_dt_bias,
           L0_attn_A_log, L0_attn_norm_weight, L0_ffn_w1, L0_ffn_w2, L0_ffn_w3):
    x = np.ascontiguousarray(np.asarray(L0_attn_hidden_states, dtype=np.float32))
    common = {
        "w_qkv": np.ascontiguousarray(L0_attn_w_qkv, dtype=np.float32),
        "w_z": np.ascontiguousarray(L0_attn_w_z, dtype=np.float32),
        "w_b": np.ascontiguousarray(L0_attn_w_b, dtype=np.float32),
        "w_a": np.ascontiguousarray(L0_attn_w_a, dtype=np.float32),
        "w_out": np.ascontiguousarray(L0_attn_w_out, dtype=np.float32),
        "conv_w": np.ascontiguousarray(L0_attn_conv_w, dtype=np.float32),
        "dt_bias": np.ascontiguousarray(L0_attn_dt_bias, dtype=np.float32),
        "A_log": np.ascontiguousarray(L0_attn_A_log, dtype=np.float32),
        "norm_weight": np.ascontiguousarray(L0_attn_norm_weight, dtype=np.float32),
        "w1": np.ascontiguousarray(L0_ffn_w1, dtype=np.float32),
        "w2": np.ascontiguousarray(L0_ffn_w2, dtype=np.float32),
        "w3": np.ascontiguousarray(L0_ffn_w3, dtype=np.float32),
    }
    nc, _ = build_core()
    in_maps = [dict(common, x=np.ascontiguousarray(x[b])) for b in range(B)]
    res = run_bass_kernel_spmd(nc, in_maps, core_ids=list(range(B)))
    return np.stack([res.results[b]["y"] for b in range(B)], axis=0)
